# version 02
# baseline (speedup 1.0000x reference)
import math
from contextlib import ExitStack

import numpy as np

import concourse.bass as bass
import concourse.mybir as mybir
from concourse.bass_utils import run_bass_kernel_spmd

F32 = mybir.dt.float32
BF16 = mybir.dt.bfloat16
I32 = mybir.dt.int32
U8 = mybir.dt.uint8
AF = mybir.ActivationFunctionType
OP = mybir.AluOpType

D = 4096
T = 2048
NKC = 32
NTC = 16
FI = 11008
NIC = 86
GRP = 64
ISCALE = 1.0 / math.sqrt(128.0)
NEG = -1.0e30
NEGPAD = -1.0e35
CS = [12, 16, 28, 32]
_AB = 2.0 ** (-(2.0 ** -(math.log2(32) - 3)))
SLOPES = [_AB ** (i + 1) for i in range(32)]
LN_EPS = 1e-5

DEBUG = ()
UPTO = 99
TRANSPOSE_DOWN = True


def own_tokens(parity):
    if parity == 0:
        return list(range(0, 1024)) + list(range(3072, 4096))
    return list(range(1024, 3072))


def own_q0s(parity):
    return [0, 512, 3072, 3584] if parity == 0 else [1024, 1536, 2048, 2560]


def pos_block_map(g):
    if g in (0, 1):
        return (0, 512 * g)
    if g in (2, 3, 4, 5):
        return (1, 512 * (g - 2))
    return (0, 1024 + 512 * (g - 6))


class P:

    def __init__(self, debug=()):
        self.debug = set(debug)
        self.nc = bass.Bass("TRN2", target_bir_lowering=False,
                            detect_race_conditions=False)
        self.ctx = ExitStack()
        self.sem = {}
        self.cnt = {}
        for name in ["sq", "sqp", "gq", "gqp", "te", "ve", "se", "t1", "t3", "sx", "sm", "pf", "pd"]:
            self.sem[name] = self.ctx.enter_context(self.nc.semaphore(name))
            self.cnt[name] = 0
        self.sb = {}
        self.dram = {}
        self.pspar = 0
        self.par_snap = {}
        self.qn = {"sync": 0, "gp": 0}
        self.qpend = {"sync": [], "gp": []}
        self.qpub = {"sync": 0, "gp": 0}
        self.qdepth = 6
        self.poll_done = 0

    def dma(self, e, which, out, in_, transpose=False):
        raw = self.sem["sq" if which == "sync" else "gq"]
        if transpose:
            e.dma_start_transpose(out, in_).then_inc(raw, 16)
        else:
            e.dma_start(out=out, in_=in_).then_inc(raw, 16)
        idx = self.qn[which]
        self.qn[which] += 1
        self.qpend[which].append(idx)
        if len(self.qpend[which]) > self.qdepth:
            self._flush(e, which)
        return idx

    def _flush(self, e, which):
        if not self.qpend[which]:
            return
        raw = self.sem["sq" if which == "sync" else "gq"]
        pub = self.sem["sqp" if which == "sync" else "gqp"]
        e.wait_ge(raw, 16 * self.qn[which])
        e.sem_inc(pub, len(self.qpend[which]))
        self.qpub[which] = self.qn[which]
        self.qpend[which].clear()

    def qdrain(self, e, which):
        self._flush(e, which)

    def wait_dma(self, e, which, idx):
        e.wait_ge(self.sem["sqp" if which == "sync" else "gqp"], idx + 1)

    def bump(self, name, amt=1):
        self.cnt[name] += amt
        return self.cnt[name]

    def din(self, name, shape, dtype):
        self.dram[name] = self.nc.dram_tensor(name, shape, dtype,
                                              kind="ExternalInput")
        return self.dram[name]

    def dout(self, name, shape, dtype):
        self.dram[name] = self.nc.dram_tensor(name, shape, dtype,
                                              kind="ExternalOutput")
        return self.dram[name]

    def dtmp(self, name, shape, dtype, shared=False):
        kind = "ExternalOutput" if name in self.debug else "Internal"
        self.dram[name] = self.nc.dram_tensor(
            name, shape, dtype, kind=kind,
            addr_space="Shared" if shared else "Local")
        return self.dram[name]

    def sbuf(self, name, shape, dtype):
        self.sb[name] = self.ctx.enter_context(
            self.nc.sbuf_tensor(name, shape, dtype))
        return self.sb[name]

    def alias(self, name, shape, dtype, byte_off):
        base = self.nc.lookup_mloc(self.sb["big"]).addr
        self.sb[name] = self.nc.alloc_sbuf_tensor_at(
            name, shape, dtype, offset=base + byte_off)
        return self.sb[name]


def declare_io(p: P):
    p.din("xin", [D, T], F32)
    p.din("wq_a", [D, D], F32)
    p.din("wkv_a", [D, 256], F32)
    p.din("wo_a", [D, D], F32)
    p.din("wq_b", [D, D], F32)
    p.din("wk_b", [D, 1024], F32)
    p.din("wv_b", [D, 1024], F32)
    p.din("wo_b", [D, D], F32)
    p.din("gate_b", [FI, D // 2], U8)
    p.din("up_b", [FI, D // 2], U8)
    p.din("down_b", [D, FI // 2], U8)
    p.din("gate_s", [FI, D // GRP], F32)
    p.din("gate_z", [FI, D // GRP], F32)
    p.din("up_s", [FI, D // GRP], F32)
    p.din("up_z", [FI, D // GRP], F32)
    p.din("down_s", [D, FI // GRP], F32)
    p.din("down_z", [D, FI // GRP], F32)
    p.din("rel128", [128, 512], BF16)
    p.din("relmask", [4, 128, 512], BF16)
    p.din("cmask", [4, 128, 512], BF16)
    p.din("smat", [128, 128], BF16)
    p.din("ident", [128, 128], F32)
    p.din("offv_a", [128, 128], F32)
    p.din("offv_b", [128, 128], F32)
    p.din("pos", [1, T], F32)
    p.din("nonce", [1, 2], I32)

    p.dout("xout", [D, T], F32)

    p.dtmp("x1", [D, T], F32)
    p.dtmp("x2", [D, T], F32)
    p.dtmp("x3", [D, T], F32)
    p.dtmp("wd_T", [FI, D], BF16)
    p.dtmp("wgS", [NIC, 128, 4096], BF16)
    p.dtmp("wuS", [NIC, 128, 4096], BF16)
    p.dtmp("qa", [D, T], BF16)
    p.dtmp("qb", [D, T], BF16)
    p.dtmp("ctxa", [D, T], BF16)
    p.dtmp("ctxb", [D, T], BF16)
    p.dtmp("hbuf", [FI, T], BF16)
    p.dtmp("ka_loc", [128, T], BF16)
    p.dtmp("va_loc", [T, 128], BF16)
    p.dtmp("kb_loc", [1024, T], BF16)
    p.dtmp("vb_loc", [T, 1024], BF16)
    p.dtmp("sh_ka", [2, 128, T], BF16, shared=True)
    p.dtmp("sh_va", [2, T, 128], BF16, shared=True)
    p.dtmp("sh_kb", [2, 1024, T], BF16, shared=True)
    p.dtmp("sh_vb", [2, T, 1024], BF16, shared=True)
    p.dtmp("sh_flag", [2, 2], I32, shared=True)
    if "xln_dbg" in p.debug:
        p.dtmp("xln_dbg", [D, T], BF16)


def alloc_sbuf(p: P):
    p.sbuf("c_rel128", [128, 512], BF16)
    p.sbuf("c_relmask", [128, 4 * 512], BF16)
    p.sbuf("c_cmask", [128, 4 * 512], BF16)
    p.sbuf("c_smat", [128, 128], BF16)
    p.sbuf("c_ident", [128, 128], F32)
    p.sbuf("c_offa", [128, 128], F32)
    p.sbuf("c_offb", [128, 128], F32)

    p.sbuf("c_nonce", [1, 2], I32)
    p.sbuf("c_ones_bf", [128, 1], BF16)
    p.sbuf("c_ones_f", [1, 128], F32)
    p.sbuf("c_invf", [1, 128], F32)
    p.sbuf("c_bias", [128, 64], F32)
    p.sbuf("c_zs", [128, 384], F32)
    p.sbuf("r_rows", [1, 2048], F32)
    p.sbuf("r_poll", [1, 2], I32)
    p.sbuf("i512", [128, 512], I32)
    p.sbuf("fbuf", [128, 2048], F32)
    p.sbuf("obuf", [128, 4096], BF16)
    p.sbuf("hstage", [128, 1024], BF16)
    p.sbuf("kvstage", [128, 1024], BF16)
    p.sbuf("big", [128, 83968], BF16)
    rem = p.nc.sbuf_bytes_remaining
    assert rem >= 0, f"sbuf over budget by {-rem}"

    p.alias("dq_b32", [128, 2 * 2752], I32, 0)
    p.alias("dq_nib", [128, 2752], I32, 22016)
    p.alias("dq_d1", [128, 5504], F32, 33024)
    p.alias("dq_wf", [128, 2 * 5504], F32, 55040)
    p.alias("tstage", [128, 2 * 5504], BF16, 99072)
    p.alias("xln", [128, NKC * T], BF16, 0)
    p.alias("wslab", [128, 2 * 4096], BF16, 131072)
    p.alias("wkv_sb", [128, 8192], BF16, 131072)
    p.alias("ffnblk", [128, 4 * 4096], BF16, 131072)
    p.alias("cs_fm", [128, 2 * T], BF16, 147456)
    p.alias("c_pos", [1, T], F32, 155648)
    p.alias("k_sb", [128, 4096], BF16, 0)
    p.alias("v_sb", [128, 4096], BF16, 8192)
    p.alias("q_sb", [128, 2 * 2048], BF16, 16384)
    p.alias("e_sb", [128, 4 * 512], BF16, 24576)
    p.alias("pr_sb", [128, 4 * 512], BF16, 28672)
    p.alias("a_sb", [128, 2 * 5 * 512], BF16, 32768)
    p.alias("co_sb", [128, 2 * 512], BF16, 43008)
    p.alias("rb_sb", [128, 2 * 512], F32, 45056)
    p.alias("kdiag", [128, 4 * 512], BF16, 49152)
    p.alias("vdiag", [128, 4 * 512], BF16, 53248)
    p.alias("h_tt", [128, NIC * 512], BF16, 0)
    p.alias("wd_slab", [128, 2 * 1024], BF16, 90112)
    p.PS = [p.ctx.enter_context(p.nc.psum_tensor(f"P{i}", [128, 512], F32))
            for i in range(8)]




def _set_parity(p: P, e, tag):
    r = e.alloc_register(f"par_{tag}")
    ro = e.alloc_register(f"opar_{tag}")
    for arm in e.switch_core_id(128):
        e.reg_mov(r, int(arm) & 1)
    e.reg_mov(ro, 1)
    e.reg_alu(ro, ro, r, OP.subtract)
    p.par_snap[tag] = e.snap(r, min_val=0, max_val=1)
    if not hasattr(p, "opar_snap"):
        p.opar_snap = {}
    p.opar_snap[tag] = e.snap(ro, min_val=0, max_val=1)


def qbarrier(p: P, e, sems=("te", "ve", "se"), own=None):
    if own is not None:
        p.qdrain(e, own)
    if p.qn["sync"]:
        e.wait_ge(p.sem["sqp"], p.qn["sync"])
    if p.qn["gp"]:
        e.wait_ge(p.sem["gqp"], p.qn["gp"])
    for s_ in sems:
        if p.cnt[s_]:
            e.wait_ge(p.sem[s_], p.cnt[s_])


def stage_init(p: P):
    nc, sb, dr = p.nc, p.sb, p.dram
    with nc.Block() as block:
        @block.sync
        def _(e):
            pairs = [
                (sb["c_rel128"][:, :], dr["rel128"][:, :]),
                (bass.AP(sb["c_relmask"], 0, [[2048, 128], [512, 4], [1, 512]]),
                 bass.AP(dr["relmask"], 0, [[512, 128], [128 * 512, 4], [1, 512]])),
                (bass.AP(sb["c_cmask"], 0, [[2048, 128], [512, 4], [1, 512]]),
                 bass.AP(dr["cmask"], 0, [[512, 128], [128 * 512, 4], [1, 512]])),
                (sb["c_smat"][:, :], dr["smat"][:, :]),
                (sb["c_ident"][:, :], dr["ident"][:, :]),
                (sb["c_offa"][:, :], dr["offv_a"][:, :]),
                (sb["c_offb"][:, :], dr["offv_b"][:, :]),
                (sb["c_nonce"][:, :], dr["nonce"][:, :]),
            ]
            for o, i in pairs:
                p.dma(e, "sync", o, i)
            p.qdrain(e, "sync")
            _set_parity(p, e, "sync")

        @block.gpsimd
        def _(e):
            e.memset(sb["c_ones_bf"][:, :], 1.0)
            e.memset(sb["c_ones_f"][:, :], 1.0)
            e.memset(sb["r_poll"][:, :], 0).then_inc(p.sem["sm"], 1)
            p.bump("sm", 1)
            _set_parity(p, e, "gp")
            e.wait_ge(p.sem["sm"], p.cnt["sm"])
            e.dma_start(out=dr["sh_flag"][bass.ds(p.par_snap["gp"], 1), 0:2],
                        in_=sb["r_poll"][0:1, 0:2]).then_inc(p.sem["pf"], 16)


def stage_dequant(p: P, bname, zname, sname, oname, nout, nin, blocks):
    nc, sb, dr = p.nc, p.sb, p.dram
    PS = p.PS
    noc = nout // 128
    passes = [(k * 5504, min(5504, nin - k * 5504))
              for k in range((nin + 5503) // 5504)]
    jobs = [(ic, off, w) for ic in range(noc) for (off, w) in passes]
    ntl = [w // 128 for (_, _, w) in jobs]
    cumt = [0]
    for t_ in ntl:
        cumt.append(cumt[-1] + t_)
    ve0, te0, se0 = p.cnt["ve"], p.cnt["te"], p.cnt["se"]
    ldix = {}
    stix = {}
    with nc.Block() as block:
        @block.gpsimd
        def _(e):
            qbarrier(p, e, own="gp")
            for n, (ic, off, w) in enumerate(jobs):
                if n >= 2:
                    p.qdrain(e, "gp")
                    e.wait_ge(p.sem["ve"], ve0 + (n - 1))
                sl = n % 2
                ng = w // GRP
                p.dma(e, "gp",
                      sb["dq_b32"][:, sl * 2752: sl * 2752 + w // 2],
                      dr[bname][128 * ic:128 * (ic + 1),
                                off // 2: off // 2 + w // 2])
                p.dma(e, "gp",
                      sb["c_zs"][:, sl * 192: sl * 192 + ng],
                      dr[zname][128 * ic:128 * (ic + 1),
                                off // GRP: off // GRP + ng])
                ldix[n] = p.dma(
                    e, "gp",
                    sb["c_zs"][:, sl * 192 + 96: sl * 192 + 96 + ng],
                    dr[sname][128 * ic:128 * (ic + 1),
                              off // GRP: off // GRP + ng])
            p.qdrain(e, "gp")

        @block.sync
        def _(e):
            for n, (ic, off, w) in enumerate(jobs):
                sl = n % 2
                e.wait_ge(p.sem["se"], se0 + cumt[n + 1])
                if blocks:
                    stix[n] = p.dma(
                        e, "sync",
                        dr[oname][ic],
                        sb["tstage"][:, sl * 5504: sl * 5504 + w])
                else:
                    stix[n] = p.dma(
                        e, "sync",
                        bass.AP(dr[oname], off * nout + 128 * ic,
                                [[nout, 128], [128 * nout, w // 128],
                                 [1, 128]]),
                        bass.AP(sb["tstage"], sl * 5504,
                                [[2 * 5504, 128], [128, w // 128], [1, 128]]))
                p.qdrain(e, "sync")
            p.qdrain(e, "sync")

        @block.tensor
        def _(e):
            for n, (ic, off, w) in enumerate(jobs):
                sl = n % 2
                e.wait_ge(p.sem["ve"], ve0 + (n + 1))
                for kt in range(w // 128):
                    g = cumt[n] + kt
                    if g >= 8:
                        e.wait_ge(p.sem["se"], se0 + g - 7)
                    e.transpose(
                        PS[g % 8][:, 0:128],
                        sb["dq_wf"][:, sl * 5504 + kt * 128:
                                    sl * 5504 + kt * 128 + 128],
                        sb["c_ident"][:, :]).then_inc(p.sem["te"], 1)
                    p.bump("te", 1)

        @block.scalar
        def _(e):
            for n, (ic, off, w) in enumerate(jobs):
                sl = n % 2
                if n >= 2:
                    p.wait_dma(e, "sync", stix[n - 2])
                for kt in range(w // 128):
                    g = cumt[n] + kt
                    e.wait_ge(p.sem["te"], te0 + g + 1)
                    e.copy(sb["tstage"][:, sl * 5504 + kt * 128:
                                        sl * 5504 + kt * 128 + 128],
                           PS[g % 8][:, 0:128]).then_inc(p.sem["se"], 1)
                    p.bump("se", 1)

        @block.vector
        def _(e):
            for n, (ic, off, w) in enumerate(jobs):
                sl = n % 2
                ng = w // GRP
                p.wait_dma(e, "gp", ldix[n])
                if n >= 2:
                    e.wait_ge(p.sem["te"], te0 + cumt[n - 1])
                bt = sb["dq_b32"][:, sl * 2752: sl * 2752 + w // 2]
                zb = bass.AP(sb["c_zs"], sl * 192, [[384, 128], [1, ng], [0, 32]])
                sc = bass.AP(sb["c_zs"], sl * 192 + 96,
                             [[384, 128], [1, ng], [0, 32]])
                nib = sb["dq_nib"][:, 0:w // 2]
                for half, op_, ofs in ((0, OP.logical_shift_right, 4),
                                       (1, OP.bitwise_and, 15)):
                    e.tensor_scalar(nib, bt, ofs, None, op_)
                    d1 = bass.AP(sb["dq_d1"], half, [[5504, 128], [GRP, ng], [2, 32]])
                    n3 = bass.AP(sb["dq_nib"], 0, [[2752, 128], [32, ng], [1, 32]])
                    e.tensor_tensor(d1, n3, zb, OP.subtract)
                    wq_ = bass.AP(sb["dq_wf"], sl * 5504 + half,
                                  [[11008, 128], [GRP, ng], [2, 32]])
                    ins = e.tensor_tensor(wq_, d1, sc, OP.mult)
                ins.then_inc(p.sem["ve"], 1)
                p.bump("ve", 1)


def stage_ln(p: P, srcname, tap=None):
    nc, sb, dr = p.nc, p.sb, p.dram
    PS = p.PS

    def xsl(kc, tt):
        return sb["xln"][:, kc * T + tt * 512: kc * T + tt * 512 + 512]

    ldix = {}
    for tt in range(4):
        te0, ve0, se0 = (p.cnt[k] for k in ("te", "ve", "se"))
        with nc.Block() as block:
            @block.gpsimd
            def _(e):
                if tt == 0:
                    qbarrier(p, e, own="gp")
                ldix[tt] = p.dma(
                    e, "gp",
                    bass.AP(sb["xln"], tt * 512,
                            [[NKC * T, 128], [T, NKC], [1, 512]]),
                    bass.AP(dr[srcname], tt * 512,
                            [[T, 128], [128 * T, NKC], [1, 512]]))
                p.qdrain(e, "gp")

            @block.tensor
            def _(e):
                p.wait_dma(e, "gp", ldix[tt])
                for kc in range(NKC):
                    ins = e.matmul(PS[6][0:1, :], sb["c_ones_bf"][:, :],
                                   xsl(kc, tt), start=(kc == 0),
                                   stop=(kc == NKC - 1))
                ins.then_inc(p.sem["te"], 1)
                p.bump("te", 1)
                for kc in range(NKC):
                    e.wait_ge(p.sem["ve"], ve0 + kc + 1)
                    e.matmul(PS[7][0:1, :], sb["c_ones_bf"][:, :],
                             sb["obuf"][:, (kc % 2) * 512:(kc % 2) * 512 + 512],
                             start=(kc == 0), stop=(kc == NKC - 1)
                             ).then_inc(p.sem["te"], 1)
                    p.bump("te", 1)
                e.wait_ge(p.sem["ve"], ve0 + NKC + 2)
                e.matmul(PS[6][:, :], sb["c_ones_f"][:, :],
                         sb["r_rows"][0:1, 512:1024], start=True,
                         stop=True).then_inc(p.sem["te"], 1)
                e.matmul(PS[7][:, :], sb["c_ones_f"][:, :],
                         sb["r_rows"][0:1, 0:512], start=True,
                         stop=True).then_inc(p.sem["te"], 1)
                p.bump("te", 2)

            @block.vector
            def _(e):
                p.wait_dma(e, "gp", ldix[tt])
                for kc in range(NKC):
                    if kc >= 2:
                        e.wait_ge(p.sem["te"], te0 + 1 + (kc - 1))
                    e.tensor_mul(
                        sb["obuf"][:, (kc % 2) * 512:(kc % 2) * 512 + 512],
                        xsl(kc, tt), xsl(kc, tt)).then_inc(p.sem["ve"], 1)
                    p.bump("ve", 1)
                r = sb["r_rows"]
                e.wait_ge(p.sem["te"], te0 + 1 + NKC)
                e.tensor_scalar(r[0:1, 1024:1536], PS[6][0:1, :], 1.0 / D,
                                None, OP.mult)
                e.tensor_scalar(r[0:1, 1536:2048], PS[7][0:1, :], 1.0 / D,
                                None, OP.mult)
                e.tensor_mul(r[0:1, 0:512], r[0:1, 1024:1536],
                             r[0:1, 1024:1536])
                e.tensor_sub(r[0:1, 1536:2048], r[0:1, 1536:2048],
                             r[0:1, 0:512])
                e.tensor_scalar(r[0:1, 1536:2048], r[0:1, 1536:2048],
                                LN_EPS, None, OP.add).then_inc(p.sem["ve"], 1)
                p.bump("ve", 1)
                e.wait_ge(p.sem["se"], se0 + 1)
                e.reciprocal(r[0:1, 512:1024], r[0:1, 0:512])
                e.scalar_tensor_tensor(
                    r[0:1, 0:512], r[0:1, 1024:1536], -1.0,
                    r[0:1, 512:1024], OP.mult, OP.mult
                ).then_inc(p.sem["ve"], 1)
                p.bump("ve", 1)
                e.wait_ge(p.sem["se"], se0 + 3)
                for kc in range(NKC):
                    e.tensor_mul(sb["fbuf"][:, 0:512], xsl(kc, tt),
                                 sb["fbuf"][:, 1024:1536])
                    ins = e.tensor_add(xsl(kc, tt), sb["fbuf"][:, 0:512],
                                       sb["fbuf"][:, 1536:2048])
                ins.then_inc(p.sem["ve"], 1)
                p.bump("ve", 1)

            @block.scalar
            def _(e):
                r = sb["r_rows"]
                e.wait_ge(p.sem["ve"], ve0 + NKC + 1)
                e.activation(r[0:1, 0:512], r[0:1, 1536:2048],
                             AF.Sqrt).then_inc(p.sem["se"], 1)
                p.bump("se", 1)
                e.wait_ge(p.sem["te"], te0 + NKC + 3)
                e.copy(sb["fbuf"][:, 1024:1536], PS[6][:, :]).then_inc(
                    p.sem["se"], 1)
                e.copy(sb["fbuf"][:, 1536:2048], PS[7][:, :]).then_inc(
                    p.sem["se"], 1)
                p.bump("se", 2)

    if tap is not None and tap in p.debug:
        ve_done = p.cnt["ve"]
        with nc.Block() as block:
            @block.sync
            def _(e):
                e.wait_ge(p.sem["ve"], ve_done)
                p.dma(e, "sync",
                      bass.AP(dr[tap], 0, [[T, 128], [128 * T, NKC], [1, T]]),
                      sb["xln"][:, :])
                p.qdrain(e, "sync")


def stage_load_arena(p: P, srcname):
    nc, sb, dr = p.nc, p.sb, p.dram
    with nc.Block() as block:
        @block.sync
        def _(e):
            qbarrier(p, e, own="sync")
            for q in range(4):
                p.dma(e, "sync",
                      bass.AP(sb["xln"], q * 8 * T,
                              [[NKC * T, 128], [T, 8], [1, T]]),
                      bass.AP(dr[srcname], q * 8 * 128 * T,
                              [[T, 128], [128 * T, 8], [1, T]]))
            p.qdrain(e, "sync")
    p.arena_ready = p.qn["sync"]


def stage_proj(p: P, wname, ngroups, mode, dst, xold=None):
    nc, sb, dr = p.nc, p.sb, p.dram
    PS = p.PS
    te0, ve0, se0 = (p.cnt[k] for k in ("te", "ve", "se"))

    def asl(kc, tt):
        return sb["xln"][:, kc * T + tt * 512: kc * T + tt * 512 + 512]

    tepg = 2 if mode == "rope" else 1
    vepg = 8 if mode == "rope" else 4
    wix = {}
    xix = {}
    six = {}
    with nc.Block() as block:
        @block.gpsimd
        def _(e):
            qbarrier(p, e, own="gp")
            for ng in range(ngroups):
                sl = ng % 2
                if ng >= 2:
                    p.qdrain(e, "gp")
                    e.wait_ge(p.sem["te"], te0 + tepg * (ng - 1))
                wix[ng] = p.dma(
                    e, "gp",
                    sb["wslab"][:, sl * 4096:(sl + 1) * 4096],
                    bass.AP(dr[wname], 128 * ng,
                            [[dr[wname].shape[1], 128],
                             [128 * dr[wname].shape[1], NKC], [1, 128]]))
            p.qdrain(e, "gp")

        @block.sync
        def _(e):
            for ng in range(ngroups):
                if mode == "resid":
                    for tt in range(6):
                        if tt < 4:
                            n = ng * 4 + tt
                            if n >= 2:
                                p.qdrain(e, "sync")
                                e.wait_ge(p.sem["ve"], ve0 + 4 * ng + tt - 1)
                            xix[n] = p.dma(
                                e, "sync",
                                sb["fbuf"][:, (n % 2) * 512:(n % 2) * 512 + 512],
                                dr[xold][128 * ng:128 * (ng + 1),
                                         tt * 512:(tt + 1) * 512])
                        if tt >= 2:
                            ns = ng * 4 + tt - 2
                            p.qdrain(e, "sync")
                            e.wait_ge(p.sem["ve"], ve0 + ns + 1 - 4 * ng + 4 * ng)
                            six[ns] = p.dma(
                                e, "sync",
                                dr[dst][128 * ng:128 * (ng + 1),
                                        (tt - 2) * 512:(tt - 1) * 512],
                                sb["fbuf"][:, 1024 + (ns % 2) * 512:
                                           1024 + (ns % 2) * 512 + 512])
                else:
                    p.qdrain(e, "sync")
                    e.wait_ge(p.sem["ve"], ve0 + vepg * ng + vepg)
                    if mode == "store":
                        osl = (ng % 2) * 2048
                        six[ng] = p.dma(
                            e, "sync",
                            dr[dst][128 * ng:128 * (ng + 1), :],
                            sb["obuf"][:, osl:osl + 2048])
                    else:
                        six[ng] = p.dma(
                            e, "sync",
                            dr[dst][128 * ng:128 * (ng + 1), :],
                            sb["obuf"][:, 2048:4096])
            p.qdrain(e, "sync")

        @block.tensor
        def _(e):
            for ng in range(ngroups):
                sl = ng % 2
                q = (ng % 2) * 4 if mode != "rope" else 0
                p.wait_dma(e, "gp", wix[ng])
                if (mode != "rope" and ng >= 2) or (mode == "rope" and ng >= 1):
                    back = 1 if mode == "rope" else 2
                    e.wait_ge(p.sem["ve"], ve0 + vepg * (ng - back) + vepg)
                for kc in range(NKC):
                    for tt in range(4):
                        ins = e.matmul(PS[q + tt][:, :],
                                       sb["wslab"][:, sl * 4096 + kc * 128:
                                                   sl * 4096 + kc * 128 + 128],
                                       asl(kc, tt), start=(kc == 0),
                                       stop=(kc == NKC - 1))
                ins.then_inc(p.sem["te"], 1)
                p.bump("te", 1)
                if mode == "rope":
                    e.wait_ge(p.sem["ve"], ve0 + vepg * ng + 4)
                    for tt in range(4):
                        ins = e.matmul(PS[4 + tt][:, :], sb["c_smat"][:, :],
                                       sb["obuf"][:, tt * 512:(tt + 1) * 512],
                                       start=True, stop=True)
                    ins.then_inc(p.sem["te"], 1)
                    p.bump("te", 1)

        @block.vector
        def _(e):
            for ng in range(ngroups):
                q = (ng % 2) * 4 if mode != "rope" else 0
                if mode == "store":
                    e.wait_ge(p.sem["te"], te0 + tepg * ng + 1)
                    if ng >= 2:
                        p.wait_dma(e, "sync", six[ng - 2])
                    osl = (ng % 2) * 2048
                    for tt in range(4):
                        e.tensor_copy(
                            sb["obuf"][:, osl + tt * 512: osl + tt * 512 + 512],
                            PS[q + tt][:, :]).then_inc(p.sem["ve"], 1)
                        p.bump("ve", 1)
                elif mode == "resid":
                    e.wait_ge(p.sem["te"], te0 + tepg * ng + 1)
                    for tt in range(4):
                        n = ng * 4 + tt
                        p.wait_dma(e, "sync", xix[n])
                        if n >= 2:
                            p.wait_dma(e, "sync", six[n - 2])
                        fs = (n % 2) * 512
                        e.tensor_tensor(
                            sb["fbuf"][:, 1024 + fs:1024 + fs + 512],
                            PS[q + tt][:, :], sb["fbuf"][:, fs:fs + 512],
                            OP.add).then_inc(p.sem["ve"], 1)
                        p.bump("ve", 1)
                else:
                    e.wait_ge(p.sem["te"], te0 + tepg * ng + 1)
                    if ng >= 1:
                        p.wait_dma(e, "sync", six[ng - 1])
                    for tt in range(4):
                        e.tensor_copy(
                            sb["obuf"][:, tt * 512: tt * 512 + 512],
                            PS[tt][:, :]).then_inc(p.sem["ve"], 1)
                        p.bump("ve", 1)
                    e.wait_ge(p.sem["te"], te0 + tepg * ng + 2)
                    for tt in range(4):
                        csl = sb["cs_fm"][:, tt * 512: tt * 512 + 512]
                        ssl = sb["cs_fm"][:, T + tt * 512: T + tt * 512 + 512]
                        e.tensor_mul(sb["fbuf"][:, 0:512],
                                     sb["obuf"][:, tt * 512: tt * 512 + 512],
                                     csl)
                        e.tensor_mul(sb["fbuf"][:, 512:1024],
                                     PS[4 + tt][:, :], ssl)
                        e.tensor_add(
                            sb["obuf"][:, 2048 + tt * 512: 2048 + tt * 512 + 512],
                            sb["fbuf"][:, 0:512],
                            sb["fbuf"][:, 512:1024]).then_inc(p.sem["ve"], 1)
                        p.bump("ve", 1)


def stage_kv_a(p: P):
    nc, sb, dr = p.nc, p.sb, p.dram
    PS = p.PS
    te0, ve0 = p.cnt["te"], p.cnt["ve"]
    wix = {}
    six = {}
    with nc.Block() as block:
        @block.gpsimd
        def _(e):
            qbarrier(p, e, own="gp")
            wix[0] = p.dma(e, "gp", sb["wkv_sb"][:, 0:4096],
                           bass.AP(dr["wkv_a"], 0,
                                   [[256, 128], [128 * 256, NKC], [1, 128]]))
            wix[1] = p.dma(e, "gp", sb["wkv_sb"][:, 4096:8192],
                           bass.AP(dr["wkv_a"], 128,
                                   [[256, 128], [128 * 256, NKC], [1, 128]]))
            p.qdrain(e, "gp")

        @block.sync
        def _(e):
            p.qdrain(e, "sync")
            e.wait_ge(p.sem["ve"], ve0 + 4)
            six[0] = p.dma(e, "sync", dr["ka_loc"][:, :],
                           sb["obuf"][:, 0:2048])
            for tcg in range(2):
                p.qdrain(e, "sync")
                e.wait_ge(p.sem["ve"], ve0 + 4 + 8 * (tcg + 1))
                six[1 + tcg] = p.dma(
                    e, "sync",
                    bass.AP(dr["va_loc"], tcg * 1024 * 128,
                            [[128, 128], [128 * 128, 8], [1, 128]]),
                    bass.AP(sb["kvstage"], 0,
                            [[1024, 128], [128, 8], [1, 128]]))
            p.qdrain(e, "sync")

        @block.tensor
        def _(e):
            p.wait_dma(e, "gp", wix[1])
            for kc in range(NKC):
                for tt in range(4):
                    ins = e.matmul(PS[tt][:, :],
                                   sb["wkv_sb"][:, kc * 128: kc * 128 + 128],
                                   sb["xln"][:, kc * T + tt * 512:
                                             kc * T + tt * 512 + 512],
                                   start=(kc == 0), stop=(kc == NKC - 1))
            ins.then_inc(p.sem["te"], 1)
            p.bump("te", 1)
            for tcg in range(2):
                e.wait_ge(p.sem["ve"], ve0 + 4 + 8 * tcg)
                for kc in range(NKC):
                    for t8 in range(8):
                        tc = tcg * 8 + t8
                        ins = e.matmul(
                            PS[t8][:, 0:128],
                            sb["xln"][:, kc * T + tc * 128: kc * T + tc * 128 + 128],
                            sb["wkv_sb"][:, 4096 + kc * 128: 4096 + kc * 128 + 128],
                            start=(kc == 0), stop=(kc == NKC - 1))
                ins.then_inc(p.sem["te"], 1)
                p.bump("te", 1)

        @block.vector
        def _(e):
            e.wait_ge(p.sem["te"], te0 + 1)
            for tt in range(4):
                e.tensor_copy(sb["obuf"][:, tt * 512:(tt + 1) * 512],
                              PS[tt][:, :]).then_inc(p.sem["ve"], 1)
                p.bump("ve", 1)
            for tcg in range(2):
                e.wait_ge(p.sem["te"], te0 + 2 + tcg)
                if tcg == 1:
                    p.wait_dma(e, "sync", six[1])
                for t8 in range(8):
                    e.tensor_copy(
                        sb["kvstage"][:, t8 * 128: t8 * 128 + 128],
                        PS[t8][:, 0:128]).then_inc(p.sem["ve"], 1)
                    p.bump("ve", 1)


def stage_vb(p: P):
    nc, sb, dr = p.nc, p.sb, p.dram
    PS = p.PS
    te0, ve0 = p.cnt["te"], p.cnt["ve"]
    wix = {}
    six = {}
    with nc.Block() as block:
        @block.gpsimd
        def _(e):
            qbarrier(p, e, own="gp")
            for tcg in range(4):
                for kc in range(NKC):
                    n = tcg * NKC + kc
                    if n >= 2:
                        p.qdrain(e, "gp")
                        e.wait_ge(p.sem["te"], te0 + (n - 1))
                    sl = n % 2
                    wix[n] = p.dma(
                        e, "gp",
                        sb["wslab"][:, sl * 4096: sl * 4096 + 1024],
                        dr["wv_b"][128 * kc:128 * (kc + 1), :])
            p.qdrain(e, "gp")

        @block.sync
        def _(e):
            for tcg in range(4):
                p.qdrain(e, "sync")
                e.wait_ge(p.sem["ve"], ve0 + 8 * (tcg + 1))
                six[tcg] = p.dma(
                    e, "sync",
                    bass.AP(dr["vb_loc"], tcg * 512 * 1024,
                            [[1024, 128], [128 * 1024, 4], [1, 1024]]),
                    bass.AP(sb["obuf"], 0,
                            [[4096, 128], [1024, 4], [1, 1024]]))
            p.qdrain(e, "sync")

        @block.tensor
        def _(e):
            for tcg in range(4):
                if tcg >= 1:
                    e.wait_ge(p.sem["ve"], ve0 + 8 * tcg)
                for kc in range(NKC):
                    n = tcg * NKC + kc
                    sl = n % 2
                    p.wait_dma(e, "gp", wix[n])
                    for t4 in range(4):
                        tc = tcg * 4 + t4
                        for hf in range(2):
                            ins = e.matmul(
                                PS[t4 * 2 + hf][:, :],
                                sb["xln"][:, kc * T + tc * 128:
                                          kc * T + tc * 128 + 128],
                                sb["wslab"][:, sl * 4096 + hf * 512:
                                            sl * 4096 + hf * 512 + 512],
                                start=(kc == 0), stop=(kc == NKC - 1))
                    ins.then_inc(p.sem["te"], 1)
                    p.bump("te", 1)

        @block.vector
        def _(e):
            for tcg in range(4):
                e.wait_ge(p.sem["te"], te0 + NKC * (tcg + 1))
                if tcg >= 1:
                    p.wait_dma(e, "sync", six[tcg - 1])
                for t4 in range(4):
                    for hf in range(2):
                        e.tensor_copy(
                            sb["obuf"][:, t4 * 1024 + hf * 512:
                                       t4 * 1024 + hf * 512 + 512],
                            PS[t4 * 2 + hf][:, :]).then_inc(p.sem["ve"], 1)
                        p.bump("ve", 1)


def stage_flagpoll(p: P, phase, copies):
    nc, sb, dr = p.nc, p.sb, p.dram
    with nc.Block() as block:
        @block.sync
        def _(e):
            qbarrier(p, e, sems=(), own="sync")
            for shname, locname in copies:
                p.dma(e, "sync",
                      dr[shname][bass.ds(p.par_snap["sync"], 1)],
                      dr[locname].ap())
            p.qdrain(e, "sync")

        @block.gpsimd
        def _(e):
            e.wait_ge(p.sem["sqp"], p.qn["sync"])
            e.dma_start(
                out=dr["sh_flag"][bass.ds(p.par_snap["gp"], 1),
                                  phase:phase + 1],
                in_=sb["c_nonce"][0:1, phase:phase + 1],
            ).then_inc(p.sem["pf"], 16)
            with (e.register(f"pr{phase}") as pr, e.register(f"pe{phase}") as pe,
                  e.register(f"nd{phase}") as nd):
                e.reg_load(pe, sb["c_nonce"][0:1, phase:phase + 1])
                e.memset(sb["r_poll"][0:1, 0:1], 0)
                e.reg_mov(nd, 1)
                with e.While(nd):
                    e.dma_start(
                        out=sb["r_poll"][0:1, 0:1],
                        in_=dr["sh_flag"][bass.ds(p.opar_snap["gp"], 1),
                                          phase:phase + 1]
                    ).then_inc(p.sem["pf"], 16)
                    e.nop(cycle_cnt=30000)
                    e.reg_load(pr, sb["r_poll"][0:1, 0:1])
                    e.reg_alu(nd, pr, pe, OP.not_equal)
            e.sem_inc(p.sem["pd"], 1)
            p.poll_done = getattr(p, "poll_done", 0) + 1


def stage_cossin(p: P):
    nc, sb, dr = p.nc, p.sb, p.dram
    PS = p.PS
    TWO_PI = 2.0 * math.pi
    ve0, se0 = p.cnt["ve"], p.cnt["se"]
    ldp = {}
    with nc.Block() as block:
        @block.sync
        def _(e):
            qbarrier(p, e, own="sync")
            ldp[0] = p.dma(e, "sync", sb["c_pos"][:, :], dr["pos"][:, :])
            p.qdrain(e, "sync")

        @block.gpsimd
        def _(e):
            e.iota(sb["r_rows"][0:1, 0:64], [[1, 64]], base=0,
                   channel_multiplier=0,
                   allow_small_or_imprecise_dtypes=True).then_inc(
                       p.sem["sm"], 1)
            p.bump("sm", 1)

        @block.scalar
        def _(e):
            e.wait_ge(p.sem["sm"], p.cnt["sm"])
            e.activation(sb["r_rows"][0:1, 64:128], sb["r_rows"][0:1, 0:64],
                         AF.Exp, scale=-math.log(500000.0) / 64.0
                         ).then_inc(p.sem["se"], 1)
            p.bump("se", 1)

        @block.vector
        def _(e):
            e.wait_ge(p.sem["se"], se0 + 1)
            r = sb["r_rows"]
            f0 = r[0:1, 64:128]
            e.tensor_scalar(r[0:1, 128:192], f0, TWO_PI / 8192.0, None,
                            OP.is_lt)
            e.tensor_scalar(r[0:1, 192:256], f0, TWO_PI / 2048.0, None,
                            OP.is_gt)
            e.tensor_scalar(r[0:1, 256:320], f0, 8192.0 / (TWO_PI * 3.0),
                            -1.0 / 3.0, OP.mult, OP.add)
            e.tensor_scalar(r[0:1, 256:320], r[0:1, 256:320], 31.0 / 32.0,
                            1.0 / 32.0, OP.mult, OP.add)
            e.tensor_mul(r[0:1, 256:320], r[0:1, 256:320], f0)
            e.tensor_scalar(r[0:1, 320:384], r[0:1, 128:192], -1.0, 1.0,
                            OP.mult, OP.add)
            e.tensor_scalar(r[0:1, 384:448], r[0:1, 192:256], -1.0, 1.0,
                            OP.mult, OP.add)
            e.tensor_mul(r[0:1, 320:384], r[0:1, 320:384], r[0:1, 384:448])
            e.tensor_scalar(r[0:1, 384:448], f0, 1.0 / 32.0, None, OP.mult)
            e.tensor_mul(r[0:1, 384:448], r[0:1, 384:448], r[0:1, 128:192])
            e.tensor_mul(r[0:1, 448:512], f0, r[0:1, 192:256])
            e.tensor_add(r[0:1, 384:448], r[0:1, 384:448], r[0:1, 448:512])
            e.tensor_mul(r[0:1, 256:320], r[0:1, 256:320], r[0:1, 320:384])
            e.tensor_add(r[0:1, 384:448], r[0:1, 384:448], r[0:1, 256:320])
            e.tensor_copy(sb["c_invf"][0:1, 0:64], r[0:1, 384:448])
            e.tensor_copy(sb["c_invf"][0:1, 64:128], r[0:1, 384:448]
                          ).then_inc(p.sem["ve"], 1)
            p.bump("ve", 1)

    for tt in range(4):
        for fn in range(2):
            te0b, ve0b, se0b = p.cnt["te"], p.cnt["ve"], p.cnt["se"]
            with nc.Block() as block:
                @block.tensor
                def _(e):
                    e.wait_ge(p.sem["ve"], ve0 + 1)
                    p.wait_dma(e, "sync", ldp[0])
                    if fn == 1 or tt > 0:
                        e.wait_ge(p.sem["se"], se0b)
                    e.matmul(PS[6][:, :], sb["c_invf"][0:1, :],
                             sb["c_pos"][0:1, tt * 512:(tt + 1) * 512],
                             start=True, stop=True).then_inc(p.sem["te"], 1)
                    p.bump("te", 1)

                @block.vector
                def _(e):
                    e.wait_ge(p.sem["te"], te0b + 1)
                    f = sb["fbuf"]
                    e.tensor_scalar(f[:, 0:512], PS[6][:, :], 1.0 / TWO_PI,
                                    0.25 if fn == 0 else 0.0, OP.mult, OP.add)
                    e.tensor_copy(sb["i512"][:, :], f[:, 0:512])
                    e.tensor_copy(f[:, 512:1024], sb["i512"][:, :])
                    e.tensor_sub(f[:, 0:512], f[:, 0:512], f[:, 512:1024])
                    e.tensor_scalar(f[:, 512:1024], f[:, 0:512], 0.5, None,
                                    OP.is_gt)
                    e.tensor_sub(f[:, 0:512], f[:, 0:512], f[:, 512:1024])
                    e.tensor_scalar(f[:, 512:1024], f[:, 0:512], -0.5, None,
                                    OP.is_lt)
                    e.tensor_add(f[:, 0:512], f[:, 0:512], f[:, 512:1024]
                                 ).then_inc(p.sem["ve"], 1)
                    p.bump("ve", 1)

                @block.scalar
                def _(e):
                    e.wait_ge(p.sem["ve"], ve0b + 1)
                    dst = sb["cs_fm"][:, fn * T + tt * 512:
                                      fn * T + tt * 512 + 512]
                    e.activation(dst, sb["fbuf"][:, 0:512], AF.Sin,
                                 scale=TWO_PI).then_inc(p.sem["se"], 1)
                    p.bump("se", 1)


CSC = [0, 12, 28, 56, 88]
TPH = 88


def stage_attn(p: P, which):
    nc, sb, dr = p.nc, p.sb, p.dram
    PS = p.PS
    a = which == "a"
    KVG, HPG = (1, 32) if a else (8, 4)
    qsrc = "qa" if a else "qb"
    cdst = "ctxa" if a else "ctxb"
    V = 3 if a else 2
    SEH = 5 if a else 4
    t10, t30, sx0, sm0, te0, ve0, se0 = (
        p.cnt[k] for k in ("t1", "t3", "sx", "sm", "te", "ve", "se"))

    def kof(h, s, idx):
        return TPH * h + CSC[s] + idx

    def kend(q):
        return TPH * (q // 4) + CSC[(q % 4) + 1]

    def rbev(q):
        return se0 + SEH * (q // 4) + (q % 4) + (2 if a else 1)

    def kvsrc_k(kvh, g):
        slot, loc = pos_block_map(g)
        if a:
            return dr["sh_ka"][slot][:, loc:loc + 512]
        return dr["sh_kb"][slot][kvh * 128:(kvh + 1) * 128, loc:loc + 512]

    def kvsrc_v(kvh, g):
        slot, loc = pos_block_map(g)
        if a:
            return bass.AP(dr["sh_va"], T * 128 * slot + loc * 128,
                           [[128, 128], [128 * 128, 4], [1, 128]])
        return bass.AP(dr["sh_vb"], T * 1024 * slot + loc * 1024 + kvh * 128,
                       [[1024, 128], [128 * 1024, 4], [1, 128]])

    kvix = {}
    qix = {}
    six = {}
    with nc.Block() as block:
        @block.sync
        def _(e):
            e.wait_ge(p.sem["pd"], p.poll_done)

            def store_q(h_, s_):
                q = h_ * 4 + s_
                p.qdrain(e, "sync")
                e.wait_ge(p.sem["ve"], ve0 + V * q + V)
                six[q] = p.dma(
                    e, "sync",
                    dr[cdst][128 * h_:128 * (h_ + 1),
                             512 * s_:512 * (s_ + 1)],
                    sb["co_sb"][:, (q % 2) * 512:(q % 2) * 512 + 512])

            for kvg in range(KVG):
                if kvg >= 1:
                    for s_ in range(4):
                        store_q(kvg * HPG - 1, s_)
                    p.qdrain(e, "sync")
                    e.wait_ge(p.sem["t3"], t30 + TPH * HPG * kvg)
                last = None
                for g in range(8):
                    p.dma(e, "sync", sb["k_sb"][:, 512 * g:512 * (g + 1)],
                          kvsrc_k(kvg, g))
                    last = p.dma(
                        e, "sync",
                        bass.AP(sb["v_sb"], 512 * g,
                                [[4096, 128], [128, 4], [1, 128]]),
                        kvsrc_v(kvg, g))
                for s_ in range(4):
                    if a:
                        ksrc = dr["ka_loc"][:, 512 * s_:512 * (s_ + 1)]
                    else:
                        ksrc = dr["kb_loc"][kvg * 128:(kvg + 1) * 128,
                                            512 * s_:512 * (s_ + 1)]
                    p.dma(e, "sync", sb["kdiag"][:, s_ * 512:(s_ + 1) * 512],
                          ksrc)
                    for kt in range(4):
                        row0 = 512 * s_ + 128 * kt
                        if a:
                            vsrc = dr["va_loc"][row0:row0 + 128, :]
                        else:
                            vsrc = dr["vb_loc"][row0:row0 + 128,
                                                kvg * 128:(kvg + 1) * 128]
                        last = p.dma(
                            e, "sync",
                            sb["vdiag"][:, s_ * 512 + kt * 128:
                                        s_ * 512 + kt * 128 + 128],
                            vsrc)
                kvix[kvg] = last
                for hh in range(HPG):
                    h = kvg * HPG + hh
                    if h >= 2:
                        p.qdrain(e, "sync")
                        e.wait_ge(p.sem["t1"], t10 + TPH * (h - 1))
                    qix[h] = p.dma(
                        e, "sync",
                        sb["q_sb"][:, (h % 2) * 2048:(h % 2) * 2048 + 2048],
                        dr[qsrc][128 * h:128 * (h + 1), :])
                    if hh >= 1:
                        for s_ in range(4):
                            store_q(h - 1, s_)
            for s_ in range(4):
                store_q(KVG * HPG - 1, s_)
            p.qdrain(e, "sync")

        @block.tensor
        def _(e):
            for kvg in range(KVG):
                for hh in range(HPG):
                    h = kvg * HPG + hh
                    p.wait_dma(e, "sync", qix[h])
                    p.wait_dma(e, "sync", kvix[kvg])
                    for s in range(4):
                        q = h * 4 + s
                        nst = CS[s] - 4
                        e.wait_ge(p.sem["ve"], ve0 + V * q)
                        for idx in range(CS[s]):
                            k = kof(h, s, idx)
                            diag = idx >= nst
                            m = idx - nst
                            if k >= 4:
                                e.wait_ge(p.sem["sx"], sx0 + k - 3)
                            if diag:
                                lw = sb["kdiag"][:, s * 512 + m * 128:
                                                 s * 512 + m * 128 + 128]
                            else:
                                lw = sb["k_sb"][:, 128 * idx:128 * idx + 128]
                            e.matmul(PS[2 + (k % 4)][:, :], lw,
                                     sb["q_sb"][:, (h % 2) * 2048 + 512 * s:
                                                (h % 2) * 2048 + 512 * s + 512],
                                     start=True, stop=True
                                     ).then_inc(p.sem["t1"], 1)
                            p.bump("t1", 1)
                            if a or diag:
                                e.wait_ge(p.sem["sm"],
                                          sm0 + (k + 1 if a else q * 4 + m + 1))
                            else:
                                e.wait_ge(p.sem["sx"], sx0 + k + 1)
                            if diag:
                                lv = sb["vdiag"][:, s * 512 + m * 128:
                                                 s * 512 + m * 128 + 128]
                            else:
                                lv = sb["v_sb"][:, 128 * idx:128 * idx + 128]
                            prs = sb["pr_sb"][:, (k % 4) * 512:(k % 4) * 512 + 512]
                            e.matmul(PS[0][:, :], lv, prs,
                                     start=(idx == 0), stop=(idx == CS[s] - 1))
                            e.matmul(PS[1][0:1, :], sb["c_ones_bf"][:, :], prs,
                                     start=(idx == 0), stop=(idx == CS[s] - 1)
                                     ).then_inc(p.sem["t3"], 1)
                            p.bump("t3", 1)
                        e.wait_ge(p.sem["ve"], ve0 + V * q + (3 if a else 2) - 1)
                        if q >= 1:
                            e.wait_ge(p.sem["se"], rbev(q - 1))
                        e.matmul(PS[6][:, :], sb["c_ones_f"][:, :],
                                 sb["r_rows"][0:1, (q % 2) * 512:
                                              (q % 2) * 512 + 512],
                                 start=True, stop=True).then_inc(p.sem["te"], 1)
                        p.bump("te", 1)

        @block.scalar
        def _(e):
            for kvg in range(KVG):
                for hh in range(HPG):
                    h = kvg * HPG + hh
                    asl = (h % 2) * 2560
                    if a:
                        if h >= 2:
                            e.wait_ge(p.sem["sm"], sm0 + TPH * (h - 1))
                        sl_ = SLOPES[h]
                        e.activation(sb["a_sb"][:, asl:asl + 512],
                                     sb["c_rel128"][:, :], AF.Exp, scale=sl_)
                        for m in range(4):
                            ins = e.activation(
                                sb["a_sb"][:, asl + (1 + m) * 512:
                                           asl + (2 + m) * 512],
                                sb["c_relmask"][:, m * 512:(m + 1) * 512],
                                AF.Exp, scale=sl_)
                        ins.then_inc(p.sem["se"], 1)
                        p.bump("se", 1)
                    for s in range(4):
                        q = h * 4 + s
                        nst = CS[s] - 4
                        for idx in range(CS[s]):
                            k = kof(h, s, idx)
                            diag = idx >= nst
                            e.wait_ge(p.sem["t1"], t10 + k + 1)
                            if a:
                                if k >= 4:
                                    e.wait_ge(p.sem["sm"], sm0 + k - 3)
                                if idx == 0:
                                    e.wait_ge(p.sem["ve"], ve0 + V * q + 1)
                                dst = sb["e_sb"][:, (k % 4) * 512:
                                                 (k % 4) * 512 + 512]
                                bias = (None if diag else
                                        sb["c_bias"][:, (q % 2) * 32 + idx:
                                                     (q % 2) * 32 + idx + 1])
                            else:
                                if k >= 4:
                                    e.wait_ge(p.sem["t3"], t30 + k - 3)
                                dst = sb["pr_sb"][:, (k % 4) * 512:
                                                  (k % 4) * 512 + 512]
                                bias = (None if diag else
                                        sb["c_offb"][:, s * 32 + idx:
                                                     s * 32 + idx + 1])
                            if bias is None:
                                ins = e.activation(dst, PS[2 + (k % 4)][:, :],
                                                   AF.Exp, scale=ISCALE)
                            else:
                                ins = e.activation(dst, PS[2 + (k % 4)][:, :],
                                                   AF.Exp, bias=bias,
                                                   scale=ISCALE)
                            ins.then_inc(p.sem["sx"], 1)
                            p.bump("sx", 1)
                        e.wait_ge(p.sem["te"], te0 + q + 1)
                        if q >= 2:
                            e.wait_ge(p.sem["ve"], ve0 + V * (q - 2) + V)
                        e.copy(sb["rb_sb"][:, (q % 2) * 512:(q % 2) * 512 + 512],
                               PS[6][:, :]).then_inc(p.sem["se"], 1)
                        p.bump("se", 1)

        @block.vector
        def _(e):
            for kvg in range(KVG):
                for hh in range(HPG):
                    h = kvg * HPG + hh
                    asl = (h % 2) * 2560
                    for s in range(4):
                        q = h * 4 + s
                        nst = CS[s] - 4
                        if a:
                            if q >= 2:
                                e.wait_ge(p.sem["sx"], sx0 + kend(q - 2))
                            e.tensor_scalar(
                                sb["c_bias"][:, (q % 2) * 32:(q % 2) * 32 + 32],
                                sb["c_offa"][:, s * 32:s * 32 + 32],
                                SLOPES[h], None, OP.mult
                            ).then_inc(p.sem["ve"], 1)
                            p.bump("ve", 1)
                        for idx in range(CS[s]):
                            k = kof(h, s, idx)
                            diag = idx >= nst
                            m = idx - nst
                            if a:
                                e.wait_ge(p.sem["sx"], sx0 + k + 1)
                                if k >= 4:
                                    e.wait_ge(p.sem["t3"], t30 + k - 3)
                                if idx == 0 and s == 0:
                                    e.wait_ge(p.sem["se"], se0 + SEH * h + 1)
                                aof = asl + ((1 + m) * 512 if diag else 0)
                                e.tensor_mul(
                                    sb["pr_sb"][:, (k % 4) * 512:
                                                (k % 4) * 512 + 512],
                                    sb["e_sb"][:, (k % 4) * 512:
                                               (k % 4) * 512 + 512],
                                    sb["a_sb"][:, aof:aof + 512]
                                ).then_inc(p.sem["sm"], 1)
                                p.bump("sm", 1)
                            elif diag:
                                e.wait_ge(p.sem["sx"], sx0 + k + 1)
                                prs = sb["pr_sb"][:, (k % 4) * 512:
                                                  (k % 4) * 512 + 512]
                                e.tensor_mul(prs, prs,
                                             sb["c_cmask"][:, m * 512:
                                                           (m + 1) * 512]
                                             ).then_inc(p.sem["sm"], 1)
                                p.bump("sm", 1)
                        e.wait_ge(p.sem["t3"], t30 + kend(q))
                        e.reciprocal(sb["r_rows"][0:1, (q % 2) * 512:
                                                  (q % 2) * 512 + 512],
                                     PS[1][0:1, :]).then_inc(p.sem["ve"], 1)
                        p.bump("ve", 1)
                        e.wait_ge(p.sem["se"], rbev(q))
                        if q >= 2 and (q - 2) in six:
                            p.wait_dma(e, "sync", six[q - 2])
                        e.tensor_tensor(
                            sb["co_sb"][:, (q % 2) * 512:(q % 2) * 512 + 512],
                            PS[0][:, :],
                            sb["rb_sb"][:, (q % 2) * 512:(q % 2) * 512 + 512],
                            OP.mult).then_inc(p.sem["ve"], 1)
                        p.bump("ve", 1)


def stage_ffn_gu(p: P):
    nc, sb, dr = p.nc, p.sb, p.dram
    PS = p.PS
    te0, ve0, se0 = (p.cnt[k] for k in ("te", "ve", "se"))

    def asl(kc, tt):
        return sb["xln"][:, kc * T + tt * 512: kc * T + tt * 512 + 512]

    wixg = {}
    wixu = {}
    six = {}
    with nc.Block() as block:
        @block.sync
        def _(e):
            qbarrier(p, e, own="sync")
            for ic in range(NIC):
                sl = ic % 2
                if ic >= 2:
                    p.qdrain(e, "sync")
                    e.wait_ge(p.sem["te"], te0 + 2 * (ic - 1))
                wixg[ic] = p.dma(e, "sync",
                                 sb["ffnblk"][:, sl * 4096:(sl + 1) * 4096],
                                 dr["wgS"][ic])
                wixu[ic] = p.dma(e, "sync",
                                 sb["ffnblk"][:, 8192 + sl * 4096:
                                              8192 + (sl + 1) * 4096],
                                 dr["wuS"][ic])
                if ic >= 1:
                    for tt in range(4):
                        n = (ic - 1) * 4 + tt
                        p.qdrain(e, "sync")
                        e.wait_ge(p.sem["ve"], ve0 + n + 1)
                        six[n] = p.dma(
                            e, "sync",
                            dr["hbuf"][128 * (ic - 1):128 * ic,
                                       tt * 512:(tt + 1) * 512],
                            sb["hstage"][:, (n % 2) * 512:(n % 2) * 512 + 512])
            for tt in range(4):
                n = (NIC - 1) * 4 + tt
                p.qdrain(e, "sync")
                e.wait_ge(p.sem["ve"], ve0 + n + 1)
                six[n] = p.dma(
                    e, "sync",
                    dr["hbuf"][128 * (NIC - 1):128 * NIC,
                               tt * 512:(tt + 1) * 512],
                    sb["hstage"][:, (n % 2) * 512:(n % 2) * 512 + 512])
            p.qdrain(e, "sync")

        @block.tensor
        def _(e):
            for ic in range(NIC):
                sl = ic % 2
                p.wait_dma(e, "sync", wixu[ic])
                if ic >= 1:
                    e.wait_ge(p.sem["se"], se0 + 4 * ic)
                for kc in range(NKC):
                    for tt in range(4):
                        ins = e.matmul(PS[tt][:, :],
                                       sb["ffnblk"][:, sl * 4096 + kc * 128:
                                                    sl * 4096 + kc * 128 + 128],
                                       asl(kc, tt), start=(kc == 0),
                                       stop=(kc == NKC - 1))
                ins.then_inc(p.sem["te"], 1)
                p.bump("te", 1)
                if ic >= 1:
                    e.wait_ge(p.sem["ve"], ve0 + 4 * ic)
                for kc in range(NKC):
                    for tt in range(4):
                        ins = e.matmul(PS[4 + tt][:, :],
                                       sb["ffnblk"][:, 8192 + sl * 4096 + kc * 128:
                                                    8192 + sl * 4096 + kc * 128 + 128],
                                       asl(kc, tt), start=(kc == 0),
                                       stop=(kc == NKC - 1))
                ins.then_inc(p.sem["te"], 1)
                p.bump("te", 1)

        @block.scalar
        def _(e):
            for ic in range(NIC):
                e.wait_ge(p.sem["te"], te0 + 2 * ic + 1)
                for tt in range(4):
                    if ic >= 1:
                        e.wait_ge(p.sem["ve"], ve0 + 4 * (ic - 1) + tt + 1)
                    e.activation(sb["obuf"][:, tt * 512:(tt + 1) * 512],
                                 PS[tt][:, :], AF.Silu).then_inc(p.sem["se"], 1)
                    p.bump("se", 1)

        @block.vector
        def _(e):
            for ic in range(NIC):
                e.wait_ge(p.sem["te"], te0 + 2 * ic + 2)
                for tt in range(4):
                    n = ic * 4 + tt
                    e.wait_ge(p.sem["se"], se0 + 4 * ic + tt + 1)
                    if n >= 2:
                        p.wait_dma(e, "sync", six[n - 2])
                    e.tensor_tensor(
                        sb["hstage"][:, (n % 2) * 512:(n % 2) * 512 + 512],
                        sb["obuf"][:, tt * 512:(tt + 1) * 512],
                        PS[4 + tt][:, :], OP.mult).then_inc(p.sem["ve"], 1)
                    p.bump("ve", 1)


def stage_ffn_down(p: P, xold, dst):
    nc, sb, dr = p.nc, p.sb, p.dram
    PS = p.PS
    te0, ve0 = p.cnt["te"], p.cnt["ve"]
    hix = {}
    wix = {}
    xix = {}
    six = {}
    with nc.Block() as block:
        @block.sync
        def _(e):
            qbarrier(p, e, own="sync")
            for tt in range(4):
                if tt >= 1:
                    p.qdrain(e, "sync")
                    e.wait_ge(p.sem["te"], te0 + NIC * 4 * tt)
                for half in range(2):
                    hix[tt] = p.dma(
                        e, "sync",
                        bass.AP(sb["h_tt"], 43 * 512 * half,
                                [[NIC * 512, 128], [512, 43], [1, 512]]),
                        bass.AP(dr["hbuf"], 43 * 128 * T * half + tt * 512,
                                [[T, 128], [128 * T, 43], [1, 512]]))
                for ncg in range(4):
                    G = tt * 4 + ncg
                    for ic in range(NIC):
                        n = G * NIC + ic
                        sl = n % 2
                        if n >= 2:
                            p.qdrain(e, "sync")
                            e.wait_ge(p.sem["te"], te0 + n - 1)
                        wix[n] = p.dma(
                            e, "sync",
                            sb["wd_slab"][:, sl * 1024:(sl + 1) * 1024],
                            dr["wd_T"][128 * ic:128 * (ic + 1),
                                       1024 * ncg:1024 * (ncg + 1)])
                    for nc2 in range(10):
                        if nc2 < 8:
                            j = G * 8 + nc2
                            if j >= 2:
                                p.qdrain(e, "sync")
                                e.wait_ge(p.sem["ve"], ve0 + j - 1)
                            xix[j] = p.dma(
                                e, "sync",
                                sb["fbuf"][:, (j % 2) * 512:(j % 2) * 512 + 512],
                                dr[xold][1024 * ncg + 128 * nc2:
                                         1024 * ncg + 128 * (nc2 + 1),
                                         tt * 512:(tt + 1) * 512])
                        if nc2 >= 2:
                            js = G * 8 + nc2 - 2
                            p.qdrain(e, "sync")
                            e.wait_ge(p.sem["ve"], ve0 + js + 1)
                            six[js] = p.dma(
                                e, "sync",
                                dr[dst][1024 * ncg + 128 * (nc2 - 2):
                                        1024 * ncg + 128 * (nc2 - 1),
                                        tt * 512:(tt + 1) * 512],
                                sb["fbuf"][:, 1024 + (js % 2) * 512:
                                           1024 + (js % 2) * 512 + 512])
            p.qdrain(e, "sync")

        @block.tensor
        def _(e):
            for tt in range(4):
                for ncg in range(4):
                    G = tt * 4 + ncg
                    for ic in range(NIC):
                        n = G * NIC + ic
                        sl = n % 2
                        p.wait_dma(e, "sync", wix[n])
                        if ic == 0:
                            p.wait_dma(e, "sync", hix[tt])
                        for nc2 in range(8):
                            if ic == 0 and G >= 1:
                                e.wait_ge(p.sem["ve"],
                                          ve0 + 8 * (G - 1) + nc2 + 1)
                            ins = e.matmul(
                                PS[nc2][:, :],
                                sb["wd_slab"][:, sl * 1024 + nc2 * 128:
                                              sl * 1024 + nc2 * 128 + 128],
                                sb["h_tt"][:, ic * 512:(ic + 1) * 512],
                                start=(ic == 0), stop=(ic == NIC - 1))
                        ins.then_inc(p.sem["te"], 1)
                        p.bump("te", 1)

        @block.vector
        def _(e):
            for tt in range(4):
                for ncg in range(4):
                    G = tt * 4 + ncg
                    e.wait_ge(p.sem["te"], te0 + NIC * (G + 1))
                    for nc2 in range(8):
                        j = G * 8 + nc2
                        p.wait_dma(e, "sync", xix[j])
                        if j >= 2:
                            p.wait_dma(e, "sync", six[j - 2])
                        e.tensor_tensor(
                            sb["fbuf"][:, 1024 + (j % 2) * 512:
                                       1024 + (j % 2) * 512 + 512],
                            PS[nc2][:, :],
                            sb["fbuf"][:, (j % 2) * 512:(j % 2) * 512 + 512],
                            OP.add).then_inc(p.sem["ve"], 1)
                        p.bump("ve", 1)


def stage_final(p: P):
    nc = p.nc
    with nc.Block() as block:
        @block.sync
        def _(e):
            qbarrier(p, e, own="sync")
            if p.qn["gp"]:
                e.wait_ge(p.sem["gq"], 16 * p.qn["gp"])


def build_program(debug=(), upto=99):
    p = P(debug=debug)
    declare_io(p)
    alloc_sbuf(p)
    stages = [
        lambda: stage_init(p),
        lambda: stage_dequant(p, "gate_b", "gate_z", "gate_s", "wgS", FI, D, True),
        lambda: stage_dequant(p, "up_b", "up_z", "up_s", "wuS", FI, D, True),
        lambda: stage_dequant(p, "down_b", "down_z", "down_s", "wd_T", D, FI, False),
        lambda: stage_ln(p, "xin", tap="xln_dbg"),
        lambda: stage_proj(p, "wq_a", 32, "store", "qa"),
        lambda: stage_kv_a(p),
        lambda: stage_flagpoll(p, 0, [("sh_ka", "ka_loc"), ("sh_va", "va_loc")]),
        lambda: stage_attn(p, "a"),
        lambda: stage_load_arena(p, "ctxa"),
        lambda: stage_proj(p, "wo_a", 32, "resid", "x1", xold="xin"),
        lambda: stage_ln(p, "x1"),
        lambda: stage_ffn_gu(p),
        lambda: stage_ffn_down(p, "x1", "x2"),
        lambda: stage_ln(p, "x2"),
        lambda: stage_cossin(p),
        lambda: stage_proj(p, "wq_b", 32, "rope", "qb"),
        lambda: stage_proj(p, "wk_b", 8, "rope", "kb_loc"),
        lambda: stage_vb(p),
        lambda: stage_flagpoll(p, 1, [("sh_kb", "kb_loc"), ("sh_vb", "vb_loc")]),
        lambda: stage_attn(p, "b"),
        lambda: stage_load_arena(p, "ctxb"),
        lambda: stage_proj(p, "wo_b", 32, "resid", "x3", xold="x2"),
        lambda: stage_ln(p, "x3"),
        lambda: stage_ffn_gu(p),
        lambda: stage_ffn_down(p, "x3", "xout"),
    ]
    only = globals().get("ONLY_STAGES")
    for i, fn in enumerate(stages):
        if i >= upto:
            break
        if only is not None and i not in only:
            continue
        fn()
    stage_final(p)
    return p


def host_prepare(inputs, nonce):
    import ml_dtypes
    x = np.asarray(inputs["L0_attn_a_hidden_states"])

    def bf(a):
        return np.ascontiguousarray(a).astype(ml_dtypes.bfloat16)

    wq_a = np.ascontiguousarray(np.asarray(inputs["L0_attn_a_Wq"], np.float32))
    wkv_a = np.ascontiguousarray(np.asarray(inputs["L0_attn_a_Wkv"], np.float32))
    wo_a = np.ascontiguousarray(np.asarray(inputs["L0_attn_a_Wo"], np.float32))
    wq_b = np.ascontiguousarray(np.asarray(inputs["L0_attn_b_Wq"], np.float32).T)
    wk_b = np.ascontiguousarray(np.asarray(inputs["L0_attn_b_Wk"], np.float32).T)
    wv_b = np.ascontiguousarray(np.asarray(inputs["L0_attn_b_Wv"], np.float32).T)
    wo_b = np.ascontiguousarray(np.asarray(inputs["L0_attn_b_Wo"], np.float32).T)

    def u8view(a):
        a = np.ascontiguousarray(np.asarray(a, np.int32))
        return np.ascontiguousarray(
            a.view(np.uint8).reshape(a.shape[0], a.shape[1], 4)[:, :, 0])

    gate_b = u8view(inputs["L0_ffn_gate_Wq"])
    up_b = u8view(inputs["L0_ffn_up_Wq"])
    down_b = u8view(inputs["L0_ffn_down_Wq"])

    dk = np.arange(128)[:, None].astype(np.float32)
    dq = np.arange(512)[None, :].astype(np.float32)
    rel = dk - dq
    rel128 = bf(rel - 128.0)
    relmask = bf(np.stack([np.where(rel + 128 * m <= 0, rel + 128 * m, -1e9)
                           for m in range(4)]))
    cmask = bf(np.stack([(rel + 128 * m <= 0).astype(np.float32)
                         for m in range(4)]))
    smat = np.zeros((128, 128), np.float32)
    for m_ in range(64):
        smat[m_ + 64, m_] = -1.0
        smat[m_, m_ + 64] = 1.0
    smat = bf(smat)

    common = dict(
        wq_a=wq_a, wkv_a=wkv_a, wo_a=wo_a, wq_b=wq_b, wk_b=wk_b,
        wv_b=wv_b, wo_b=wo_b, gate_b=gate_b, up_b=up_b, down_b=down_b,
        gate_s=np.asarray(inputs["L0_ffn_gate_scale"], np.float32),
        gate_z=np.asarray(inputs["L0_ffn_gate_zero"], np.float32),
        up_s=np.asarray(inputs["L0_ffn_up_scale"], np.float32),
        up_z=np.asarray(inputs["L0_ffn_up_zero"], np.float32),
        down_s=np.asarray(inputs["L0_ffn_down_scale"], np.float32),
        down_z=np.asarray(inputs["L0_ffn_down_zero"], np.float32),
        rel128=rel128, relmask=relmask, cmask=cmask, smat=smat,
        ident=np.eye(128, dtype=np.float32),
        nonce=nonce,
    )

    in_maps = []
    shard_info = []
    for c in range(8):
        b, par = c // 2, c % 2
        toks = own_tokens(par)
        q0s = own_q0s(par)
        shard_info.append((b, toks))
        xc = np.ascontiguousarray(x[b][toks, :].T.astype(np.float32))
        offv = np.full((4, 32), NEGPAD, np.float32)
        offb = np.full((4, 32), NEG, np.float32)
        for s_ in range(4):
            nst = CS[s_] - 4
            for j in range(32):
                if j < nst and 128 * j < q0s[s_]:
                    offv[s_, j] = 128 * j - q0s[s_] + 128.0
                    offb[s_, j] = 0.0
        m = dict(common)
        m.update(
            xin=xc,
            offv_a=np.broadcast_to(offv.reshape(1, 128), (128, 128)).copy(),
            offv_b=np.broadcast_to(offb.reshape(1, 128), (128, 128)).copy(),
            pos=np.asarray(toks, np.float32).reshape(1, T),
        )
        in_maps.append(m)
    return in_maps, shard_info


_PROG_CACHE = {}


def kernel(**inputs):
    nonce = (np.random.randint(1, 2 ** 30, size=(1, 2)).astype(np.int32)
             + np.asarray([[0, 2 ** 30]], np.int32))
    in_maps, shard_info = host_prepare(inputs, nonce)
    key = (tuple(sorted(DEBUG)), UPTO)
    if key not in _PROG_CACHE:
        _PROG_CACHE[key] = build_program(debug=DEBUG, upto=UPTO)
    p = _PROG_CACHE[key]
    res = run_bass_kernel_spmd(p.nc, in_maps, core_ids=list(range(8)))
    out = np.zeros((4, 4096, 4096), np.float32)
    for c in range(8):
        b, toks = shard_info[c]
        out[b][np.asarray(toks), :] = np.asarray(
            res.results[c]["xout"], np.float32).T
    if DEBUG:
        kernel._debug_results = res.results
    return out


# version 03 (final)
# speedup vs baseline: 1.3041x; 1.3041x over previous
import math
from contextlib import ExitStack

import numpy as np

import concourse.bass as bass
import concourse.mybir as mybir
from concourse.bass_utils import run_bass_kernel_spmd

F32 = mybir.dt.float32
BF16 = mybir.dt.bfloat16
I32 = mybir.dt.int32
U8 = mybir.dt.uint8
AF = mybir.ActivationFunctionType
OP = mybir.AluOpType

D = 4096
T = 2048
NKC = 32
NTC = 16
FI = 11008
NIC = 86
GRP = 64
ISCALE = 1.0 / math.sqrt(128.0)
NEG = -1.0e30
NEGPAD = -1.0e35
CS = [12, 16, 28, 32]
_AB = 2.0 ** (-(2.0 ** -(math.log2(32) - 3)))
SLOPES = [_AB ** (i + 1) for i in range(32)]
LN_EPS = 1e-5

DEBUG = ()
UPTO = 99
TRANSPOSE_DOWN = True


def own_tokens(parity):
    if parity == 0:
        return list(range(0, 1024)) + list(range(3072, 4096))
    return list(range(1024, 3072))


def own_q0s(parity):
    return [0, 512, 3072, 3584] if parity == 0 else [1024, 1536, 2048, 2560]


def pos_block_map(g):
    if g in (0, 1):
        return (0, 512 * g)
    if g in (2, 3, 4, 5):
        return (1, 512 * (g - 2))
    return (0, 1024 + 512 * (g - 6))


class P:

    def __init__(self, debug=()):
        self.debug = set(debug)
        self.nc = bass.Bass("TRN2", target_bir_lowering=False,
                            detect_race_conditions=False)
        self.ctx = ExitStack()
        self.sem = {}
        self.cnt = {}
        for name in ["sq", "sqp", "gq", "gqp", "te", "ve", "se", "t1", "t3", "sx", "sm", "pf", "pd"]:
            self.sem[name] = self.ctx.enter_context(self.nc.semaphore(name))
            self.cnt[name] = 0
        self.sb = {}
        self.dram = {}
        self.pspar = 0
        self.par_snap = {}
        self.qn = {"sync": 0, "gp": 0}
        self.qpend = {"sync": [], "gp": []}
        self.qpub = {"sync": 0, "gp": 0}
        self.qdepth = 6
        self.poll_done = 0

    def dma(self, e, which, out, in_, transpose=False):
        raw = self.sem["sq" if which == "sync" else "gq"]
        if transpose:
            e.dma_start_transpose(out, in_).then_inc(raw, 16)
        else:
            e.dma_start(out=out, in_=in_).then_inc(raw, 16)
        idx = self.qn[which]
        self.qn[which] += 1
        self.qpend[which].append(idx)
        if len(self.qpend[which]) > self.qdepth:
            self._flush(e, which)
        return idx

    def _flush(self, e, which):
        if not self.qpend[which]:
            return
        raw = self.sem["sq" if which == "sync" else "gq"]
        pub = self.sem["sqp" if which == "sync" else "gqp"]
        e.wait_ge(raw, 16 * self.qn[which])
        e.sem_inc(pub, len(self.qpend[which]))
        self.qpub[which] = self.qn[which]
        self.qpend[which].clear()

    def qdrain(self, e, which):
        self._flush(e, which)

    def wait_dma(self, e, which, idx):
        e.wait_ge(self.sem["sqp" if which == "sync" else "gqp"], idx + 1)

    def bump(self, name, amt=1):
        self.cnt[name] += amt
        return self.cnt[name]

    def din(self, name, shape, dtype):
        self.dram[name] = self.nc.dram_tensor(name, shape, dtype,
                                              kind="ExternalInput")
        return self.dram[name]

    def dout(self, name, shape, dtype):
        self.dram[name] = self.nc.dram_tensor(name, shape, dtype,
                                              kind="ExternalOutput")
        return self.dram[name]

    def dtmp(self, name, shape, dtype, shared=False):
        kind = "ExternalOutput" if name in self.debug else "Internal"
        self.dram[name] = self.nc.dram_tensor(
            name, shape, dtype, kind=kind,
            addr_space="Shared" if shared else "Local")
        return self.dram[name]

    def sbuf(self, name, shape, dtype):
        self.sb[name] = self.ctx.enter_context(
            self.nc.sbuf_tensor(name, shape, dtype))
        return self.sb[name]

    def alias(self, name, shape, dtype, byte_off):
        base = self.nc.lookup_mloc(self.sb["big"]).addr
        self.sb[name] = self.nc.alloc_sbuf_tensor_at(
            name, shape, dtype, offset=base + byte_off)
        return self.sb[name]


def declare_io(p: P):
    p.din("xin", [D, T], F32)
    p.din("wq_a", [D, D], F32)
    p.din("wkv_a", [D, 256], F32)
    p.din("wo_a", [D, D], F32)
    p.din("wq_b", [D, D], F32)
    p.din("wk_b", [D, 1024], F32)
    p.din("wv_b", [D, 1024], F32)
    p.din("wo_b", [D, D], F32)
    p.din("gate_b", [FI, D // 2], U8)
    p.din("up_b", [FI, D // 2], U8)
    p.din("down_b", [D, FI // 2], U8)
    p.din("gate_s", [FI, D // GRP], F32)
    p.din("gate_z", [FI, D // GRP], F32)
    p.din("up_s", [FI, D // GRP], F32)
    p.din("up_z", [FI, D // GRP], F32)
    p.din("down_s", [D, FI // GRP], F32)
    p.din("down_z", [D, FI // GRP], F32)
    p.din("rel128", [128, 512], BF16)
    p.din("relmask", [4, 128, 512], BF16)
    p.din("cmask", [4, 128, 512], BF16)
    p.din("smat", [128, 128], BF16)
    p.din("ident", [128, 128], F32)
    p.din("offv_a", [128, 128], F32)
    p.din("offv_b", [128, 128], F32)
    p.din("pos", [1, T], F32)
    p.din("nonce", [1, 2], I32)

    p.dout("xout", [D, T], F32)

    p.dtmp("x1", [D, T], F32)
    p.dtmp("x2", [D, T], F32)
    p.dtmp("x3", [D, T], F32)
    p.dtmp("wd_T", [FI, D], BF16)
    p.dtmp("wgS", [NIC, 128, 4096], BF16)
    p.dtmp("wuS", [NIC, 128, 4096], BF16)
    p.dtmp("qa", [D, T], BF16)
    p.dtmp("qb", [D, T], BF16)
    p.dtmp("ctxa", [D, T], BF16)
    p.dtmp("ctxb", [D, T], BF16)
    p.dtmp("hbuf", [FI, T], BF16)
    p.dtmp("ka_loc", [128, T], BF16)
    p.dtmp("va_loc", [T, 128], BF16)
    p.dtmp("kb_loc", [1024, T], BF16)
    p.dtmp("vb_loc", [T, 1024], BF16)
    p.dtmp("sh_ka", [2, 128, T], BF16, shared=True)
    p.dtmp("sh_va", [2, T, 128], BF16, shared=True)
    p.dtmp("sh_kb", [2, 1024, T], BF16, shared=True)
    p.dtmp("sh_vb", [2, T, 1024], BF16, shared=True)
    p.dtmp("sh_flag", [2, 2], I32, shared=True)
    if "xln_dbg" in p.debug:
        p.dtmp("xln_dbg", [D, T], BF16)


def alloc_sbuf(p: P):
    p.sbuf("c_rel128", [128, 512], BF16)
    p.sbuf("c_relmask", [128, 4 * 512], BF16)
    p.sbuf("c_cmask", [128, 4 * 512], BF16)
    p.sbuf("c_smat", [128, 128], BF16)
    p.sbuf("c_ident", [128, 128], F32)
    p.sbuf("c_offa", [128, 128], F32)
    p.sbuf("c_offb", [128, 128], F32)

    p.sbuf("c_nonce", [1, 2], I32)
    p.sbuf("c_ones_bf", [128, 1], BF16)
    p.sbuf("c_ones_f", [1, 128], F32)
    p.sbuf("c_invf", [1, 128], F32)
    p.sbuf("c_bias", [128, 64], F32)
    p.sbuf("c_zs", [128, 384], F32)
    p.sbuf("r_rows", [1, 2048], F32)
    p.sbuf("r_poll", [1, 2], I32)
    p.sbuf("i512", [128, 512], I32)
    p.sbuf("fbuf", [128, 2048], F32)
    p.sbuf("obuf", [128, 4096], BF16)
    p.sbuf("hstage", [128, 1024], BF16)
    p.sbuf("kvstage", [128, 1024], BF16)
    p.sbuf("big", [128, 83968], BF16)
    rem = p.nc.sbuf_bytes_remaining
    assert rem >= 0, f"sbuf over budget by {-rem}"

    p.alias("dq_b32", [128, 2 * 2752], I32, 0)
    p.alias("dq_nib", [128, 2752], I32, 22016)
    p.alias("dq_d1", [128, 5504], F32, 33024)
    p.alias("dq_wf", [128, 2 * 5504], F32, 55040)
    p.alias("tstage", [128, 2 * 5504], BF16, 99072)
    p.alias("xln", [128, NKC * T], BF16, 0)
    p.alias("wslab", [128, 2 * 4096], BF16, 131072)
    p.alias("wkv_sb", [128, 8192], BF16, 131072)
    p.alias("ffnblk", [128, 4 * 4096], BF16, 131072)
    p.alias("cs_fm", [128, 2 * T], BF16, 147456)
    p.alias("c_pos", [1, T], F32, 155648)
    p.alias("k_sb", [128, 4096], BF16, 0)
    p.alias("v_sb", [128, 4096], BF16, 8192)
    p.alias("q_sb", [128, 2 * 2048], BF16, 16384)
    p.alias("e_sb", [128, 4 * 512], BF16, 24576)
    p.alias("pr_sb", [128, 4 * 512], BF16, 28672)
    p.alias("a_sb", [128, 2 * 5 * 512], BF16, 32768)
    p.alias("co_sb", [128, 2 * 512], BF16, 43008)
    p.alias("rb_sb", [128, 2 * 512], F32, 45056)
    p.alias("kdiag", [128, 4 * 512], BF16, 49152)
    p.alias("vdiag", [128, 4 * 512], BF16, 53248)
    p.alias("h_tt", [128, NIC * 512], BF16, 0)
    p.alias("wd_slab", [128, 4 * 1024], BF16, 90112)
    p.PS = [p.ctx.enter_context(p.nc.psum_tensor(f"P{i}", [128, 512], F32))
            for i in range(8)]




def _set_parity(p: P, e, tag):
    r = e.alloc_register(f"par_{tag}")
    ro = e.alloc_register(f"opar_{tag}")
    for arm in e.switch_core_id(128):
        e.reg_mov(r, int(arm) & 1)
    e.reg_mov(ro, 1)
    e.reg_alu(ro, ro, r, OP.subtract)
    p.par_snap[tag] = e.snap(r, min_val=0, max_val=1)
    if not hasattr(p, "opar_snap"):
        p.opar_snap = {}
    p.opar_snap[tag] = e.snap(ro, min_val=0, max_val=1)


def qbarrier(p: P, e, sems=("te", "ve", "se"), own=None):
    if own is not None:
        p.qdrain(e, own)
    if p.qn["sync"]:
        e.wait_ge(p.sem["sqp"], p.qn["sync"])
    if p.qn["gp"]:
        e.wait_ge(p.sem["gqp"], p.qn["gp"])
    for s_ in sems:
        if p.cnt[s_]:
            e.wait_ge(p.sem[s_], p.cnt[s_])


def stage_init(p: P):
    nc, sb, dr = p.nc, p.sb, p.dram
    with nc.Block() as block:
        @block.sync
        def _(e):
            pairs = [
                (sb["c_rel128"][:, :], dr["rel128"][:, :]),
                (bass.AP(sb["c_relmask"], 0, [[2048, 128], [512, 4], [1, 512]]),
                 bass.AP(dr["relmask"], 0, [[512, 128], [128 * 512, 4], [1, 512]])),
                (bass.AP(sb["c_cmask"], 0, [[2048, 128], [512, 4], [1, 512]]),
                 bass.AP(dr["cmask"], 0, [[512, 128], [128 * 512, 4], [1, 512]])),
                (sb["c_smat"][:, :], dr["smat"][:, :]),
                (sb["c_ident"][:, :], dr["ident"][:, :]),
                (sb["c_offa"][:, :], dr["offv_a"][:, :]),
                (sb["c_offb"][:, :], dr["offv_b"][:, :]),
                (sb["c_nonce"][:, :], dr["nonce"][:, :]),
            ]
            for o, i in pairs:
                p.dma(e, "sync", o, i)
            p.qdrain(e, "sync")
            _set_parity(p, e, "sync")

        @block.gpsimd
        def _(e):
            e.memset(sb["c_ones_bf"][:, :], 1.0)
            e.memset(sb["c_ones_f"][:, :], 1.0)
            e.memset(sb["r_poll"][:, :], 0).then_inc(p.sem["sm"], 1)
            p.bump("sm", 1)
            _set_parity(p, e, "gp")
            e.wait_ge(p.sem["sm"], p.cnt["sm"])
            e.dma_start(out=dr["sh_flag"][bass.ds(p.par_snap["gp"], 1), 0:2],
                        in_=sb["r_poll"][0:1, 0:2]).then_inc(p.sem["pf"], 16)


def stage_dequant(p: P, bname, zname, sname, oname, nout, nin, blocks):
    nc, sb, dr = p.nc, p.sb, p.dram
    PS = p.PS
    noc = nout // 128
    passes = [(k * 5504, min(5504, nin - k * 5504))
              for k in range((nin + 5503) // 5504)]
    jobs = [(ic, off, w) for ic in range(noc) for (off, w) in passes]
    ntl = [w // 128 for (_, _, w) in jobs]
    cumt = [0]
    for t_ in ntl:
        cumt.append(cumt[-1] + t_)
    ve0, te0, se0 = p.cnt["ve"], p.cnt["te"], p.cnt["se"]
    ldix = {}
    stix = {}
    with nc.Block() as block:
        @block.gpsimd
        def _(e):
            qbarrier(p, e, own="gp")
            for n, (ic, off, w) in enumerate(jobs):
                if n >= 2:
                    p.qdrain(e, "gp")
                    e.wait_ge(p.sem["ve"], ve0 + (n - 1))
                sl = n % 2
                ng = w // GRP
                p.dma(e, "gp",
                      sb["dq_b32"][:, sl * 2752: sl * 2752 + w // 2],
                      dr[bname][128 * ic:128 * (ic + 1),
                                off // 2: off // 2 + w // 2])
                p.dma(e, "gp",
                      sb["c_zs"][:, sl * 192: sl * 192 + ng],
                      dr[zname][128 * ic:128 * (ic + 1),
                                off // GRP: off // GRP + ng])
                ldix[n] = p.dma(
                    e, "gp",
                    sb["c_zs"][:, sl * 192 + 96: sl * 192 + 96 + ng],
                    dr[sname][128 * ic:128 * (ic + 1),
                              off // GRP: off // GRP + ng])
            p.qdrain(e, "gp")

        @block.sync
        def _(e):
            for n, (ic, off, w) in enumerate(jobs):
                sl = n % 2
                e.wait_ge(p.sem["se"], se0 + cumt[n + 1])
                if blocks:
                    stix[n] = p.dma(
                        e, "sync",
                        dr[oname][ic],
                        sb["tstage"][:, sl * 5504: sl * 5504 + w])
                else:
                    stix[n] = p.dma(
                        e, "sync",
                        bass.AP(dr[oname], off * nout + 128 * ic,
                                [[nout, 128], [128 * nout, w // 128],
                                 [1, 128]]),
                        bass.AP(sb["tstage"], sl * 5504,
                                [[2 * 5504, 128], [128, w // 128], [1, 128]]))
                p.qdrain(e, "sync")
            p.qdrain(e, "sync")

        @block.tensor
        def _(e):
            for n, (ic, off, w) in enumerate(jobs):
                sl = n % 2
                e.wait_ge(p.sem["ve"], ve0 + (n + 1))
                for kt in range(w // 128):
                    g = cumt[n] + kt
                    if g >= 8:
                        e.wait_ge(p.sem["se"], se0 + g - 7)
                    e.transpose(
                        PS[g % 8][:, 0:128],
                        sb["dq_wf"][:, sl * 5504 + kt * 128:
                                    sl * 5504 + kt * 128 + 128],
                        sb["c_ident"][:, :]).then_inc(p.sem["te"], 1)
                    p.bump("te", 1)

        @block.scalar
        def _(e):
            for n, (ic, off, w) in enumerate(jobs):
                sl = n % 2
                if n >= 2:
                    p.wait_dma(e, "sync", stix[n - 2])
                for kt in range(w // 128):
                    g = cumt[n] + kt
                    e.wait_ge(p.sem["te"], te0 + g + 1)
                    e.copy(sb["tstage"][:, sl * 5504 + kt * 128:
                                        sl * 5504 + kt * 128 + 128],
                           PS[g % 8][:, 0:128]).then_inc(p.sem["se"], 1)
                    p.bump("se", 1)

        @block.vector
        def _(e):
            for n, (ic, off, w) in enumerate(jobs):
                sl = n % 2
                ng = w // GRP
                p.wait_dma(e, "gp", ldix[n])
                if n >= 2:
                    e.wait_ge(p.sem["te"], te0 + cumt[n - 1])
                bt = sb["dq_b32"][:, sl * 2752: sl * 2752 + w // 2]
                zb = bass.AP(sb["c_zs"], sl * 192, [[384, 128], [1, ng], [0, 32]])
                sc = bass.AP(sb["c_zs"], sl * 192 + 96,
                             [[384, 128], [1, ng], [0, 32]])
                nib = sb["dq_nib"][:, 0:w // 2]
                for half, op_, ofs in ((0, OP.logical_shift_right, 4),
                                       (1, OP.bitwise_and, 15)):
                    e.tensor_scalar(nib, bt, ofs, None, op_)
                    d1 = bass.AP(sb["dq_d1"], half, [[5504, 128], [GRP, ng], [2, 32]])
                    n3 = bass.AP(sb["dq_nib"], 0, [[2752, 128], [32, ng], [1, 32]])
                    e.tensor_tensor(d1, n3, zb, OP.subtract)
                    wq_ = bass.AP(sb["dq_wf"], sl * 5504 + half,
                                  [[11008, 128], [GRP, ng], [2, 32]])
                    ins = e.tensor_tensor(wq_, d1, sc, OP.mult)
                ins.then_inc(p.sem["ve"], 1)
                p.bump("ve", 1)


def stage_ln(p: P, srcname, tap=None):
    nc, sb, dr = p.nc, p.sb, p.dram
    PS = p.PS

    def xsl(kc, tt):
        return sb["xln"][:, kc * T + tt * 512: kc * T + tt * 512 + 512]

    ldix = {}
    for tt in range(4):
        te0, ve0, se0 = (p.cnt[k] for k in ("te", "ve", "se"))
        with nc.Block() as block:
            @block.gpsimd
            def _(e):
                if tt == 0:
                    qbarrier(p, e, own="gp")
                ldix[tt] = p.dma(
                    e, "gp",
                    bass.AP(sb["xln"], tt * 512,
                            [[NKC * T, 128], [T, NKC], [1, 512]]),
                    bass.AP(dr[srcname], tt * 512,
                            [[T, 128], [128 * T, NKC], [1, 512]]))
                p.qdrain(e, "gp")

            @block.tensor
            def _(e):
                p.wait_dma(e, "gp", ldix[tt])
                for kc in range(NKC):
                    ins = e.matmul(PS[6][0:1, :], sb["c_ones_bf"][:, :],
                                   xsl(kc, tt), start=(kc == 0),
                                   stop=(kc == NKC - 1))
                ins.then_inc(p.sem["te"], 1)
                p.bump("te", 1)
                for kc in range(NKC):
                    e.wait_ge(p.sem["ve"], ve0 + kc + 1)
                    e.matmul(PS[7][0:1, :], sb["c_ones_bf"][:, :],
                             sb["obuf"][:, (kc % 2) * 512:(kc % 2) * 512 + 512],
                             start=(kc == 0), stop=(kc == NKC - 1)
                             ).then_inc(p.sem["te"], 1)
                    p.bump("te", 1)
                e.wait_ge(p.sem["ve"], ve0 + NKC + 2)
                e.matmul(PS[6][:, :], sb["c_ones_f"][:, :],
                         sb["r_rows"][0:1, 512:1024], start=True,
                         stop=True).then_inc(p.sem["te"], 1)
                e.matmul(PS[7][:, :], sb["c_ones_f"][:, :],
                         sb["r_rows"][0:1, 0:512], start=True,
                         stop=True).then_inc(p.sem["te"], 1)
                p.bump("te", 2)

            @block.vector
            def _(e):
                p.wait_dma(e, "gp", ldix[tt])
                for kc in range(NKC):
                    if kc >= 2:
                        e.wait_ge(p.sem["te"], te0 + 1 + (kc - 1))
                    e.tensor_mul(
                        sb["obuf"][:, (kc % 2) * 512:(kc % 2) * 512 + 512],
                        xsl(kc, tt), xsl(kc, tt)).then_inc(p.sem["ve"], 1)
                    p.bump("ve", 1)
                r = sb["r_rows"]
                e.wait_ge(p.sem["te"], te0 + 1 + NKC)
                e.tensor_scalar(r[0:1, 1024:1536], PS[6][0:1, :], 1.0 / D,
                                None, OP.mult)
                e.tensor_scalar(r[0:1, 1536:2048], PS[7][0:1, :], 1.0 / D,
                                None, OP.mult)
                e.tensor_mul(r[0:1, 0:512], r[0:1, 1024:1536],
                             r[0:1, 1024:1536])
                e.tensor_sub(r[0:1, 1536:2048], r[0:1, 1536:2048],
                             r[0:1, 0:512])
                e.tensor_scalar(r[0:1, 1536:2048], r[0:1, 1536:2048],
                                LN_EPS, None, OP.add).then_inc(p.sem["ve"], 1)
                p.bump("ve", 1)
                e.wait_ge(p.sem["se"], se0 + 1)
                e.reciprocal(r[0:1, 512:1024], r[0:1, 0:512])
                e.scalar_tensor_tensor(
                    r[0:1, 0:512], r[0:1, 1024:1536], -1.0,
                    r[0:1, 512:1024], OP.mult, OP.mult
                ).then_inc(p.sem["ve"], 1)
                p.bump("ve", 1)
                e.wait_ge(p.sem["se"], se0 + 3)
                for kc in range(NKC):
                    e.tensor_mul(sb["fbuf"][:, 0:512], xsl(kc, tt),
                                 sb["fbuf"][:, 1024:1536])
                    ins = e.tensor_add(xsl(kc, tt), sb["fbuf"][:, 0:512],
                                       sb["fbuf"][:, 1536:2048])
                ins.then_inc(p.sem["ve"], 1)
                p.bump("ve", 1)

            @block.scalar
            def _(e):
                r = sb["r_rows"]
                e.wait_ge(p.sem["ve"], ve0 + NKC + 1)
                e.activation(r[0:1, 0:512], r[0:1, 1536:2048],
                             AF.Sqrt).then_inc(p.sem["se"], 1)
                p.bump("se", 1)
                e.wait_ge(p.sem["te"], te0 + NKC + 3)
                e.copy(sb["fbuf"][:, 1024:1536], PS[6][:, :]).then_inc(
                    p.sem["se"], 1)
                e.copy(sb["fbuf"][:, 1536:2048], PS[7][:, :]).then_inc(
                    p.sem["se"], 1)
                p.bump("se", 2)

    if tap is not None and tap in p.debug:
        ve_done = p.cnt["ve"]
        with nc.Block() as block:
            @block.sync
            def _(e):
                e.wait_ge(p.sem["ve"], ve_done)
                p.dma(e, "sync",
                      bass.AP(dr[tap], 0, [[T, 128], [128 * T, NKC], [1, T]]),
                      sb["xln"][:, :])
                p.qdrain(e, "sync")


def stage_load_arena(p: P, srcname):
    nc, sb, dr = p.nc, p.sb, p.dram
    with nc.Block() as block:
        @block.sync
        def _(e):
            qbarrier(p, e, own="sync")
            for q in range(4):
                p.dma(e, "sync",
                      bass.AP(sb["xln"], q * 8 * T,
                              [[NKC * T, 128], [T, 8], [1, T]]),
                      bass.AP(dr[srcname], q * 8 * 128 * T,
                              [[T, 128], [128 * T, 8], [1, T]]))
            p.qdrain(e, "sync")
    p.arena_ready = p.qn["sync"]


def stage_proj(p: P, wname, ngroups, mode, dst, xold=None):
    nc, sb, dr = p.nc, p.sb, p.dram
    PS = p.PS
    te0, ve0, se0 = (p.cnt[k] for k in ("te", "ve", "se"))

    def asl(kc, tt):
        return sb["xln"][:, kc * T + tt * 512: kc * T + tt * 512 + 512]

    tepg = 2 if mode == "rope" else 1
    vepg = 8 if mode == "rope" else 4
    wix = {}
    xix = {}
    six = {}
    with nc.Block() as block:
        @block.gpsimd
        def _(e):
            qbarrier(p, e, own="gp")
            for ng in range(ngroups):
                sl = ng % 2
                if ng >= 2:
                    p.qdrain(e, "gp")
                    e.wait_ge(p.sem["te"], te0 + tepg * (ng - 1))
                wix[ng] = p.dma(
                    e, "gp",
                    sb["wslab"][:, sl * 4096:(sl + 1) * 4096],
                    bass.AP(dr[wname], 128 * ng,
                            [[dr[wname].shape[1], 128],
                             [128 * dr[wname].shape[1], NKC], [1, 128]]))
            p.qdrain(e, "gp")

        @block.sync
        def _(e):
            for ng in range(ngroups):
                if mode == "resid":
                    for tt in range(6):
                        if tt < 4:
                            n = ng * 4 + tt
                            if n >= 2:
                                p.qdrain(e, "sync")
                                e.wait_ge(p.sem["ve"], ve0 + 4 * ng + tt - 1)
                            xix[n] = p.dma(
                                e, "sync",
                                sb["fbuf"][:, (n % 2) * 512:(n % 2) * 512 + 512],
                                dr[xold][128 * ng:128 * (ng + 1),
                                         tt * 512:(tt + 1) * 512])
                        if tt >= 2:
                            ns = ng * 4 + tt - 2
                            p.qdrain(e, "sync")
                            e.wait_ge(p.sem["ve"], ve0 + ns + 1 - 4 * ng + 4 * ng)
                            six[ns] = p.dma(
                                e, "sync",
                                dr[dst][128 * ng:128 * (ng + 1),
                                        (tt - 2) * 512:(tt - 1) * 512],
                                sb["fbuf"][:, 1024 + (ns % 2) * 512:
                                           1024 + (ns % 2) * 512 + 512])
                else:
                    p.qdrain(e, "sync")
                    e.wait_ge(p.sem["ve"], ve0 + vepg * ng + vepg)
                    if mode == "store":
                        osl = (ng % 2) * 2048
                        six[ng] = p.dma(
                            e, "sync",
                            dr[dst][128 * ng:128 * (ng + 1), :],
                            sb["obuf"][:, osl:osl + 2048])
                    else:
                        six[ng] = p.dma(
                            e, "sync",
                            dr[dst][128 * ng:128 * (ng + 1), :],
                            sb["obuf"][:, 2048:4096])
            p.qdrain(e, "sync")

        @block.tensor
        def _(e):
            for ng in range(ngroups):
                sl = ng % 2
                q = (ng % 2) * 4 if mode != "rope" else 0
                p.wait_dma(e, "gp", wix[ng])
                if (mode != "rope" and ng >= 2) or (mode == "rope" and ng >= 1):
                    back = 1 if mode == "rope" else 2
                    e.wait_ge(p.sem["ve"], ve0 + vepg * (ng - back) + vepg)
                for kc in range(NKC):
                    for tt in range(4):
                        ins = e.matmul(PS[q + tt][:, :],
                                       sb["wslab"][:, sl * 4096 + kc * 128:
                                                   sl * 4096 + kc * 128 + 128],
                                       asl(kc, tt), start=(kc == 0),
                                       stop=(kc == NKC - 1))
                ins.then_inc(p.sem["te"], 1)
                p.bump("te", 1)
                if mode == "rope":
                    e.wait_ge(p.sem["ve"], ve0 + vepg * ng + 4)
                    for tt in range(4):
                        ins = e.matmul(PS[4 + tt][:, :], sb["c_smat"][:, :],
                                       sb["obuf"][:, tt * 512:(tt + 1) * 512],
                                       start=True, stop=True)
                    ins.then_inc(p.sem["te"], 1)
                    p.bump("te", 1)

        @block.vector
        def _(e):
            for ng in range(ngroups):
                q = (ng % 2) * 4 if mode != "rope" else 0
                if mode == "store":
                    e.wait_ge(p.sem["te"], te0 + tepg * ng + 1)
                    if ng >= 2:
                        p.wait_dma(e, "sync", six[ng - 2])
                    osl = (ng % 2) * 2048
                    for tt in range(4):
                        e.tensor_copy(
                            sb["obuf"][:, osl + tt * 512: osl + tt * 512 + 512],
                            PS[q + tt][:, :]).then_inc(p.sem["ve"], 1)
                        p.bump("ve", 1)
                elif mode == "resid":
                    e.wait_ge(p.sem["te"], te0 + tepg * ng + 1)
                    for tt in range(4):
                        n = ng * 4 + tt
                        p.wait_dma(e, "sync", xix[n])
                        if n >= 2:
                            p.wait_dma(e, "sync", six[n - 2])
                        fs = (n % 2) * 512
                        e.tensor_tensor(
                            sb["fbuf"][:, 1024 + fs:1024 + fs + 512],
                            PS[q + tt][:, :], sb["fbuf"][:, fs:fs + 512],
                            OP.add).then_inc(p.sem["ve"], 1)
                        p.bump("ve", 1)
                else:
                    e.wait_ge(p.sem["te"], te0 + tepg * ng + 1)
                    if ng >= 1:
                        p.wait_dma(e, "sync", six[ng - 1])
                    for tt in range(4):
                        e.tensor_copy(
                            sb["obuf"][:, tt * 512: tt * 512 + 512],
                            PS[tt][:, :]).then_inc(p.sem["ve"], 1)
                        p.bump("ve", 1)
                    e.wait_ge(p.sem["te"], te0 + tepg * ng + 2)
                    for tt in range(4):
                        csl = sb["cs_fm"][:, tt * 512: tt * 512 + 512]
                        ssl = sb["cs_fm"][:, T + tt * 512: T + tt * 512 + 512]
                        e.tensor_mul(sb["fbuf"][:, 0:512],
                                     sb["obuf"][:, tt * 512: tt * 512 + 512],
                                     csl)
                        e.tensor_mul(sb["fbuf"][:, 512:1024],
                                     PS[4 + tt][:, :], ssl)
                        e.tensor_add(
                            sb["obuf"][:, 2048 + tt * 512: 2048 + tt * 512 + 512],
                            sb["fbuf"][:, 0:512],
                            sb["fbuf"][:, 512:1024]).then_inc(p.sem["ve"], 1)
                        p.bump("ve", 1)


def stage_kv_a(p: P):
    nc, sb, dr = p.nc, p.sb, p.dram
    PS = p.PS
    te0, ve0 = p.cnt["te"], p.cnt["ve"]
    wix = {}
    six = {}
    with nc.Block() as block:
        @block.gpsimd
        def _(e):
            qbarrier(p, e, own="gp")
            wix[0] = p.dma(e, "gp", sb["wkv_sb"][:, 0:4096],
                           bass.AP(dr["wkv_a"], 0,
                                   [[256, 128], [128 * 256, NKC], [1, 128]]))
            wix[1] = p.dma(e, "gp", sb["wkv_sb"][:, 4096:8192],
                           bass.AP(dr["wkv_a"], 128,
                                   [[256, 128], [128 * 256, NKC], [1, 128]]))
            p.qdrain(e, "gp")

        @block.sync
        def _(e):
            p.qdrain(e, "sync")
            e.wait_ge(p.sem["ve"], ve0 + 4)
            six[0] = p.dma(e, "sync", dr["ka_loc"][:, :],
                           sb["obuf"][:, 0:2048])
            for tcg in range(2):
                p.qdrain(e, "sync")
                e.wait_ge(p.sem["ve"], ve0 + 4 + 8 * (tcg + 1))
                six[1 + tcg] = p.dma(
                    e, "sync",
                    bass.AP(dr["va_loc"], tcg * 1024 * 128,
                            [[128, 128], [128 * 128, 8], [1, 128]]),
                    bass.AP(sb["kvstage"], 0,
                            [[1024, 128], [128, 8], [1, 128]]))
            p.qdrain(e, "sync")

        @block.tensor
        def _(e):
            p.wait_dma(e, "gp", wix[1])
            for kc in range(NKC):
                for tt in range(4):
                    ins = e.matmul(PS[tt][:, :],
                                   sb["wkv_sb"][:, kc * 128: kc * 128 + 128],
                                   sb["xln"][:, kc * T + tt * 512:
                                             kc * T + tt * 512 + 512],
                                   start=(kc == 0), stop=(kc == NKC - 1))
            ins.then_inc(p.sem["te"], 1)
            p.bump("te", 1)
            for tcg in range(2):
                e.wait_ge(p.sem["ve"], ve0 + 4 + 8 * tcg)
                for kc in range(NKC):
                    for t8 in range(8):
                        tc = tcg * 8 + t8
                        ins = e.matmul(
                            PS[t8][:, 0:128],
                            sb["xln"][:, kc * T + tc * 128: kc * T + tc * 128 + 128],
                            sb["wkv_sb"][:, 4096 + kc * 128: 4096 + kc * 128 + 128],
                            start=(kc == 0), stop=(kc == NKC - 1))
                ins.then_inc(p.sem["te"], 1)
                p.bump("te", 1)

        @block.vector
        def _(e):
            e.wait_ge(p.sem["te"], te0 + 1)
            for tt in range(4):
                e.tensor_copy(sb["obuf"][:, tt * 512:(tt + 1) * 512],
                              PS[tt][:, :]).then_inc(p.sem["ve"], 1)
                p.bump("ve", 1)
            for tcg in range(2):
                e.wait_ge(p.sem["te"], te0 + 2 + tcg)
                if tcg == 1:
                    p.wait_dma(e, "sync", six[1])
                for t8 in range(8):
                    e.tensor_copy(
                        sb["kvstage"][:, t8 * 128: t8 * 128 + 128],
                        PS[t8][:, 0:128]).then_inc(p.sem["ve"], 1)
                    p.bump("ve", 1)


def stage_vb(p: P):
    nc, sb, dr = p.nc, p.sb, p.dram
    PS = p.PS
    te0, ve0 = p.cnt["te"], p.cnt["ve"]
    wix = {}
    six = {}
    with nc.Block() as block:
        @block.gpsimd
        def _(e):
            qbarrier(p, e, own="gp")
            for tcg in range(4):
                for kc in range(NKC):
                    n = tcg * NKC + kc
                    if n >= 2:
                        p.qdrain(e, "gp")
                        e.wait_ge(p.sem["te"], te0 + (n - 1))
                    sl = n % 2
                    wix[n] = p.dma(
                        e, "gp",
                        sb["wslab"][:, sl * 4096: sl * 4096 + 1024],
                        dr["wv_b"][128 * kc:128 * (kc + 1), :])
            p.qdrain(e, "gp")

        @block.sync
        def _(e):
            for tcg in range(4):
                p.qdrain(e, "sync")
                e.wait_ge(p.sem["ve"], ve0 + 8 * (tcg + 1))
                six[tcg] = p.dma(
                    e, "sync",
                    bass.AP(dr["vb_loc"], tcg * 512 * 1024,
                            [[1024, 128], [128 * 1024, 4], [1, 1024]]),
                    bass.AP(sb["obuf"], 0,
                            [[4096, 128], [1024, 4], [1, 1024]]))
            p.qdrain(e, "sync")

        @block.tensor
        def _(e):
            for tcg in range(4):
                if tcg >= 1:
                    e.wait_ge(p.sem["ve"], ve0 + 8 * tcg)
                for kc in range(NKC):
                    n = tcg * NKC + kc
                    sl = n % 2
                    p.wait_dma(e, "gp", wix[n])
                    for t4 in range(4):
                        tc = tcg * 4 + t4
                        for hf in range(2):
                            ins = e.matmul(
                                PS[t4 * 2 + hf][:, :],
                                sb["xln"][:, kc * T + tc * 128:
                                          kc * T + tc * 128 + 128],
                                sb["wslab"][:, sl * 4096 + hf * 512:
                                            sl * 4096 + hf * 512 + 512],
                                start=(kc == 0), stop=(kc == NKC - 1))
                    ins.then_inc(p.sem["te"], 1)
                    p.bump("te", 1)

        @block.vector
        def _(e):
            for tcg in range(4):
                e.wait_ge(p.sem["te"], te0 + NKC * (tcg + 1))
                if tcg >= 1:
                    p.wait_dma(e, "sync", six[tcg - 1])
                for t4 in range(4):
                    for hf in range(2):
                        e.tensor_copy(
                            sb["obuf"][:, t4 * 1024 + hf * 512:
                                       t4 * 1024 + hf * 512 + 512],
                            PS[t4 * 2 + hf][:, :]).then_inc(p.sem["ve"], 1)
                        p.bump("ve", 1)


def stage_flagpoll(p: P, phase, copies):
    nc, sb, dr = p.nc, p.sb, p.dram
    with nc.Block() as block:
        @block.sync
        def _(e):
            qbarrier(p, e, sems=(), own="sync")
            for shname, locname in copies:
                p.dma(e, "sync",
                      dr[shname][bass.ds(p.par_snap["sync"], 1)],
                      dr[locname].ap())
            p.qdrain(e, "sync")

        @block.gpsimd
        def _(e):
            e.wait_ge(p.sem["sqp"], p.qn["sync"])
            e.dma_start(
                out=dr["sh_flag"][bass.ds(p.par_snap["gp"], 1),
                                  phase:phase + 1],
                in_=sb["c_nonce"][0:1, phase:phase + 1],
            ).then_inc(p.sem["pf"], 16)
            with (e.register(f"pr{phase}") as pr, e.register(f"pe{phase}") as pe,
                  e.register(f"nd{phase}") as nd):
                e.reg_load(pe, sb["c_nonce"][0:1, phase:phase + 1])
                e.memset(sb["r_poll"][0:1, 0:1], 0)
                e.reg_mov(nd, 1)
                with e.While(nd):
                    e.dma_start(
                        out=sb["r_poll"][0:1, 0:1],
                        in_=dr["sh_flag"][bass.ds(p.opar_snap["gp"], 1),
                                          phase:phase + 1]
                    ).then_inc(p.sem["pf"], 16)
                    e.nop(cycle_cnt=30000)
                    e.reg_load(pr, sb["r_poll"][0:1, 0:1])
                    e.reg_alu(nd, pr, pe, OP.not_equal)
            e.sem_inc(p.sem["pd"], 1)
            p.poll_done = getattr(p, "poll_done", 0) + 1


def stage_cossin(p: P):
    nc, sb, dr = p.nc, p.sb, p.dram
    PS = p.PS
    TWO_PI = 2.0 * math.pi
    ve0, se0 = p.cnt["ve"], p.cnt["se"]
    ldp = {}
    with nc.Block() as block:
        @block.sync
        def _(e):
            qbarrier(p, e, own="sync")
            ldp[0] = p.dma(e, "sync", sb["c_pos"][:, :], dr["pos"][:, :])
            p.qdrain(e, "sync")

        @block.gpsimd
        def _(e):
            e.iota(sb["r_rows"][0:1, 0:64], [[1, 64]], base=0,
                   channel_multiplier=0,
                   allow_small_or_imprecise_dtypes=True).then_inc(
                       p.sem["sm"], 1)
            p.bump("sm", 1)

        @block.scalar
        def _(e):
            e.wait_ge(p.sem["sm"], p.cnt["sm"])
            e.activation(sb["r_rows"][0:1, 64:128], sb["r_rows"][0:1, 0:64],
                         AF.Exp, scale=-math.log(500000.0) / 64.0
                         ).then_inc(p.sem["se"], 1)
            p.bump("se", 1)

        @block.vector
        def _(e):
            e.wait_ge(p.sem["se"], se0 + 1)
            r = sb["r_rows"]
            f0 = r[0:1, 64:128]
            e.tensor_scalar(r[0:1, 128:192], f0, TWO_PI / 8192.0, None,
                            OP.is_lt)
            e.tensor_scalar(r[0:1, 192:256], f0, TWO_PI / 2048.0, None,
                            OP.is_gt)
            e.tensor_scalar(r[0:1, 256:320], f0, 8192.0 / (TWO_PI * 3.0),
                            -1.0 / 3.0, OP.mult, OP.add)
            e.tensor_scalar(r[0:1, 256:320], r[0:1, 256:320], 31.0 / 32.0,
                            1.0 / 32.0, OP.mult, OP.add)
            e.tensor_mul(r[0:1, 256:320], r[0:1, 256:320], f0)
            e.tensor_scalar(r[0:1, 320:384], r[0:1, 128:192], -1.0, 1.0,
                            OP.mult, OP.add)
            e.tensor_scalar(r[0:1, 384:448], r[0:1, 192:256], -1.0, 1.0,
                            OP.mult, OP.add)
            e.tensor_mul(r[0:1, 320:384], r[0:1, 320:384], r[0:1, 384:448])
            e.tensor_scalar(r[0:1, 384:448], f0, 1.0 / 32.0, None, OP.mult)
            e.tensor_mul(r[0:1, 384:448], r[0:1, 384:448], r[0:1, 128:192])
            e.tensor_mul(r[0:1, 448:512], f0, r[0:1, 192:256])
            e.tensor_add(r[0:1, 384:448], r[0:1, 384:448], r[0:1, 448:512])
            e.tensor_mul(r[0:1, 256:320], r[0:1, 256:320], r[0:1, 320:384])
            e.tensor_add(r[0:1, 384:448], r[0:1, 384:448], r[0:1, 256:320])
            e.tensor_copy(sb["c_invf"][0:1, 0:64], r[0:1, 384:448])
            e.tensor_copy(sb["c_invf"][0:1, 64:128], r[0:1, 384:448]
                          ).then_inc(p.sem["ve"], 1)
            p.bump("ve", 1)

    for tt in range(4):
        for fn in range(2):
            te0b, ve0b, se0b = p.cnt["te"], p.cnt["ve"], p.cnt["se"]
            with nc.Block() as block:
                @block.tensor
                def _(e):
                    e.wait_ge(p.sem["ve"], ve0 + 1)
                    p.wait_dma(e, "sync", ldp[0])
                    if fn == 1 or tt > 0:
                        e.wait_ge(p.sem["se"], se0b)
                    e.matmul(PS[6][:, :], sb["c_invf"][0:1, :],
                             sb["c_pos"][0:1, tt * 512:(tt + 1) * 512],
                             start=True, stop=True).then_inc(p.sem["te"], 1)
                    p.bump("te", 1)

                @block.vector
                def _(e):
                    e.wait_ge(p.sem["te"], te0b + 1)
                    f = sb["fbuf"]
                    e.tensor_scalar(f[:, 0:512], PS[6][:, :], 1.0 / TWO_PI,
                                    0.25 if fn == 0 else 0.0, OP.mult, OP.add)
                    e.tensor_copy(sb["i512"][:, :], f[:, 0:512])
                    e.tensor_copy(f[:, 512:1024], sb["i512"][:, :])
                    e.tensor_sub(f[:, 0:512], f[:, 0:512], f[:, 512:1024])
                    e.tensor_scalar(f[:, 512:1024], f[:, 0:512], 0.5, None,
                                    OP.is_gt)
                    e.tensor_sub(f[:, 0:512], f[:, 0:512], f[:, 512:1024])
                    e.tensor_scalar(f[:, 512:1024], f[:, 0:512], -0.5, None,
                                    OP.is_lt)
                    e.tensor_add(f[:, 0:512], f[:, 0:512], f[:, 512:1024]
                                 ).then_inc(p.sem["ve"], 1)
                    p.bump("ve", 1)

                @block.scalar
                def _(e):
                    e.wait_ge(p.sem["ve"], ve0b + 1)
                    dst = sb["cs_fm"][:, fn * T + tt * 512:
                                      fn * T + tt * 512 + 512]
                    e.activation(dst, sb["fbuf"][:, 0:512], AF.Sin,
                                 scale=TWO_PI).then_inc(p.sem["se"], 1)
                    p.bump("se", 1)


CSC = [0, 12, 28, 56, 88]
TPH = 88


def stage_attn(p: P, which):
    nc, sb, dr = p.nc, p.sb, p.dram
    PS = p.PS
    a = which == "a"
    KVG, HPG = (1, 32) if a else (8, 4)
    qsrc = "qa" if a else "qb"
    cdst = "ctxa" if a else "ctxb"
    V = 3 if a else 2
    SEH = 5 if a else 4
    t10, t30, sx0, sm0, te0, ve0, se0 = (
        p.cnt[k] for k in ("t1", "t3", "sx", "sm", "te", "ve", "se"))

    def kof(h, s, idx):
        return TPH * h + CSC[s] + idx

    def kend(q):
        return TPH * (q // 4) + CSC[(q % 4) + 1]

    def rbev(q):
        return se0 + SEH * (q // 4) + (q % 4) + (2 if a else 1)

    def kvsrc_k(kvh, g):
        slot, loc = pos_block_map(g)
        if a:
            return dr["sh_ka"][slot][:, loc:loc + 512]
        return dr["sh_kb"][slot][kvh * 128:(kvh + 1) * 128, loc:loc + 512]

    def kvsrc_v(kvh, g):
        slot, loc = pos_block_map(g)
        if a:
            return bass.AP(dr["sh_va"], T * 128 * slot + loc * 128,
                           [[128, 128], [128 * 128, 4], [1, 128]])
        return bass.AP(dr["sh_vb"], T * 1024 * slot + loc * 1024 + kvh * 128,
                       [[1024, 128], [128 * 1024, 4], [1, 128]])

    kvix = {}
    qix = {}
    six = {}
    with nc.Block() as block:
        @block.sync
        def _(e):
            e.wait_ge(p.sem["pd"], p.poll_done)

            def store_q(h_, s_):
                q = h_ * 4 + s_
                p.qdrain(e, "sync")
                e.wait_ge(p.sem["ve"], ve0 + V * q + V)
                six[q] = p.dma(
                    e, "sync",
                    dr[cdst][128 * h_:128 * (h_ + 1),
                             512 * s_:512 * (s_ + 1)],
                    sb["co_sb"][:, (q % 2) * 512:(q % 2) * 512 + 512])

            for kvg in range(KVG):
                if kvg >= 1:
                    for s_ in range(4):
                        store_q(kvg * HPG - 1, s_)
                    p.qdrain(e, "sync")
                    e.wait_ge(p.sem["t3"], t30 + TPH * HPG * kvg)
                last = None
                for g in range(8):
                    p.dma(e, "sync", sb["k_sb"][:, 512 * g:512 * (g + 1)],
                          kvsrc_k(kvg, g))
                    last = p.dma(
                        e, "sync",
                        bass.AP(sb["v_sb"], 512 * g,
                                [[4096, 128], [128, 4], [1, 128]]),
                        kvsrc_v(kvg, g))
                for s_ in range(4):
                    if a:
                        ksrc = dr["ka_loc"][:, 512 * s_:512 * (s_ + 1)]
                    else:
                        ksrc = dr["kb_loc"][kvg * 128:(kvg + 1) * 128,
                                            512 * s_:512 * (s_ + 1)]
                    p.dma(e, "sync", sb["kdiag"][:, s_ * 512:(s_ + 1) * 512],
                          ksrc)
                    for kt in range(4):
                        row0 = 512 * s_ + 128 * kt
                        if a:
                            vsrc = dr["va_loc"][row0:row0 + 128, :]
                        else:
                            vsrc = dr["vb_loc"][row0:row0 + 128,
                                                kvg * 128:(kvg + 1) * 128]
                        last = p.dma(
                            e, "sync",
                            sb["vdiag"][:, s_ * 512 + kt * 128:
                                        s_ * 512 + kt * 128 + 128],
                            vsrc)
                kvix[kvg] = last
                for hh in range(HPG):
                    h = kvg * HPG + hh
                    if h >= 2:
                        p.qdrain(e, "sync")
                        e.wait_ge(p.sem["t1"], t10 + TPH * (h - 1))
                    qix[h] = p.dma(
                        e, "sync",
                        sb["q_sb"][:, (h % 2) * 2048:(h % 2) * 2048 + 2048],
                        dr[qsrc][128 * h:128 * (h + 1), :])
                    if hh >= 1:
                        for s_ in range(4):
                            store_q(h - 1, s_)
            for s_ in range(4):
                store_q(KVG * HPG - 1, s_)
            p.qdrain(e, "sync")

        @block.tensor
        def _(e):
            LA = 4
            for kvg in range(KVG):
                for hh in range(HPG):
                    h = kvg * HPG + hh
                    p.wait_dma(e, "sync", qix[h])
                    p.wait_dma(e, "sync", kvix[kvg])
                    tiles = [(s_, i_) for s_ in range(4)
                             for i_ in range(CS[s_])]
                    for j in range(len(tiles) + LA):
                        if j < len(tiles):
                            s, idx = tiles[j]
                            k = kof(h, s, idx)
                            nst = CS[s] - 4
                            diag = idx >= nst
                            m = idx - nst
                            if k >= 4:
                                e.wait_ge(p.sem["sx"], sx0 + k - 3)
                            if diag:
                                lw = sb["kdiag"][:, s * 512 + m * 128:
                                                 s * 512 + m * 128 + 128]
                            else:
                                lw = sb["k_sb"][:, 128 * idx:128 * idx + 128]
                            e.matmul(PS[2 + (k % 4)][:, :], lw,
                                     sb["q_sb"][:, (h % 2) * 2048 + 512 * s:
                                                (h % 2) * 2048 + 512 * s + 512],
                                     start=True, stop=True
                                     ).then_inc(p.sem["t1"], 1)
                            p.bump("t1", 1)
                        if j >= LA:
                            s, idx = tiles[j - LA]
                            q = h * 4 + s
                            k = kof(h, s, idx)
                            nst = CS[s] - 4
                            diag = idx >= nst
                            m = idx - nst
                            if idx == 0:
                                e.wait_ge(p.sem["ve"], ve0 + V * q)
                            if a or diag:
                                e.wait_ge(p.sem["sm"],
                                          sm0 + (k + 1 if a else q * 4 + m + 1))
                            else:
                                e.wait_ge(p.sem["sx"], sx0 + k + 1)
                            if diag:
                                lv = sb["vdiag"][:, s * 512 + m * 128:
                                                 s * 512 + m * 128 + 128]
                            else:
                                lv = sb["v_sb"][:, 128 * idx:128 * idx + 128]
                            prs = sb["pr_sb"][:, (k % 4) * 512:(k % 4) * 512 + 512]
                            e.matmul(PS[0][:, :], lv, prs,
                                     start=(idx == 0), stop=(idx == CS[s] - 1))
                            e.matmul(PS[1][0:1, :], sb["c_ones_bf"][:, :], prs,
                                     start=(idx == 0), stop=(idx == CS[s] - 1)
                                     ).then_inc(p.sem["t3"], 1)
                            p.bump("t3", 1)
                            if idx == CS[s] - 1:
                                e.wait_ge(p.sem["ve"],
                                          ve0 + V * q + (3 if a else 2) - 1)
                                if q >= 1:
                                    e.wait_ge(p.sem["se"], rbev(q - 1))
                                e.matmul(PS[6][:, :], sb["c_ones_f"][:, :],
                                         sb["r_rows"][0:1, (q % 2) * 512:
                                                      (q % 2) * 512 + 512],
                                         start=True, stop=True
                                         ).then_inc(p.sem["te"], 1)
                                p.bump("te", 1)

        @block.scalar
        def _(e):
            for kvg in range(KVG):
                for hh in range(HPG):
                    h = kvg * HPG + hh
                    asl = (h % 2) * 2560
                    if a:
                        if h >= 2:
                            e.wait_ge(p.sem["sm"], sm0 + TPH * (h - 1))
                        sl_ = SLOPES[h]
                        e.activation(sb["a_sb"][:, asl:asl + 512],
                                     sb["c_rel128"][:, :], AF.Exp, scale=sl_)
                        for m in range(4):
                            ins = e.activation(
                                sb["a_sb"][:, asl + (1 + m) * 512:
                                           asl + (2 + m) * 512],
                                sb["c_relmask"][:, m * 512:(m + 1) * 512],
                                AF.Exp, scale=sl_)
                        ins.then_inc(p.sem["se"], 1)
                        p.bump("se", 1)
                    for s in range(4):
                        q = h * 4 + s
                        nst = CS[s] - 4
                        for idx in range(CS[s]):
                            k = kof(h, s, idx)
                            diag = idx >= nst
                            e.wait_ge(p.sem["t1"], t10 + k + 1)
                            if a:
                                if k >= 4:
                                    e.wait_ge(p.sem["sm"], sm0 + k - 3)
                                if idx == 0:
                                    e.wait_ge(p.sem["ve"], ve0 + V * q + 1)
                                dst = sb["e_sb"][:, (k % 4) * 512:
                                                 (k % 4) * 512 + 512]
                                bias = (None if diag else
                                        sb["c_bias"][:, (q % 2) * 32 + idx:
                                                     (q % 2) * 32 + idx + 1])
                            else:
                                if k >= 4:
                                    e.wait_ge(p.sem["t3"], t30 + k - 3)
                                dst = sb["pr_sb"][:, (k % 4) * 512:
                                                  (k % 4) * 512 + 512]
                                bias = (None if diag else
                                        sb["c_offb"][:, s * 32 + idx:
                                                     s * 32 + idx + 1])
                            if bias is None:
                                ins = e.activation(dst, PS[2 + (k % 4)][:, :],
                                                   AF.Exp, scale=ISCALE)
                            else:
                                ins = e.activation(dst, PS[2 + (k % 4)][:, :],
                                                   AF.Exp, bias=bias,
                                                   scale=ISCALE)
                            ins.then_inc(p.sem["sx"], 1)
                            p.bump("sx", 1)
                        e.wait_ge(p.sem["te"], te0 + q + 1)
                        if q >= 2:
                            e.wait_ge(p.sem["ve"], ve0 + V * (q - 2) + V)
                        e.copy(sb["rb_sb"][:, (q % 2) * 512:(q % 2) * 512 + 512],
                               PS[6][:, :]).then_inc(p.sem["se"], 1)
                        p.bump("se", 1)

        @block.vector
        def _(e):
            for kvg in range(KVG):
                for hh in range(HPG):
                    h = kvg * HPG + hh
                    asl = (h % 2) * 2560
                    for s in range(4):
                        q = h * 4 + s
                        nst = CS[s] - 4
                        if a:
                            if q >= 2:
                                e.wait_ge(p.sem["sx"], sx0 + kend(q - 2))
                            e.tensor_scalar(
                                sb["c_bias"][:, (q % 2) * 32:(q % 2) * 32 + 32],
                                sb["c_offa"][:, s * 32:s * 32 + 32],
                                SLOPES[h], None, OP.mult
                            ).then_inc(p.sem["ve"], 1)
                            p.bump("ve", 1)
                        for idx in range(CS[s]):
                            k = kof(h, s, idx)
                            diag = idx >= nst
                            m = idx - nst
                            if a:
                                e.wait_ge(p.sem["sx"], sx0 + k + 1)
                                if k >= 4:
                                    e.wait_ge(p.sem["t3"], t30 + k - 3)
                                if idx == 0 and s == 0:
                                    e.wait_ge(p.sem["se"], se0 + SEH * h + 1)
                                aof = asl + ((1 + m) * 512 if diag else 0)
                                e.tensor_mul(
                                    sb["pr_sb"][:, (k % 4) * 512:
                                                (k % 4) * 512 + 512],
                                    sb["e_sb"][:, (k % 4) * 512:
                                               (k % 4) * 512 + 512],
                                    sb["a_sb"][:, aof:aof + 512]
                                ).then_inc(p.sem["sm"], 1)
                                p.bump("sm", 1)
                            elif diag:
                                e.wait_ge(p.sem["sx"], sx0 + k + 1)
                                prs = sb["pr_sb"][:, (k % 4) * 512:
                                                  (k % 4) * 512 + 512]
                                e.tensor_mul(prs, prs,
                                             sb["c_cmask"][:, m * 512:
                                                           (m + 1) * 512]
                                             ).then_inc(p.sem["sm"], 1)
                                p.bump("sm", 1)
                        e.wait_ge(p.sem["t3"], t30 + kend(q))
                        e.reciprocal(sb["r_rows"][0:1, (q % 2) * 512:
                                                  (q % 2) * 512 + 512],
                                     PS[1][0:1, :]).then_inc(p.sem["ve"], 1)
                        p.bump("ve", 1)
                        e.wait_ge(p.sem["se"], rbev(q))
                        if q >= 2 and (q - 2) in six:
                            p.wait_dma(e, "sync", six[q - 2])
                        e.tensor_tensor(
                            sb["co_sb"][:, (q % 2) * 512:(q % 2) * 512 + 512],
                            PS[0][:, :],
                            sb["rb_sb"][:, (q % 2) * 512:(q % 2) * 512 + 512],
                            OP.mult).then_inc(p.sem["ve"], 1)
                        p.bump("ve", 1)


def stage_ffn_gu(p: P):
    nc, sb, dr = p.nc, p.sb, p.dram
    PS = p.PS
    te0, ve0, se0 = (p.cnt[k] for k in ("te", "ve", "se"))

    def asl(kc, tt):
        return sb["xln"][:, kc * T + tt * 512: kc * T + tt * 512 + 512]

    wixg = {}
    wixu = {}
    six = {}
    with nc.Block() as block:
        @block.sync
        def _(e):
            qbarrier(p, e, own="sync")
            for ic in range(NIC):
                sl = ic % 2
                if ic >= 2:
                    p.qdrain(e, "sync")
                    e.wait_ge(p.sem["te"], te0 + 2 * (ic - 1))
                wixg[ic] = p.dma(e, "sync",
                                 sb["ffnblk"][:, sl * 4096:(sl + 1) * 4096],
                                 dr["wgS"][ic])
                wixu[ic] = p.dma(e, "sync",
                                 sb["ffnblk"][:, 8192 + sl * 4096:
                                              8192 + (sl + 1) * 4096],
                                 dr["wuS"][ic])
                if ic >= 1:
                    for tt in range(4):
                        n = (ic - 1) * 4 + tt
                        p.qdrain(e, "sync")
                        e.wait_ge(p.sem["ve"], ve0 + n + 1)
                        six[n] = p.dma(
                            e, "sync",
                            dr["hbuf"][128 * (ic - 1):128 * ic,
                                       tt * 512:(tt + 1) * 512],
                            sb["hstage"][:, (n % 2) * 512:(n % 2) * 512 + 512])
            for tt in range(4):
                n = (NIC - 1) * 4 + tt
                p.qdrain(e, "sync")
                e.wait_ge(p.sem["ve"], ve0 + n + 1)
                six[n] = p.dma(
                    e, "sync",
                    dr["hbuf"][128 * (NIC - 1):128 * NIC,
                               tt * 512:(tt + 1) * 512],
                    sb["hstage"][:, (n % 2) * 512:(n % 2) * 512 + 512])
            p.qdrain(e, "sync")

        @block.tensor
        def _(e):
            for ic in range(NIC):
                sl = ic % 2
                p.wait_dma(e, "sync", wixu[ic])
                if ic >= 1:
                    e.wait_ge(p.sem["se"], se0 + 4 * ic)
                for kc in range(NKC):
                    for tt in range(4):
                        ins = e.matmul(PS[tt][:, :],
                                       sb["ffnblk"][:, sl * 4096 + kc * 128:
                                                    sl * 4096 + kc * 128 + 128],
                                       asl(kc, tt), start=(kc == 0),
                                       stop=(kc == NKC - 1))
                ins.then_inc(p.sem["te"], 1)
                p.bump("te", 1)
                if ic >= 1:
                    e.wait_ge(p.sem["ve"], ve0 + 4 * ic)
                for kc in range(NKC):
                    for tt in range(4):
                        ins = e.matmul(PS[4 + tt][:, :],
                                       sb["ffnblk"][:, 8192 + sl * 4096 + kc * 128:
                                                    8192 + sl * 4096 + kc * 128 + 128],
                                       asl(kc, tt), start=(kc == 0),
                                       stop=(kc == NKC - 1))
                ins.then_inc(p.sem["te"], 1)
                p.bump("te", 1)

        @block.scalar
        def _(e):
            for ic in range(NIC):
                e.wait_ge(p.sem["te"], te0 + 2 * ic + 1)
                for tt in range(4):
                    if ic >= 1:
                        e.wait_ge(p.sem["ve"], ve0 + 4 * (ic - 1) + tt + 1)
                    e.activation(sb["obuf"][:, tt * 512:(tt + 1) * 512],
                                 PS[tt][:, :], AF.Silu).then_inc(p.sem["se"], 1)
                    p.bump("se", 1)

        @block.vector
        def _(e):
            for ic in range(NIC):
                e.wait_ge(p.sem["te"], te0 + 2 * ic + 2)
                for tt in range(4):
                    n = ic * 4 + tt
                    e.wait_ge(p.sem["se"], se0 + 4 * ic + tt + 1)
                    if n >= 2:
                        p.wait_dma(e, "sync", six[n - 2])
                    e.tensor_tensor(
                        sb["hstage"][:, (n % 2) * 512:(n % 2) * 512 + 512],
                        sb["obuf"][:, tt * 512:(tt + 1) * 512],
                        PS[4 + tt][:, :], OP.mult).then_inc(p.sem["ve"], 1)
                    p.bump("ve", 1)


def stage_ffn_down(p: P, xold, dst):
    nc, sb, dr = p.nc, p.sb, p.dram
    PS = p.PS
    te0, ve0 = p.cnt["te"], p.cnt["ve"]
    hix = {}
    wix = {}
    wq = {}
    xix = {}
    six = {}
    for G in range(16):
        for ic in range(NIC):
            n = G * NIC + ic
            wq[n] = "sync" if ic % 2 == 0 else "gp"
    with nc.Block() as block:
        @block.sync
        def _(e):
            qbarrier(p, e, own="sync")
            for tt in range(4):
                if tt >= 1:
                    p.qdrain(e, "sync")
                    e.wait_ge(p.sem["te"], te0 + NIC * 4 * tt)
                for half in range(2):
                    hix[tt] = p.dma(
                        e, "sync",
                        bass.AP(sb["h_tt"], 43 * 512 * half,
                                [[NIC * 512, 128], [512, 43], [1, 512]]),
                        bass.AP(dr["hbuf"], 43 * 128 * T * half + tt * 512,
                                [[T, 128], [128 * T, 43], [1, 512]]))
                for ncg in range(4):
                    G = tt * 4 + ncg
                    for ic in range(NIC):
                        n = G * NIC + ic
                        if wq[n] != "sync":
                            continue
                        sl = n % 4
                        if n >= 4:
                            p.qdrain(e, "sync")
                            e.wait_ge(p.sem["te"], te0 + n - 3)
                        wix[n] = p.dma(
                            e, "sync",
                            sb["wd_slab"][:, sl * 1024:(sl + 1) * 1024],
                            dr["wd_T"][128 * ic:128 * (ic + 1),
                                       1024 * ncg:1024 * (ncg + 1)])
                    for nc2 in range(10):
                        if nc2 < 8:
                            j = G * 8 + nc2
                            if j >= 2:
                                p.qdrain(e, "sync")
                                e.wait_ge(p.sem["ve"], ve0 + j - 1)
                            xix[j] = p.dma(
                                e, "sync",
                                sb["fbuf"][:, (j % 2) * 512:(j % 2) * 512 + 512],
                                dr[xold][1024 * ncg + 128 * nc2:
                                         1024 * ncg + 128 * (nc2 + 1),
                                         tt * 512:(tt + 1) * 512])
                        if nc2 >= 2:
                            js = G * 8 + nc2 - 2
                            p.qdrain(e, "sync")
                            e.wait_ge(p.sem["ve"], ve0 + js + 1)
                            six[js] = p.dma(
                                e, "sync",
                                dr[dst][1024 * ncg + 128 * (nc2 - 2):
                                        1024 * ncg + 128 * (nc2 - 1),
                                        tt * 512:(tt + 1) * 512],
                                sb["fbuf"][:, 1024 + (js % 2) * 512:
                                           1024 + (js % 2) * 512 + 512])
            p.qdrain(e, "sync")

        @block.gpsimd
        def _(e):
            p.qdrain(e, "gp")
            if te0:
                e.wait_ge(p.sem["te"], te0)
            for tt in range(4):
                for ncg in range(4):
                    G = tt * 4 + ncg
                    for ic in range(NIC):
                        n = G * NIC + ic
                        if wq[n] != "gp":
                            continue
                        sl = n % 4
                        if n >= 4:
                            p.qdrain(e, "gp")
                            e.wait_ge(p.sem["te"], te0 + n - 3)
                        wix[n] = p.dma(
                            e, "gp",
                            sb["wd_slab"][:, sl * 1024:(sl + 1) * 1024],
                            dr["wd_T"][128 * ic:128 * (ic + 1),
                                       1024 * ncg:1024 * (ncg + 1)])
            p.qdrain(e, "gp")

        @block.tensor
        def _(e):
            for tt in range(4):
                for ncg in range(4):
                    G = tt * 4 + ncg
                    for ic in range(NIC):
                        n = G * NIC + ic
                        sl = n % 4
                        p.wait_dma(e, wq[n], wix[n])
                        if ic == 0:
                            p.wait_dma(e, "sync", hix[tt])
                        for nc2 in range(8):
                            if ic == 0 and G >= 1:
                                e.wait_ge(p.sem["ve"],
                                          ve0 + 8 * (G - 1) + nc2 + 1)
                            ins = e.matmul(
                                PS[nc2][:, :],
                                sb["wd_slab"][:, sl * 1024 + nc2 * 128:
                                              sl * 1024 + nc2 * 128 + 128],
                                sb["h_tt"][:, ic * 512:(ic + 1) * 512],
                                start=(ic == 0), stop=(ic == NIC - 1))
                        ins.then_inc(p.sem["te"], 1)
                        p.bump("te", 1)

        @block.vector
        def _(e):
            for tt in range(4):
                for ncg in range(4):
                    G = tt * 4 + ncg
                    e.wait_ge(p.sem["te"], te0 + NIC * (G + 1))
                    for nc2 in range(8):
                        j = G * 8 + nc2
                        p.wait_dma(e, "sync", xix[j])
                        if j >= 2:
                            p.wait_dma(e, "sync", six[j - 2])
                        e.tensor_tensor(
                            sb["fbuf"][:, 1024 + (j % 2) * 512:
                                       1024 + (j % 2) * 512 + 512],
                            PS[nc2][:, :],
                            sb["fbuf"][:, (j % 2) * 512:(j % 2) * 512 + 512],
                            OP.add).then_inc(p.sem["ve"], 1)
                        p.bump("ve", 1)


def stage_final(p: P):
    nc = p.nc
    with nc.Block() as block:
        @block.sync
        def _(e):
            qbarrier(p, e, own="sync")
            if p.qn["gp"]:
                e.wait_ge(p.sem["gq"], 16 * p.qn["gp"])


def build_program(debug=(), upto=99):
    p = P(debug=debug)
    declare_io(p)
    alloc_sbuf(p)
    stages = [
        lambda: stage_init(p),
        lambda: stage_dequant(p, "gate_b", "gate_z", "gate_s", "wgS", FI, D, True),
        lambda: stage_dequant(p, "up_b", "up_z", "up_s", "wuS", FI, D, True),
        lambda: stage_dequant(p, "down_b", "down_z", "down_s", "wd_T", D, FI, False),
        lambda: stage_ln(p, "xin", tap="xln_dbg"),
        lambda: stage_proj(p, "wq_a", 32, "store", "qa"),
        lambda: stage_kv_a(p),
        lambda: stage_flagpoll(p, 0, [("sh_ka", "ka_loc"), ("sh_va", "va_loc")]),
        lambda: stage_attn(p, "a"),
        lambda: stage_load_arena(p, "ctxa"),
        lambda: stage_proj(p, "wo_a", 32, "resid", "x1", xold="xin"),
        lambda: stage_ln(p, "x1"),
        lambda: stage_ffn_gu(p),
        lambda: stage_ffn_down(p, "x1", "x2"),
        lambda: stage_ln(p, "x2"),
        lambda: stage_cossin(p),
        lambda: stage_proj(p, "wq_b", 32, "rope", "qb"),
        lambda: stage_proj(p, "wk_b", 8, "rope", "kb_loc"),
        lambda: stage_vb(p),
        lambda: stage_flagpoll(p, 1, [("sh_kb", "kb_loc"), ("sh_vb", "vb_loc")]),
        lambda: stage_attn(p, "b"),
        lambda: stage_load_arena(p, "ctxb"),
        lambda: stage_proj(p, "wo_b", 32, "resid", "x3", xold="x2"),
        lambda: stage_ln(p, "x3"),
        lambda: stage_ffn_gu(p),
        lambda: stage_ffn_down(p, "x3", "xout"),
    ]
    only = globals().get("ONLY_STAGES")
    for i, fn in enumerate(stages):
        if i >= upto:
            break
        if only is not None and i not in only:
            continue
        fn()
    stage_final(p)
    return p


def host_prepare(inputs, nonce):
    import ml_dtypes
    x = np.asarray(inputs["L0_attn_a_hidden_states"])

    def bf(a):
        return np.ascontiguousarray(a).astype(ml_dtypes.bfloat16)

    wq_a = np.ascontiguousarray(np.asarray(inputs["L0_attn_a_Wq"], np.float32))
    wkv_a = np.ascontiguousarray(np.asarray(inputs["L0_attn_a_Wkv"], np.float32))
    wo_a = np.ascontiguousarray(np.asarray(inputs["L0_attn_a_Wo"], np.float32))
    wq_b = np.ascontiguousarray(np.asarray(inputs["L0_attn_b_Wq"], np.float32).T)
    wk_b = np.ascontiguousarray(np.asarray(inputs["L0_attn_b_Wk"], np.float32).T)
    wv_b = np.ascontiguousarray(np.asarray(inputs["L0_attn_b_Wv"], np.float32).T)
    wo_b = np.ascontiguousarray(np.asarray(inputs["L0_attn_b_Wo"], np.float32).T)

    def u8view(a):
        a = np.ascontiguousarray(np.asarray(a, np.int32))
        return np.ascontiguousarray(
            a.view(np.uint8).reshape(a.shape[0], a.shape[1], 4)[:, :, 0])

    gate_b = u8view(inputs["L0_ffn_gate_Wq"])
    up_b = u8view(inputs["L0_ffn_up_Wq"])
    down_b = u8view(inputs["L0_ffn_down_Wq"])

    dk = np.arange(128)[:, None].astype(np.float32)
    dq = np.arange(512)[None, :].astype(np.float32)
    rel = dk - dq
    rel128 = bf(rel - 128.0)
    relmask = bf(np.stack([np.where(rel + 128 * m <= 0, rel + 128 * m, -1e9)
                           for m in range(4)]))
    cmask = bf(np.stack([(rel + 128 * m <= 0).astype(np.float32)
                         for m in range(4)]))
    smat = np.zeros((128, 128), np.float32)
    for m_ in range(64):
        smat[m_ + 64, m_] = -1.0
        smat[m_, m_ + 64] = 1.0
    smat = bf(smat)

    common = dict(
        wq_a=wq_a, wkv_a=wkv_a, wo_a=wo_a, wq_b=wq_b, wk_b=wk_b,
        wv_b=wv_b, wo_b=wo_b, gate_b=gate_b, up_b=up_b, down_b=down_b,
        gate_s=np.asarray(inputs["L0_ffn_gate_scale"], np.float32),
        gate_z=np.asarray(inputs["L0_ffn_gate_zero"], np.float32),
        up_s=np.asarray(inputs["L0_ffn_up_scale"], np.float32),
        up_z=np.asarray(inputs["L0_ffn_up_zero"], np.float32),
        down_s=np.asarray(inputs["L0_ffn_down_scale"], np.float32),
        down_z=np.asarray(inputs["L0_ffn_down_zero"], np.float32),
        rel128=rel128, relmask=relmask, cmask=cmask, smat=smat,
        ident=np.eye(128, dtype=np.float32),
        nonce=nonce,
    )

    in_maps = []
    shard_info = []
    for c in range(8):
        b, par = c // 2, c % 2
        toks = own_tokens(par)
        q0s = own_q0s(par)
        shard_info.append((b, toks))
        xc = np.ascontiguousarray(x[b][toks, :].T.astype(np.float32))
        offv = np.full((4, 32), NEGPAD, np.float32)
        offb = np.full((4, 32), NEG, np.float32)
        for s_ in range(4):
            nst = CS[s_] - 4
            for j in range(32):
                if j < nst and 128 * j < q0s[s_]:
                    offv[s_, j] = 128 * j - q0s[s_] + 128.0
                    offb[s_, j] = 0.0
        m = dict(common)
        m.update(
            xin=xc,
            offv_a=np.broadcast_to(offv.reshape(1, 128), (128, 128)).copy(),
            offv_b=np.broadcast_to(offb.reshape(1, 128), (128, 128)).copy(),
            pos=np.asarray(toks, np.float32).reshape(1, T),
        )
        in_maps.append(m)
    return in_maps, shard_info


_PROG_CACHE = {}


def kernel(**inputs):
    nonce = (np.random.randint(1, 2 ** 30, size=(1, 2)).astype(np.int32)
             + np.asarray([[0, 2 ** 30]], np.int32))
    in_maps, shard_info = host_prepare(inputs, nonce)
    key = (tuple(sorted(DEBUG)), UPTO)
    if key not in _PROG_CACHE:
        _PROG_CACHE[key] = build_program(debug=DEBUG, upto=UPTO)
    p = _PROG_CACHE[key]
    res = run_bass_kernel_spmd(p.nc, in_maps, core_ids=list(range(8)))
    out = np.zeros((4, 4096, 4096), np.float32)
    for c in range(8):
        b, toks = shard_info[c]
        out[b][np.asarray(toks), :] = np.asarray(
            res.results[c]["xout"], np.float32).T
    if DEBUG:
        kernel._debug_results = res.results
    return out


# version 04
# speedup vs baseline: 1.3580x; 1.0413x over previous
import math
from contextlib import ExitStack

import numpy as np

import concourse.bass as bass
import concourse.mybir as mybir
from concourse.bass_utils import run_bass_kernel_spmd

F32 = mybir.dt.float32
BF16 = mybir.dt.bfloat16
I32 = mybir.dt.int32
U8 = mybir.dt.uint8
AF = mybir.ActivationFunctionType
OP = mybir.AluOpType

D = 4096
T = 2048
NKC = 32
NTC = 16
FI = 11008
NIC = 86
GRP = 64
ISCALE = 1.0 / math.sqrt(128.0)
NEG = -1.0e30
NEGPAD = -1.0e35
CS = [12, 16, 28, 32]
_AB = 2.0 ** (-(2.0 ** -(math.log2(32) - 3)))
SLOPES = [_AB ** (i + 1) for i in range(32)]
LN_EPS = 1e-5

DEBUG = ()
UPTO = 99
TRANSPOSE_DOWN = True


def own_tokens(parity):
    if parity == 0:
        return list(range(0, 1024)) + list(range(3072, 4096))
    return list(range(1024, 3072))


def own_q0s(parity):
    return [0, 512, 3072, 3584] if parity == 0 else [1024, 1536, 2048, 2560]


def pos_block_map(g):
    if g in (0, 1):
        return (0, 512 * g)
    if g in (2, 3, 4, 5):
        return (1, 512 * (g - 2))
    return (0, 1024 + 512 * (g - 6))


class P:

    def __init__(self, debug=()):
        self.debug = set(debug)
        self.nc = bass.Bass("TRN2", target_bir_lowering=False,
                            detect_race_conditions=False)
        self.ctx = ExitStack()
        self.sem = {}
        self.cnt = {}
        for name in ["sq", "sqp", "gq", "gqp", "te", "ve", "se", "t1", "t3", "sx", "sm", "pf", "pd"]:
            self.sem[name] = self.ctx.enter_context(self.nc.semaphore(name))
            self.cnt[name] = 0
        self.sb = {}
        self.dram = {}
        self.pspar = 0
        self.par_snap = {}
        self.qn = {"sync": 0, "gp": 0}
        self.qpend = {"sync": [], "gp": []}
        self.qpub = {"sync": 0, "gp": 0}
        self.qdepth = 6
        self.poll_done = 0

    def dma(self, e, which, out, in_, transpose=False):
        raw = self.sem["sq" if which == "sync" else "gq"]
        if transpose:
            e.dma_start_transpose(out, in_).then_inc(raw, 16)
        else:
            e.dma_start(out=out, in_=in_).then_inc(raw, 16)
        idx = self.qn[which]
        self.qn[which] += 1
        self.qpend[which].append(idx)
        if len(self.qpend[which]) > self.qdepth:
            self._flush(e, which)
        return idx

    def _flush(self, e, which):
        if not self.qpend[which]:
            return
        raw = self.sem["sq" if which == "sync" else "gq"]
        pub = self.sem["sqp" if which == "sync" else "gqp"]
        e.wait_ge(raw, 16 * self.qn[which])
        e.sem_inc(pub, len(self.qpend[which]))
        self.qpub[which] = self.qn[which]
        self.qpend[which].clear()

    def qdrain(self, e, which):
        self._flush(e, which)

    def wait_dma(self, e, which, idx):
        e.wait_ge(self.sem["sqp" if which == "sync" else "gqp"], idx + 1)

    def bump(self, name, amt=1):
        self.cnt[name] += amt
        return self.cnt[name]

    def din(self, name, shape, dtype):
        self.dram[name] = self.nc.dram_tensor(name, shape, dtype,
                                              kind="ExternalInput")
        return self.dram[name]

    def dout(self, name, shape, dtype):
        self.dram[name] = self.nc.dram_tensor(name, shape, dtype,
                                              kind="ExternalOutput")
        return self.dram[name]

    def dtmp(self, name, shape, dtype, shared=False):
        kind = "ExternalOutput" if name in self.debug else "Internal"
        self.dram[name] = self.nc.dram_tensor(
            name, shape, dtype, kind=kind,
            addr_space="Shared" if shared else "Local")
        return self.dram[name]

    def sbuf(self, name, shape, dtype):
        self.sb[name] = self.ctx.enter_context(
            self.nc.sbuf_tensor(name, shape, dtype))
        return self.sb[name]

    def alias(self, name, shape, dtype, byte_off):
        base = self.nc.lookup_mloc(self.sb["big"]).addr
        self.sb[name] = self.nc.alloc_sbuf_tensor_at(
            name, shape, dtype, offset=base + byte_off)
        return self.sb[name]


def declare_io(p: P):
    p.din("xin", [D, T], F32)
    p.din("wq_a", [D, D], F32)
    p.din("wkv_a", [D, 256], F32)
    p.din("wo_a", [D, D], F32)
    p.din("wq_b", [D, D], F32)
    p.din("wk_b", [D, 1024], F32)
    p.din("wv_b", [D, 1024], F32)
    p.din("wo_b", [D, D], F32)
    p.din("gate_b", [FI, D // 2], U8)
    p.din("up_b", [FI, D // 2], U8)
    p.din("down_b", [D, FI // 2], U8)
    p.din("gate_s", [FI, D // GRP], F32)
    p.din("gate_z", [FI, D // GRP], F32)
    p.din("up_s", [FI, D // GRP], F32)
    p.din("up_z", [FI, D // GRP], F32)
    p.din("down_s", [D, FI // GRP], F32)
    p.din("down_z", [D, FI // GRP], F32)
    p.din("rel128", [128, 512], BF16)
    p.din("relmask", [4, 128, 512], BF16)
    p.din("cmask", [4, 128, 512], BF16)
    p.din("smat", [128, 128], BF16)
    p.din("ident", [128, 128], F32)
    p.din("offv_a", [128, 128], F32)
    p.din("offv_b", [128, 128], F32)
    p.din("pos", [1, T], F32)
    p.din("nonce", [1, 2], I32)

    p.dout("xout", [D, T], F32)

    p.dtmp("x1", [D, T], F32)
    p.dtmp("x2", [D, T], F32)
    p.dtmp("x3", [D, T], F32)
    p.dtmp("wd_T", [FI, D], BF16)
    p.dtmp("wgS", [NIC, 128, 4096], BF16)
    p.dtmp("wuS", [NIC, 128, 4096], BF16)
    p.dtmp("qa", [D, T], BF16)
    p.dtmp("qb", [D, T], BF16)
    p.dtmp("ctxa", [D, T], BF16)
    p.dtmp("ctxb", [D, T], BF16)
    p.dtmp("hbuf", [FI, T], BF16)
    p.dtmp("ka_loc", [128, T], BF16)
    p.dtmp("va_loc", [T, 128], BF16)
    p.dtmp("kb_loc", [1024, T], BF16)
    p.dtmp("vb_loc", [T, 1024], BF16)
    p.dtmp("sh_ka", [2, 128, T], BF16, shared=True)
    p.dtmp("sh_va", [2, T, 128], BF16, shared=True)
    p.dtmp("sh_kb", [2, 1024, T], BF16, shared=True)
    p.dtmp("sh_vb", [2, T, 1024], BF16, shared=True)
    p.dtmp("sh_flag", [2, 2], I32, shared=True)
    if "xln_dbg" in p.debug:
        p.dtmp("xln_dbg", [D, T], BF16)


def alloc_sbuf(p: P):
    p.sbuf("c_rel128", [128, 512], BF16)
    p.sbuf("c_relmask", [128, 4 * 512], BF16)
    p.sbuf("c_cmask", [128, 4 * 512], BF16)
    p.sbuf("c_smat", [128, 128], BF16)
    p.sbuf("c_ident", [128, 128], F32)
    p.sbuf("c_offa", [128, 128], F32)
    p.sbuf("c_offb", [128, 128], F32)

    p.sbuf("c_nonce", [1, 2], I32)
    p.sbuf("c_ones_bf", [128, 1], BF16)
    p.sbuf("c_ones_f", [1, 128], F32)
    p.sbuf("c_invf", [1, 128], F32)
    p.sbuf("c_bias", [128, 64], F32)
    p.sbuf("c_zs", [128, 384], F32)
    p.sbuf("r_rows", [1, 2048], F32)
    p.sbuf("r_poll", [1, 2], I32)
    p.sbuf("i512", [128, 512], I32)
    p.sbuf("fbuf", [128, 2048], F32)
    p.sbuf("obuf", [128, 4096], BF16)
    p.sbuf("hstage", [128, 1024], BF16)
    p.sbuf("kvstage", [128, 1024], BF16)
    p.sbuf("big", [128, 83968], BF16)
    rem = p.nc.sbuf_bytes_remaining
    assert rem >= 0, f"sbuf over budget by {-rem}"

    p.alias("dq_b32", [128, 2 * 2752], I32, 0)
    p.alias("dq_nib", [128, 2752], I32, 22016)
    p.alias("dq_d1", [128, 5504], F32, 33024)
    p.alias("dq_wf", [128, 2 * 5504], F32, 55040)
    p.alias("tstage", [128, 2 * 5504], BF16, 99072)
    p.alias("xln", [128, NKC * T], BF16, 0)
    p.alias("wslab", [128, 2 * 4096], BF16, 131072)
    p.alias("wkv_sb", [128, 8192], BF16, 131072)
    p.alias("ffnblk", [128, 4 * 4096], BF16, 131072)
    p.alias("cs_fm", [128, 2 * T], BF16, 147456)
    p.alias("c_pos", [1, T], F32, 155648)
    p.alias("k_sb", [128, 4096], BF16, 0)
    p.alias("v_sb", [128, 4096], BF16, 8192)
    p.alias("q_sb", [128, 2 * 2048], BF16, 16384)
    p.alias("e_sb", [128, 4 * 512], BF16, 24576)
    p.alias("pr_sb", [128, 4 * 512], BF16, 28672)
    p.alias("a_sb", [128, 2 * 5 * 512], BF16, 32768)
    p.alias("co_sb", [128, 2 * 512], BF16, 43008)
    p.alias("rb_sb", [128, 2 * 512], F32, 45056)
    p.alias("kdiag", [128, 4 * 512], BF16, 49152)
    p.alias("vdiag", [128, 4 * 512], BF16, 53248)
    p.alias("h_tt", [128, NIC * 512], BF16, 0)
    p.alias("wd_slab", [128, 8 * 1024], BF16, 90112)
    p.PS = [p.ctx.enter_context(p.nc.psum_tensor(f"P{i}", [128, 512], F32))
            for i in range(8)]




def _set_parity(p: P, e, tag):
    r = e.alloc_register(f"par_{tag}")
    ro = e.alloc_register(f"opar_{tag}")
    for arm in e.switch_core_id(128):
        e.reg_mov(r, int(arm) & 1)
    e.reg_mov(ro, 1)
    e.reg_alu(ro, ro, r, OP.subtract)
    p.par_snap[tag] = e.snap(r, min_val=0, max_val=1)
    if not hasattr(p, "opar_snap"):
        p.opar_snap = {}
    p.opar_snap[tag] = e.snap(ro, min_val=0, max_val=1)


def qbarrier(p: P, e, sems=("te", "ve", "se"), own=None):
    if own is not None:
        p.qdrain(e, own)
    if p.qn["sync"]:
        e.wait_ge(p.sem["sqp"], p.qn["sync"])
    if p.qn["gp"]:
        e.wait_ge(p.sem["gqp"], p.qn["gp"])
    for s_ in sems:
        if p.cnt[s_]:
            e.wait_ge(p.sem[s_], p.cnt[s_])


def stage_init(p: P):
    nc, sb, dr = p.nc, p.sb, p.dram
    with nc.Block() as block:
        @block.sync
        def _(e):
            pairs = [
                (sb["c_rel128"][:, :], dr["rel128"][:, :]),
                (bass.AP(sb["c_relmask"], 0, [[2048, 128], [512, 4], [1, 512]]),
                 bass.AP(dr["relmask"], 0, [[512, 128], [128 * 512, 4], [1, 512]])),
                (bass.AP(sb["c_cmask"], 0, [[2048, 128], [512, 4], [1, 512]]),
                 bass.AP(dr["cmask"], 0, [[512, 128], [128 * 512, 4], [1, 512]])),
                (sb["c_smat"][:, :], dr["smat"][:, :]),
                (sb["c_ident"][:, :], dr["ident"][:, :]),
                (sb["c_offa"][:, :], dr["offv_a"][:, :]),
                (sb["c_offb"][:, :], dr["offv_b"][:, :]),
                (sb["c_nonce"][:, :], dr["nonce"][:, :]),
            ]
            for o, i in pairs:
                p.dma(e, "sync", o, i)
            p.qdrain(e, "sync")
            _set_parity(p, e, "sync")

        @block.gpsimd
        def _(e):
            e.memset(sb["c_ones_bf"][:, :], 1.0)
            e.memset(sb["c_ones_f"][:, :], 1.0)
            e.memset(sb["r_poll"][:, :], 0).then_inc(p.sem["sm"], 1)
            p.bump("sm", 1)
            _set_parity(p, e, "gp")
            e.wait_ge(p.sem["sm"], p.cnt["sm"])
            e.dma_start(out=dr["sh_flag"][bass.ds(p.par_snap["gp"], 1), 0:2],
                        in_=sb["r_poll"][0:1, 0:2]).then_inc(p.sem["pf"], 16)


def stage_dequant(p: P, bname, zname, sname, oname, nout, nin, blocks):
    nc, sb, dr = p.nc, p.sb, p.dram
    PS = p.PS
    noc = nout // 128
    passes = [(k * 5504, min(5504, nin - k * 5504))
              for k in range((nin + 5503) // 5504)]
    jobs = [(ic, off, w) for ic in range(noc) for (off, w) in passes]
    ntl = [w // 128 for (_, _, w) in jobs]
    cumt = [0]
    for t_ in ntl:
        cumt.append(cumt[-1] + t_)
    ve0, te0, se0 = p.cnt["ve"], p.cnt["te"], p.cnt["se"]
    ldix = {}
    stix = {}
    with nc.Block() as block:
        @block.gpsimd
        def _(e):
            qbarrier(p, e, own="gp")
            for n, (ic, off, w) in enumerate(jobs):
                if n >= 2:
                    p.qdrain(e, "gp")
                    e.wait_ge(p.sem["ve"], ve0 + (n - 1))
                sl = n % 2
                ng = w // GRP
                p.dma(e, "gp",
                      sb["dq_b32"][:, sl * 2752: sl * 2752 + w // 2],
                      dr[bname][128 * ic:128 * (ic + 1),
                                off // 2: off // 2 + w // 2])
                p.dma(e, "gp",
                      sb["c_zs"][:, sl * 192: sl * 192 + ng],
                      dr[zname][128 * ic:128 * (ic + 1),
                                off // GRP: off // GRP + ng])
                ldix[n] = p.dma(
                    e, "gp",
                    sb["c_zs"][:, sl * 192 + 96: sl * 192 + 96 + ng],
                    dr[sname][128 * ic:128 * (ic + 1),
                              off // GRP: off // GRP + ng])
            p.qdrain(e, "gp")

        @block.sync
        def _(e):
            for n, (ic, off, w) in enumerate(jobs):
                sl = n % 2
                e.wait_ge(p.sem["se"], se0 + cumt[n + 1])
                if blocks:
                    stix[n] = p.dma(
                        e, "sync",
                        dr[oname][ic],
                        sb["tstage"][:, sl * 5504: sl * 5504 + w])
                else:
                    stix[n] = p.dma(
                        e, "sync",
                        bass.AP(dr[oname], off * nout + 128 * ic,
                                [[nout, 128], [128 * nout, w // 128],
                                 [1, 128]]),
                        bass.AP(sb["tstage"], sl * 5504,
                                [[2 * 5504, 128], [128, w // 128], [1, 128]]))
                p.qdrain(e, "sync")
            p.qdrain(e, "sync")

        @block.tensor
        def _(e):
            for n, (ic, off, w) in enumerate(jobs):
                sl = n % 2
                e.wait_ge(p.sem["ve"], ve0 + (n + 1))
                for kt in range(w // 128):
                    g = cumt[n] + kt
                    if g >= 8:
                        e.wait_ge(p.sem["se"], se0 + g - 7)
                    e.transpose(
                        PS[g % 8][:, 0:128],
                        sb["dq_wf"][:, sl * 5504 + kt * 128:
                                    sl * 5504 + kt * 128 + 128],
                        sb["c_ident"][:, :]).then_inc(p.sem["te"], 1)
                    p.bump("te", 1)

        @block.scalar
        def _(e):
            for n, (ic, off, w) in enumerate(jobs):
                sl = n % 2
                if n >= 2:
                    p.wait_dma(e, "sync", stix[n - 2])
                for kt in range(w // 128):
                    g = cumt[n] + kt
                    e.wait_ge(p.sem["te"], te0 + g + 1)
                    e.copy(sb["tstage"][:, sl * 5504 + kt * 128:
                                        sl * 5504 + kt * 128 + 128],
                           PS[g % 8][:, 0:128]).then_inc(p.sem["se"], 1)
                    p.bump("se", 1)

        @block.vector
        def _(e):
            for n, (ic, off, w) in enumerate(jobs):
                sl = n % 2
                ng = w // GRP
                p.wait_dma(e, "gp", ldix[n])
                if n >= 2:
                    e.wait_ge(p.sem["te"], te0 + cumt[n - 1])
                bt = sb["dq_b32"][:, sl * 2752: sl * 2752 + w // 2]
                zb = bass.AP(sb["c_zs"], sl * 192, [[384, 128], [1, ng], [0, 32]])
                sc = bass.AP(sb["c_zs"], sl * 192 + 96,
                             [[384, 128], [1, ng], [0, 32]])
                nib = sb["dq_nib"][:, 0:w // 2]
                for half, op_, ofs in ((0, OP.logical_shift_right, 4),
                                       (1, OP.bitwise_and, 15)):
                    e.tensor_scalar(nib, bt, ofs, None, op_)
                    d1 = bass.AP(sb["dq_d1"], half, [[5504, 128], [GRP, ng], [2, 32]])
                    n3 = bass.AP(sb["dq_nib"], 0, [[2752, 128], [32, ng], [1, 32]])
                    e.tensor_tensor(d1, n3, zb, OP.subtract)
                    wq_ = bass.AP(sb["dq_wf"], sl * 5504 + half,
                                  [[11008, 128], [GRP, ng], [2, 32]])
                    ins = e.tensor_tensor(wq_, d1, sc, OP.mult)
                ins.then_inc(p.sem["ve"], 1)
                p.bump("ve", 1)


def stage_ln(p: P, srcname, tap=None):
    nc, sb, dr = p.nc, p.sb, p.dram
    PS = p.PS

    def xsl(kc, tt):
        return sb["xln"][:, kc * T + tt * 512: kc * T + tt * 512 + 512]

    ldix = {}
    for tt in range(4):
        te0, ve0, se0 = (p.cnt[k] for k in ("te", "ve", "se"))
        with nc.Block() as block:
            @block.gpsimd
            def _(e):
                if tt == 0:
                    qbarrier(p, e, own="gp")
                ldix[tt] = p.dma(
                    e, "gp",
                    bass.AP(sb["xln"], tt * 512,
                            [[NKC * T, 128], [T, NKC], [1, 512]]),
                    bass.AP(dr[srcname], tt * 512,
                            [[T, 128], [128 * T, NKC], [1, 512]]))
                p.qdrain(e, "gp")

            @block.tensor
            def _(e):
                p.wait_dma(e, "gp", ldix[tt])
                for kc in range(NKC):
                    ins = e.matmul(PS[6][0:1, :], sb["c_ones_bf"][:, :],
                                   xsl(kc, tt), start=(kc == 0),
                                   stop=(kc == NKC - 1))
                ins.then_inc(p.sem["te"], 1)
                p.bump("te", 1)
                for kc in range(NKC):
                    e.wait_ge(p.sem["ve"], ve0 + kc + 1)
                    e.matmul(PS[7][0:1, :], sb["c_ones_bf"][:, :],
                             sb["obuf"][:, (kc % 2) * 512:(kc % 2) * 512 + 512],
                             start=(kc == 0), stop=(kc == NKC - 1)
                             ).then_inc(p.sem["te"], 1)
                    p.bump("te", 1)
                e.wait_ge(p.sem["ve"], ve0 + NKC + 2)
                e.matmul(PS[6][:, :], sb["c_ones_f"][:, :],
                         sb["r_rows"][0:1, 512:1024], start=True,
                         stop=True).then_inc(p.sem["te"], 1)
                e.matmul(PS[7][:, :], sb["c_ones_f"][:, :],
                         sb["r_rows"][0:1, 0:512], start=True,
                         stop=True).then_inc(p.sem["te"], 1)
                p.bump("te", 2)

            @block.vector
            def _(e):
                p.wait_dma(e, "gp", ldix[tt])
                for kc in range(NKC):
                    if kc >= 2:
                        e.wait_ge(p.sem["te"], te0 + 1 + (kc - 1))
                    e.tensor_mul(
                        sb["obuf"][:, (kc % 2) * 512:(kc % 2) * 512 + 512],
                        xsl(kc, tt), xsl(kc, tt)).then_inc(p.sem["ve"], 1)
                    p.bump("ve", 1)
                r = sb["r_rows"]
                e.wait_ge(p.sem["te"], te0 + 1 + NKC)
                e.tensor_scalar(r[0:1, 1024:1536], PS[6][0:1, :], 1.0 / D,
                                None, OP.mult)
                e.tensor_scalar(r[0:1, 1536:2048], PS[7][0:1, :], 1.0 / D,
                                None, OP.mult)
                e.tensor_mul(r[0:1, 0:512], r[0:1, 1024:1536],
                             r[0:1, 1024:1536])
                e.tensor_sub(r[0:1, 1536:2048], r[0:1, 1536:2048],
                             r[0:1, 0:512])
                e.tensor_scalar(r[0:1, 1536:2048], r[0:1, 1536:2048],
                                LN_EPS, None, OP.add).then_inc(p.sem["ve"], 1)
                p.bump("ve", 1)
                e.wait_ge(p.sem["se"], se0 + 1)
                e.reciprocal(r[0:1, 512:1024], r[0:1, 0:512])
                e.scalar_tensor_tensor(
                    r[0:1, 0:512], r[0:1, 1024:1536], -1.0,
                    r[0:1, 512:1024], OP.mult, OP.mult
                ).then_inc(p.sem["ve"], 1)
                p.bump("ve", 1)
                e.wait_ge(p.sem["se"], se0 + 3)
                for kc in range(NKC):
                    e.tensor_mul(sb["fbuf"][:, 0:512], xsl(kc, tt),
                                 sb["fbuf"][:, 1024:1536])
                    ins = e.tensor_add(xsl(kc, tt), sb["fbuf"][:, 0:512],
                                       sb["fbuf"][:, 1536:2048])
                ins.then_inc(p.sem["ve"], 1)
                p.bump("ve", 1)

            @block.scalar
            def _(e):
                r = sb["r_rows"]
                e.wait_ge(p.sem["ve"], ve0 + NKC + 1)
                e.activation(r[0:1, 0:512], r[0:1, 1536:2048],
                             AF.Sqrt).then_inc(p.sem["se"], 1)
                p.bump("se", 1)
                e.wait_ge(p.sem["te"], te0 + NKC + 3)
                e.copy(sb["fbuf"][:, 1024:1536], PS[6][:, :]).then_inc(
                    p.sem["se"], 1)
                e.copy(sb["fbuf"][:, 1536:2048], PS[7][:, :]).then_inc(
                    p.sem["se"], 1)
                p.bump("se", 2)

    if tap is not None and tap in p.debug:
        ve_done = p.cnt["ve"]
        with nc.Block() as block:
            @block.sync
            def _(e):
                e.wait_ge(p.sem["ve"], ve_done)
                p.dma(e, "sync",
                      bass.AP(dr[tap], 0, [[T, 128], [128 * T, NKC], [1, T]]),
                      sb["xln"][:, :])
                p.qdrain(e, "sync")


def stage_load_arena(p: P, srcname):
    nc, sb, dr = p.nc, p.sb, p.dram
    with nc.Block() as block:
        @block.sync
        def _(e):
            qbarrier(p, e, own="sync")
            for q in range(4):
                p.dma(e, "sync",
                      bass.AP(sb["xln"], q * 8 * T,
                              [[NKC * T, 128], [T, 8], [1, T]]),
                      bass.AP(dr[srcname], q * 8 * 128 * T,
                              [[T, 128], [128 * T, 8], [1, T]]))
            p.qdrain(e, "sync")
    p.arena_ready = p.qn["sync"]


def stage_proj(p: P, wname, ngroups, mode, dst, xold=None):
    nc, sb, dr = p.nc, p.sb, p.dram
    PS = p.PS
    te0, ve0, se0 = (p.cnt[k] for k in ("te", "ve", "se"))

    def asl(kc, tt):
        return sb["xln"][:, kc * T + tt * 512: kc * T + tt * 512 + 512]

    tepg = 2 if mode == "rope" else 1
    vepg = 8 if mode == "rope" else 4
    wix = {}
    xix = {}
    six = {}
    with nc.Block() as block:
        @block.gpsimd
        def _(e):
            qbarrier(p, e, own="gp")
            for ng in range(ngroups):
                sl = ng % 2
                if ng >= 2:
                    p.qdrain(e, "gp")
                    e.wait_ge(p.sem["te"], te0 + tepg * (ng - 1))
                wix[ng] = p.dma(
                    e, "gp",
                    sb["wslab"][:, sl * 4096:(sl + 1) * 4096],
                    bass.AP(dr[wname], 128 * ng,
                            [[dr[wname].shape[1], 128],
                             [128 * dr[wname].shape[1], NKC], [1, 128]]))
            p.qdrain(e, "gp")

        @block.sync
        def _(e):
            for ng in range(ngroups):
                if mode == "resid":
                    for tt in range(6):
                        if tt < 4:
                            n = ng * 4 + tt
                            if n >= 2:
                                p.qdrain(e, "sync")
                                e.wait_ge(p.sem["ve"], ve0 + 4 * ng + tt - 1)
                            xix[n] = p.dma(
                                e, "sync",
                                sb["fbuf"][:, (n % 2) * 512:(n % 2) * 512 + 512],
                                dr[xold][128 * ng:128 * (ng + 1),
                                         tt * 512:(tt + 1) * 512])
                        if tt >= 2:
                            ns = ng * 4 + tt - 2
                            p.qdrain(e, "sync")
                            e.wait_ge(p.sem["ve"], ve0 + ns + 1 - 4 * ng + 4 * ng)
                            six[ns] = p.dma(
                                e, "sync",
                                dr[dst][128 * ng:128 * (ng + 1),
                                        (tt - 2) * 512:(tt - 1) * 512],
                                sb["fbuf"][:, 1024 + (ns % 2) * 512:
                                           1024 + (ns % 2) * 512 + 512])
                else:
                    p.qdrain(e, "sync")
                    e.wait_ge(p.sem["ve"], ve0 + vepg * ng + vepg)
                    if mode == "store":
                        osl = (ng % 2) * 2048
                        six[ng] = p.dma(
                            e, "sync",
                            dr[dst][128 * ng:128 * (ng + 1), :],
                            sb["obuf"][:, osl:osl + 2048])
                    else:
                        six[ng] = p.dma(
                            e, "sync",
                            dr[dst][128 * ng:128 * (ng + 1), :],
                            sb["obuf"][:, 2048:4096])
            p.qdrain(e, "sync")

        @block.tensor
        def _(e):
            for ng in range(ngroups):
                sl = ng % 2
                q = (ng % 2) * 4 if mode != "rope" else 0
                p.wait_dma(e, "gp", wix[ng])
                if (mode != "rope" and ng >= 2) or (mode == "rope" and ng >= 1):
                    back = 1 if mode == "rope" else 2
                    e.wait_ge(p.sem["ve"], ve0 + vepg * (ng - back) + vepg)
                for kc in range(NKC):
                    for tt in range(4):
                        ins = e.matmul(PS[q + tt][:, :],
                                       sb["wslab"][:, sl * 4096 + kc * 128:
                                                   sl * 4096 + kc * 128 + 128],
                                       asl(kc, tt), start=(kc == 0),
                                       stop=(kc == NKC - 1))
                ins.then_inc(p.sem["te"], 1)
                p.bump("te", 1)
                if mode == "rope":
                    e.wait_ge(p.sem["ve"], ve0 + vepg * ng + 4)
                    for tt in range(4):
                        ins = e.matmul(PS[4 + tt][:, :], sb["c_smat"][:, :],
                                       sb["obuf"][:, tt * 512:(tt + 1) * 512],
                                       start=True, stop=True)
                    ins.then_inc(p.sem["te"], 1)
                    p.bump("te", 1)

        @block.vector
        def _(e):
            for ng in range(ngroups):
                q = (ng % 2) * 4 if mode != "rope" else 0
                if mode == "store":
                    e.wait_ge(p.sem["te"], te0 + tepg * ng + 1)
                    if ng >= 2:
                        p.wait_dma(e, "sync", six[ng - 2])
                    osl = (ng % 2) * 2048
                    for tt in range(4):
                        e.tensor_copy(
                            sb["obuf"][:, osl + tt * 512: osl + tt * 512 + 512],
                            PS[q + tt][:, :]).then_inc(p.sem["ve"], 1)
                        p.bump("ve", 1)
                elif mode == "resid":
                    e.wait_ge(p.sem["te"], te0 + tepg * ng + 1)
                    for tt in range(4):
                        n = ng * 4 + tt
                        p.wait_dma(e, "sync", xix[n])
                        if n >= 2:
                            p.wait_dma(e, "sync", six[n - 2])
                        fs = (n % 2) * 512
                        e.tensor_tensor(
                            sb["fbuf"][:, 1024 + fs:1024 + fs + 512],
                            PS[q + tt][:, :], sb["fbuf"][:, fs:fs + 512],
                            OP.add).then_inc(p.sem["ve"], 1)
                        p.bump("ve", 1)
                else:
                    e.wait_ge(p.sem["te"], te0 + tepg * ng + 1)
                    if ng >= 1:
                        p.wait_dma(e, "sync", six[ng - 1])
                    for tt in range(4):
                        e.tensor_copy(
                            sb["obuf"][:, tt * 512: tt * 512 + 512],
                            PS[tt][:, :]).then_inc(p.sem["ve"], 1)
                        p.bump("ve", 1)
                    e.wait_ge(p.sem["te"], te0 + tepg * ng + 2)
                    for tt in range(4):
                        csl = sb["cs_fm"][:, tt * 512: tt * 512 + 512]
                        ssl = sb["cs_fm"][:, T + tt * 512: T + tt * 512 + 512]
                        e.tensor_mul(sb["fbuf"][:, 0:512],
                                     sb["obuf"][:, tt * 512: tt * 512 + 512],
                                     csl)
                        e.tensor_mul(sb["fbuf"][:, 512:1024],
                                     PS[4 + tt][:, :], ssl)
                        e.tensor_add(
                            sb["obuf"][:, 2048 + tt * 512: 2048 + tt * 512 + 512],
                            sb["fbuf"][:, 0:512],
                            sb["fbuf"][:, 512:1024]).then_inc(p.sem["ve"], 1)
                        p.bump("ve", 1)


def stage_kv_a(p: P):
    nc, sb, dr = p.nc, p.sb, p.dram
    PS = p.PS
    te0, ve0 = p.cnt["te"], p.cnt["ve"]
    wix = {}
    six = {}
    with nc.Block() as block:
        @block.gpsimd
        def _(e):
            qbarrier(p, e, own="gp")
            wix[0] = p.dma(e, "gp", sb["wkv_sb"][:, 0:4096],
                           bass.AP(dr["wkv_a"], 0,
                                   [[256, 128], [128 * 256, NKC], [1, 128]]))
            wix[1] = p.dma(e, "gp", sb["wkv_sb"][:, 4096:8192],
                           bass.AP(dr["wkv_a"], 128,
                                   [[256, 128], [128 * 256, NKC], [1, 128]]))
            p.qdrain(e, "gp")

        @block.sync
        def _(e):
            p.qdrain(e, "sync")
            e.wait_ge(p.sem["ve"], ve0 + 4)
            six[0] = p.dma(e, "sync", dr["ka_loc"][:, :],
                           sb["obuf"][:, 0:2048])
            for tcg in range(2):
                p.qdrain(e, "sync")
                e.wait_ge(p.sem["ve"], ve0 + 4 + 8 * (tcg + 1))
                six[1 + tcg] = p.dma(
                    e, "sync",
                    bass.AP(dr["va_loc"], tcg * 1024 * 128,
                            [[128, 128], [128 * 128, 8], [1, 128]]),
                    bass.AP(sb["kvstage"], 0,
                            [[1024, 128], [128, 8], [1, 128]]))
            p.qdrain(e, "sync")

        @block.tensor
        def _(e):
            p.wait_dma(e, "gp", wix[1])
            for kc in range(NKC):
                for tt in range(4):
                    ins = e.matmul(PS[tt][:, :],
                                   sb["wkv_sb"][:, kc * 128: kc * 128 + 128],
                                   sb["xln"][:, kc * T + tt * 512:
                                             kc * T + tt * 512 + 512],
                                   start=(kc == 0), stop=(kc == NKC - 1))
            ins.then_inc(p.sem["te"], 1)
            p.bump("te", 1)
            for tcg in range(2):
                e.wait_ge(p.sem["ve"], ve0 + 4 + 8 * tcg)
                for kc in range(NKC):
                    for t8 in range(8):
                        tc = tcg * 8 + t8
                        ins = e.matmul(
                            PS[t8][:, 0:128],
                            sb["xln"][:, kc * T + tc * 128: kc * T + tc * 128 + 128],
                            sb["wkv_sb"][:, 4096 + kc * 128: 4096 + kc * 128 + 128],
                            start=(kc == 0), stop=(kc == NKC - 1))
                ins.then_inc(p.sem["te"], 1)
                p.bump("te", 1)

        @block.vector
        def _(e):
            e.wait_ge(p.sem["te"], te0 + 1)
            for tt in range(4):
                e.tensor_copy(sb["obuf"][:, tt * 512:(tt + 1) * 512],
                              PS[tt][:, :]).then_inc(p.sem["ve"], 1)
                p.bump("ve", 1)
            for tcg in range(2):
                e.wait_ge(p.sem["te"], te0 + 2 + tcg)
                if tcg == 1:
                    p.wait_dma(e, "sync", six[1])
                for t8 in range(8):
                    e.tensor_copy(
                        sb["kvstage"][:, t8 * 128: t8 * 128 + 128],
                        PS[t8][:, 0:128]).then_inc(p.sem["ve"], 1)
                    p.bump("ve", 1)


def stage_vb(p: P):
    nc, sb, dr = p.nc, p.sb, p.dram
    PS = p.PS
    te0, ve0 = p.cnt["te"], p.cnt["ve"]
    wix = {}
    six = {}
    with nc.Block() as block:
        @block.gpsimd
        def _(e):
            qbarrier(p, e, own="gp")
            for tcg in range(4):
                for kc in range(NKC):
                    n = tcg * NKC + kc
                    if n >= 2:
                        p.qdrain(e, "gp")
                        e.wait_ge(p.sem["te"], te0 + (n - 1))
                    sl = n % 2
                    wix[n] = p.dma(
                        e, "gp",
                        sb["wslab"][:, sl * 4096: sl * 4096 + 1024],
                        dr["wv_b"][128 * kc:128 * (kc + 1), :])
            p.qdrain(e, "gp")

        @block.sync
        def _(e):
            for tcg in range(4):
                p.qdrain(e, "sync")
                e.wait_ge(p.sem["ve"], ve0 + 8 * (tcg + 1))
                six[tcg] = p.dma(
                    e, "sync",
                    bass.AP(dr["vb_loc"], tcg * 512 * 1024,
                            [[1024, 128], [128 * 1024, 4], [1, 1024]]),
                    bass.AP(sb["obuf"], 0,
                            [[4096, 128], [1024, 4], [1, 1024]]))
            p.qdrain(e, "sync")

        @block.tensor
        def _(e):
            for tcg in range(4):
                if tcg >= 1:
                    e.wait_ge(p.sem["ve"], ve0 + 8 * tcg)
                for kc in range(NKC):
                    n = tcg * NKC + kc
                    sl = n % 2
                    p.wait_dma(e, "gp", wix[n])
                    for t4 in range(4):
                        tc = tcg * 4 + t4
                        for hf in range(2):
                            ins = e.matmul(
                                PS[t4 * 2 + hf][:, :],
                                sb["xln"][:, kc * T + tc * 128:
                                          kc * T + tc * 128 + 128],
                                sb["wslab"][:, sl * 4096 + hf * 512:
                                            sl * 4096 + hf * 512 + 512],
                                start=(kc == 0), stop=(kc == NKC - 1))
                    ins.then_inc(p.sem["te"], 1)
                    p.bump("te", 1)

        @block.vector
        def _(e):
            for tcg in range(4):
                e.wait_ge(p.sem["te"], te0 + NKC * (tcg + 1))
                if tcg >= 1:
                    p.wait_dma(e, "sync", six[tcg - 1])
                for t4 in range(4):
                    for hf in range(2):
                        e.tensor_copy(
                            sb["obuf"][:, t4 * 1024 + hf * 512:
                                       t4 * 1024 + hf * 512 + 512],
                            PS[t4 * 2 + hf][:, :]).then_inc(p.sem["ve"], 1)
                        p.bump("ve", 1)


def stage_flagpoll(p: P, phase, copies):
    nc, sb, dr = p.nc, p.sb, p.dram
    with nc.Block() as block:
        @block.sync
        def _(e):
            qbarrier(p, e, sems=(), own="sync")
            for shname, locname in copies:
                p.dma(e, "sync",
                      dr[shname][bass.ds(p.par_snap["sync"], 1)],
                      dr[locname].ap())
            p.qdrain(e, "sync")

        @block.gpsimd
        def _(e):
            e.wait_ge(p.sem["sqp"], p.qn["sync"])
            e.dma_start(
                out=dr["sh_flag"][bass.ds(p.par_snap["gp"], 1),
                                  phase:phase + 1],
                in_=sb["c_nonce"][0:1, phase:phase + 1],
            ).then_inc(p.sem["pf"], 16)
            with (e.register(f"pr{phase}") as pr, e.register(f"pe{phase}") as pe,
                  e.register(f"nd{phase}") as nd):
                e.reg_load(pe, sb["c_nonce"][0:1, phase:phase + 1])
                e.memset(sb["r_poll"][0:1, 0:1], 0)
                e.reg_mov(nd, 1)
                with e.While(nd):
                    e.dma_start(
                        out=sb["r_poll"][0:1, 0:1],
                        in_=dr["sh_flag"][bass.ds(p.opar_snap["gp"], 1),
                                          phase:phase + 1]
                    ).then_inc(p.sem["pf"], 16)
                    e.nop(cycle_cnt=30000)
                    e.reg_load(pr, sb["r_poll"][0:1, 0:1])
                    e.reg_alu(nd, pr, pe, OP.not_equal)
            e.sem_inc(p.sem["pd"], 1)
            p.poll_done = getattr(p, "poll_done", 0) + 1


def stage_cossin(p: P):
    nc, sb, dr = p.nc, p.sb, p.dram
    PS = p.PS
    TWO_PI = 2.0 * math.pi
    ve0, se0 = p.cnt["ve"], p.cnt["se"]
    ldp = {}
    with nc.Block() as block:
        @block.sync
        def _(e):
            qbarrier(p, e, own="sync")
            ldp[0] = p.dma(e, "sync", sb["c_pos"][:, :], dr["pos"][:, :])
            p.qdrain(e, "sync")

        @block.gpsimd
        def _(e):
            e.iota(sb["r_rows"][0:1, 0:64], [[1, 64]], base=0,
                   channel_multiplier=0,
                   allow_small_or_imprecise_dtypes=True).then_inc(
                       p.sem["sm"], 1)
            p.bump("sm", 1)

        @block.scalar
        def _(e):
            e.wait_ge(p.sem["sm"], p.cnt["sm"])
            e.activation(sb["r_rows"][0:1, 64:128], sb["r_rows"][0:1, 0:64],
                         AF.Exp, scale=-math.log(500000.0) / 64.0
                         ).then_inc(p.sem["se"], 1)
            p.bump("se", 1)

        @block.vector
        def _(e):
            e.wait_ge(p.sem["se"], se0 + 1)
            r = sb["r_rows"]
            f0 = r[0:1, 64:128]
            e.tensor_scalar(r[0:1, 128:192], f0, TWO_PI / 8192.0, None,
                            OP.is_lt)
            e.tensor_scalar(r[0:1, 192:256], f0, TWO_PI / 2048.0, None,
                            OP.is_gt)
            e.tensor_scalar(r[0:1, 256:320], f0, 8192.0 / (TWO_PI * 3.0),
                            -1.0 / 3.0, OP.mult, OP.add)
            e.tensor_scalar(r[0:1, 256:320], r[0:1, 256:320], 31.0 / 32.0,
                            1.0 / 32.0, OP.mult, OP.add)
            e.tensor_mul(r[0:1, 256:320], r[0:1, 256:320], f0)
            e.tensor_scalar(r[0:1, 320:384], r[0:1, 128:192], -1.0, 1.0,
                            OP.mult, OP.add)
            e.tensor_scalar(r[0:1, 384:448], r[0:1, 192:256], -1.0, 1.0,
                            OP.mult, OP.add)
            e.tensor_mul(r[0:1, 320:384], r[0:1, 320:384], r[0:1, 384:448])
            e.tensor_scalar(r[0:1, 384:448], f0, 1.0 / 32.0, None, OP.mult)
            e.tensor_mul(r[0:1, 384:448], r[0:1, 384:448], r[0:1, 128:192])
            e.tensor_mul(r[0:1, 448:512], f0, r[0:1, 192:256])
            e.tensor_add(r[0:1, 384:448], r[0:1, 384:448], r[0:1, 448:512])
            e.tensor_mul(r[0:1, 256:320], r[0:1, 256:320], r[0:1, 320:384])
            e.tensor_add(r[0:1, 384:448], r[0:1, 384:448], r[0:1, 256:320])
            e.tensor_copy(sb["c_invf"][0:1, 0:64], r[0:1, 384:448])
            e.tensor_copy(sb["c_invf"][0:1, 64:128], r[0:1, 384:448]
                          ).then_inc(p.sem["ve"], 1)
            p.bump("ve", 1)

    for tt in range(4):
        for fn in range(2):
            te0b, ve0b, se0b = p.cnt["te"], p.cnt["ve"], p.cnt["se"]
            with nc.Block() as block:
                @block.tensor
                def _(e):
                    e.wait_ge(p.sem["ve"], ve0 + 1)
                    p.wait_dma(e, "sync", ldp[0])
                    if fn == 1 or tt > 0:
                        e.wait_ge(p.sem["se"], se0b)
                    e.matmul(PS[6][:, :], sb["c_invf"][0:1, :],
                             sb["c_pos"][0:1, tt * 512:(tt + 1) * 512],
                             start=True, stop=True).then_inc(p.sem["te"], 1)
                    p.bump("te", 1)

                @block.vector
                def _(e):
                    e.wait_ge(p.sem["te"], te0b + 1)
                    f = sb["fbuf"]
                    e.tensor_scalar(f[:, 0:512], PS[6][:, :], 1.0 / TWO_PI,
                                    0.25 if fn == 0 else 0.0, OP.mult, OP.add)
                    e.tensor_copy(sb["i512"][:, :], f[:, 0:512])
                    e.tensor_copy(f[:, 512:1024], sb["i512"][:, :])
                    e.tensor_sub(f[:, 0:512], f[:, 0:512], f[:, 512:1024])
                    e.tensor_scalar(f[:, 512:1024], f[:, 0:512], 0.5, None,
                                    OP.is_gt)
                    e.tensor_sub(f[:, 0:512], f[:, 0:512], f[:, 512:1024])
                    e.tensor_scalar(f[:, 512:1024], f[:, 0:512], -0.5, None,
                                    OP.is_lt)
                    e.tensor_add(f[:, 0:512], f[:, 0:512], f[:, 512:1024]
                                 ).then_inc(p.sem["ve"], 1)
                    p.bump("ve", 1)

                @block.scalar
                def _(e):
                    e.wait_ge(p.sem["ve"], ve0b + 1)
                    dst = sb["cs_fm"][:, fn * T + tt * 512:
                                      fn * T + tt * 512 + 512]
                    e.activation(dst, sb["fbuf"][:, 0:512], AF.Sin,
                                 scale=TWO_PI).then_inc(p.sem["se"], 1)
                    p.bump("se", 1)


CSC = [0, 12, 28, 56, 88]
TPH = 88


def stage_attn(p: P, which):
    nc, sb, dr = p.nc, p.sb, p.dram
    PS = p.PS
    a = which == "a"
    KVG, HPG = (1, 32) if a else (8, 4)
    qsrc = "qa" if a else "qb"
    cdst = "ctxa" if a else "ctxb"
    V = 3 if a else 2
    SEH = 5 if a else 4
    t10, t30, sx0, sm0, te0, ve0, se0 = (
        p.cnt[k] for k in ("t1", "t3", "sx", "sm", "te", "ve", "se"))

    def kof(h, s, idx):
        return TPH * h + CSC[s] + idx

    def kend(q):
        return TPH * (q // 4) + CSC[(q % 4) + 1]

    def rbev(q):
        return se0 + SEH * (q // 4) + (q % 4) + (2 if a else 1)

    def kvsrc_k(kvh, g):
        slot, loc = pos_block_map(g)
        if a:
            return dr["sh_ka"][slot][:, loc:loc + 512]
        return dr["sh_kb"][slot][kvh * 128:(kvh + 1) * 128, loc:loc + 512]

    def kvsrc_v(kvh, g):
        slot, loc = pos_block_map(g)
        if a:
            return bass.AP(dr["sh_va"], T * 128 * slot + loc * 128,
                           [[128, 128], [128 * 128, 4], [1, 128]])
        return bass.AP(dr["sh_vb"], T * 1024 * slot + loc * 1024 + kvh * 128,
                       [[1024, 128], [128 * 1024, 4], [1, 128]])

    kvix = {}
    qix = {}
    six = {}
    with nc.Block() as block:
        @block.sync
        def _(e):
            e.wait_ge(p.sem["pd"], p.poll_done)

            def store_q(h_, s_):
                q = h_ * 4 + s_
                p.qdrain(e, "sync")
                e.wait_ge(p.sem["ve"], ve0 + V * q + V)
                six[q] = p.dma(
                    e, "sync",
                    dr[cdst][128 * h_:128 * (h_ + 1),
                             512 * s_:512 * (s_ + 1)],
                    sb["co_sb"][:, (q % 2) * 512:(q % 2) * 512 + 512])

            for kvg in range(KVG):
                if kvg >= 1:
                    for s_ in range(4):
                        store_q(kvg * HPG - 1, s_)
                    p.qdrain(e, "sync")
                    e.wait_ge(p.sem["t3"], t30 + TPH * HPG * kvg)
                last = None
                for g in range(8):
                    p.dma(e, "sync", sb["k_sb"][:, 512 * g:512 * (g + 1)],
                          kvsrc_k(kvg, g))
                    last = p.dma(
                        e, "sync",
                        bass.AP(sb["v_sb"], 512 * g,
                                [[4096, 128], [128, 4], [1, 128]]),
                        kvsrc_v(kvg, g))
                for s_ in range(4):
                    if a:
                        ksrc = dr["ka_loc"][:, 512 * s_:512 * (s_ + 1)]
                    else:
                        ksrc = dr["kb_loc"][kvg * 128:(kvg + 1) * 128,
                                            512 * s_:512 * (s_ + 1)]
                    p.dma(e, "sync", sb["kdiag"][:, s_ * 512:(s_ + 1) * 512],
                          ksrc)
                    for kt in range(4):
                        row0 = 512 * s_ + 128 * kt
                        if a:
                            vsrc = dr["va_loc"][row0:row0 + 128, :]
                        else:
                            vsrc = dr["vb_loc"][row0:row0 + 128,
                                                kvg * 128:(kvg + 1) * 128]
                        last = p.dma(
                            e, "sync",
                            sb["vdiag"][:, s_ * 512 + kt * 128:
                                        s_ * 512 + kt * 128 + 128],
                            vsrc)
                kvix[kvg] = last
                for hh in range(HPG):
                    h = kvg * HPG + hh
                    if h >= 2:
                        p.qdrain(e, "sync")
                        e.wait_ge(p.sem["t1"], t10 + TPH * (h - 1))
                    qix[h] = p.dma(
                        e, "sync",
                        sb["q_sb"][:, (h % 2) * 2048:(h % 2) * 2048 + 2048],
                        dr[qsrc][128 * h:128 * (h + 1), :])
                    if hh >= 1:
                        for s_ in range(4):
                            store_q(h - 1, s_)
            for s_ in range(4):
                store_q(KVG * HPG - 1, s_)
            p.qdrain(e, "sync")

        @block.tensor
        def _(e):
            LA = 4
            for kvg in range(KVG):
                for hh in range(HPG):
                    h = kvg * HPG + hh
                    p.wait_dma(e, "sync", qix[h])
                    p.wait_dma(e, "sync", kvix[kvg])
                    tiles = [(s_, i_) for s_ in range(4)
                             for i_ in range(CS[s_])]
                    for j in range(len(tiles) + LA):
                        if j < len(tiles):
                            s, idx = tiles[j]
                            k = kof(h, s, idx)
                            nst = CS[s] - 4
                            diag = idx >= nst
                            m = idx - nst
                            if k >= 4:
                                e.wait_ge(p.sem["sx"], sx0 + k - 3)
                            if diag:
                                lw = sb["kdiag"][:, s * 512 + m * 128:
                                                 s * 512 + m * 128 + 128]
                            else:
                                lw = sb["k_sb"][:, 128 * idx:128 * idx + 128]
                            e.matmul(PS[2 + (k % 4)][:, :], lw,
                                     sb["q_sb"][:, (h % 2) * 2048 + 512 * s:
                                                (h % 2) * 2048 + 512 * s + 512],
                                     start=True, stop=True
                                     ).then_inc(p.sem["t1"], 1)
                            p.bump("t1", 1)
                        if j >= LA:
                            s, idx = tiles[j - LA]
                            q = h * 4 + s
                            k = kof(h, s, idx)
                            nst = CS[s] - 4
                            diag = idx >= nst
                            m = idx - nst
                            if idx == 0:
                                e.wait_ge(p.sem["ve"], ve0 + V * q)
                            if a or diag:
                                e.wait_ge(p.sem["sm"],
                                          sm0 + (k + 1 if a else q * 4 + m + 1))
                            else:
                                e.wait_ge(p.sem["sx"], sx0 + k + 1)
                            if diag:
                                lv = sb["vdiag"][:, s * 512 + m * 128:
                                                 s * 512 + m * 128 + 128]
                            else:
                                lv = sb["v_sb"][:, 128 * idx:128 * idx + 128]
                            prs = sb["pr_sb"][:, (k % 4) * 512:(k % 4) * 512 + 512]
                            e.matmul(PS[0][:, :], lv, prs,
                                     start=(idx == 0), stop=(idx == CS[s] - 1))
                            e.matmul(PS[1][0:1, :], sb["c_ones_bf"][:, :], prs,
                                     start=(idx == 0), stop=(idx == CS[s] - 1)
                                     ).then_inc(p.sem["t3"], 1)
                            p.bump("t3", 1)
                            if idx == CS[s] - 1:
                                e.wait_ge(p.sem["ve"],
                                          ve0 + V * q + (3 if a else 2) - 1)
                                if q >= 1:
                                    e.wait_ge(p.sem["se"], rbev(q - 1))
                                e.matmul(PS[6][:, :], sb["c_ones_f"][:, :],
                                         sb["r_rows"][0:1, (q % 2) * 512:
                                                      (q % 2) * 512 + 512],
                                         start=True, stop=True
                                         ).then_inc(p.sem["te"], 1)
                                p.bump("te", 1)

        @block.scalar
        def _(e):
            for kvg in range(KVG):
                for hh in range(HPG):
                    h = kvg * HPG + hh
                    asl = (h % 2) * 2560
                    if a:
                        if h >= 2:
                            e.wait_ge(p.sem["sm"], sm0 + TPH * (h - 1))
                        sl_ = SLOPES[h]
                        e.activation(sb["a_sb"][:, asl:asl + 512],
                                     sb["c_rel128"][:, :], AF.Exp, scale=sl_)
                        for m in range(4):
                            ins = e.activation(
                                sb["a_sb"][:, asl + (1 + m) * 512:
                                           asl + (2 + m) * 512],
                                sb["c_relmask"][:, m * 512:(m + 1) * 512],
                                AF.Exp, scale=sl_)
                        ins.then_inc(p.sem["se"], 1)
                        p.bump("se", 1)
                    for s in range(4):
                        q = h * 4 + s
                        nst = CS[s] - 4
                        for idx in range(CS[s]):
                            k = kof(h, s, idx)
                            diag = idx >= nst
                            e.wait_ge(p.sem["t1"], t10 + k + 1)
                            if a:
                                if k >= 4:
                                    e.wait_ge(p.sem["sm"], sm0 + k - 3)
                                if idx == 0:
                                    e.wait_ge(p.sem["ve"], ve0 + V * q + 1)
                                dst = sb["e_sb"][:, (k % 4) * 512:
                                                 (k % 4) * 512 + 512]
                                bias = (None if diag else
                                        sb["c_bias"][:, (q % 2) * 32 + idx:
                                                     (q % 2) * 32 + idx + 1])
                            else:
                                if k >= 4:
                                    e.wait_ge(p.sem["t3"], t30 + k - 3)
                                dst = sb["pr_sb"][:, (k % 4) * 512:
                                                  (k % 4) * 512 + 512]
                                bias = (None if diag else
                                        sb["c_offb"][:, s * 32 + idx:
                                                     s * 32 + idx + 1])
                            if bias is None:
                                ins = e.activation(dst, PS[2 + (k % 4)][:, :],
                                                   AF.Exp, scale=ISCALE)
                            else:
                                ins = e.activation(dst, PS[2 + (k % 4)][:, :],
                                                   AF.Exp, bias=bias,
                                                   scale=ISCALE)
                            ins.then_inc(p.sem["sx"], 1)
                            p.bump("sx", 1)
                        e.wait_ge(p.sem["te"], te0 + q + 1)
                        if q >= 2:
                            e.wait_ge(p.sem["ve"], ve0 + V * (q - 2) + V)
                        e.copy(sb["rb_sb"][:, (q % 2) * 512:(q % 2) * 512 + 512],
                               PS[6][:, :]).then_inc(p.sem["se"], 1)
                        p.bump("se", 1)

        @block.vector
        def _(e):
            for kvg in range(KVG):
                for hh in range(HPG):
                    h = kvg * HPG + hh
                    asl = (h % 2) * 2560
                    for s in range(4):
                        q = h * 4 + s
                        nst = CS[s] - 4
                        if a:
                            if q >= 2:
                                e.wait_ge(p.sem["sx"], sx0 + kend(q - 2))
                            e.tensor_scalar(
                                sb["c_bias"][:, (q % 2) * 32:(q % 2) * 32 + 32],
                                sb["c_offa"][:, s * 32:s * 32 + 32],
                                SLOPES[h], None, OP.mult
                            ).then_inc(p.sem["ve"], 1)
                            p.bump("ve", 1)
                        for idx in range(CS[s]):
                            k = kof(h, s, idx)
                            diag = idx >= nst
                            m = idx - nst
                            if a:
                                e.wait_ge(p.sem["sx"], sx0 + k + 1)
                                if k >= 4:
                                    e.wait_ge(p.sem["t3"], t30 + k - 3)
                                if idx == 0 and s == 0:
                                    e.wait_ge(p.sem["se"], se0 + SEH * h + 1)
                                aof = asl + ((1 + m) * 512 if diag else 0)
                                e.tensor_mul(
                                    sb["pr_sb"][:, (k % 4) * 512:
                                                (k % 4) * 512 + 512],
                                    sb["e_sb"][:, (k % 4) * 512:
                                               (k % 4) * 512 + 512],
                                    sb["a_sb"][:, aof:aof + 512]
                                ).then_inc(p.sem["sm"], 1)
                                p.bump("sm", 1)
                            elif diag:
                                e.wait_ge(p.sem["sx"], sx0 + k + 1)
                                prs = sb["pr_sb"][:, (k % 4) * 512:
                                                  (k % 4) * 512 + 512]
                                e.tensor_mul(prs, prs,
                                             sb["c_cmask"][:, m * 512:
                                                           (m + 1) * 512]
                                             ).then_inc(p.sem["sm"], 1)
                                p.bump("sm", 1)
                        e.wait_ge(p.sem["t3"], t30 + kend(q))
                        e.reciprocal(sb["r_rows"][0:1, (q % 2) * 512:
                                                  (q % 2) * 512 + 512],
                                     PS[1][0:1, :]).then_inc(p.sem["ve"], 1)
                        p.bump("ve", 1)
                        e.wait_ge(p.sem["se"], rbev(q))
                        if q >= 2 and (q - 2) in six:
                            p.wait_dma(e, "sync", six[q - 2])
                        e.tensor_tensor(
                            sb["co_sb"][:, (q % 2) * 512:(q % 2) * 512 + 512],
                            PS[0][:, :],
                            sb["rb_sb"][:, (q % 2) * 512:(q % 2) * 512 + 512],
                            OP.mult).then_inc(p.sem["ve"], 1)
                        p.bump("ve", 1)


def stage_ffn_gu(p: P):
    nc, sb, dr = p.nc, p.sb, p.dram
    PS = p.PS
    te0, ve0, se0 = (p.cnt[k] for k in ("te", "ve", "se"))

    def asl(kc, tt):
        return sb["xln"][:, kc * T + tt * 512: kc * T + tt * 512 + 512]

    wixg = {}
    wixu = {}
    six = {}
    with nc.Block() as block:
        @block.sync
        def _(e):
            qbarrier(p, e, own="sync")
            for ic in range(NIC):
                sl = ic % 2
                if ic >= 2:
                    p.qdrain(e, "sync")
                    e.wait_ge(p.sem["te"], te0 + 2 * (ic - 1))
                wixg[ic] = p.dma(e, "sync",
                                 sb["ffnblk"][:, sl * 4096:(sl + 1) * 4096],
                                 dr["wgS"][ic])
                wixu[ic] = p.dma(e, "sync",
                                 sb["ffnblk"][:, 8192 + sl * 4096:
                                              8192 + (sl + 1) * 4096],
                                 dr["wuS"][ic])
                if ic >= 1:
                    for tt in range(4):
                        n = (ic - 1) * 4 + tt
                        p.qdrain(e, "sync")
                        e.wait_ge(p.sem["ve"], ve0 + n + 1)
                        six[n] = p.dma(
                            e, "sync",
                            dr["hbuf"][128 * (ic - 1):128 * ic,
                                       tt * 512:(tt + 1) * 512],
                            sb["hstage"][:, (n % 2) * 512:(n % 2) * 512 + 512])
            for tt in range(4):
                n = (NIC - 1) * 4 + tt
                p.qdrain(e, "sync")
                e.wait_ge(p.sem["ve"], ve0 + n + 1)
                six[n] = p.dma(
                    e, "sync",
                    dr["hbuf"][128 * (NIC - 1):128 * NIC,
                               tt * 512:(tt + 1) * 512],
                    sb["hstage"][:, (n % 2) * 512:(n % 2) * 512 + 512])
            p.qdrain(e, "sync")

        @block.tensor
        def _(e):
            for ic in range(NIC):
                sl = ic % 2
                p.wait_dma(e, "sync", wixu[ic])
                if ic >= 1:
                    e.wait_ge(p.sem["se"], se0 + 4 * ic)
                for kc in range(NKC):
                    for tt in range(4):
                        ins = e.matmul(PS[tt][:, :],
                                       sb["ffnblk"][:, sl * 4096 + kc * 128:
                                                    sl * 4096 + kc * 128 + 128],
                                       asl(kc, tt), start=(kc == 0),
                                       stop=(kc == NKC - 1))
                ins.then_inc(p.sem["te"], 1)
                p.bump("te", 1)
                if ic >= 1:
                    e.wait_ge(p.sem["ve"], ve0 + 4 * ic)
                for kc in range(NKC):
                    for tt in range(4):
                        ins = e.matmul(PS[4 + tt][:, :],
                                       sb["ffnblk"][:, 8192 + sl * 4096 + kc * 128:
                                                    8192 + sl * 4096 + kc * 128 + 128],
                                       asl(kc, tt), start=(kc == 0),
                                       stop=(kc == NKC - 1))
                ins.then_inc(p.sem["te"], 1)
                p.bump("te", 1)

        @block.scalar
        def _(e):
            for ic in range(NIC):
                e.wait_ge(p.sem["te"], te0 + 2 * ic + 1)
                for tt in range(4):
                    if ic >= 1:
                        e.wait_ge(p.sem["ve"], ve0 + 4 * (ic - 1) + tt + 1)
                    e.activation(sb["obuf"][:, tt * 512:(tt + 1) * 512],
                                 PS[tt][:, :], AF.Silu).then_inc(p.sem["se"], 1)
                    p.bump("se", 1)

        @block.vector
        def _(e):
            for ic in range(NIC):
                e.wait_ge(p.sem["te"], te0 + 2 * ic + 2)
                for tt in range(4):
                    n = ic * 4 + tt
                    e.wait_ge(p.sem["se"], se0 + 4 * ic + tt + 1)
                    if n >= 2:
                        p.wait_dma(e, "sync", six[n - 2])
                    e.tensor_tensor(
                        sb["hstage"][:, (n % 2) * 512:(n % 2) * 512 + 512],
                        sb["obuf"][:, tt * 512:(tt + 1) * 512],
                        PS[4 + tt][:, :], OP.mult).then_inc(p.sem["ve"], 1)
                    p.bump("ve", 1)


def stage_ffn_down(p: P, xold, dst):
    nc, sb, dr = p.nc, p.sb, p.dram
    PS = p.PS
    te0, ve0 = p.cnt["te"], p.cnt["ve"]
    hix = {}
    wix = {}
    wq = {}
    xix = {}
    six = {}
    for G in range(16):
        for ic in range(NIC):
            n = G * NIC + ic
            wq[n] = "sync" if ic % 2 == 0 else "gp"
    with nc.Block() as block:
        @block.sync
        def _(e):
            qbarrier(p, e, own="sync")
            for tt in range(4):
                if tt >= 1:
                    p.qdrain(e, "sync")
                    e.wait_ge(p.sem["te"], te0 + NIC * 4 * (tt - 1) + 3 * NIC + 43)
                hix[(tt, 0)] = p.dma(
                    e, "sync",
                    bass.AP(sb["h_tt"], 0,
                            [[NIC * 512, 128], [512, 43], [1, 512]]),
                    bass.AP(dr["hbuf"], tt * 512,
                            [[T, 128], [128 * T, 43], [1, 512]]))
                for ncg in range(4):
                    G = tt * 4 + ncg
                    for ic in range(NIC):
                        n = G * NIC + ic
                        if wq[n] != "sync":
                            continue
                        sl = n % 8
                        if n >= 8:
                            p.qdrain(e, "sync")
                            e.wait_ge(p.sem["te"], te0 + n - 7)
                        wix[n] = p.dma(
                            e, "sync",
                            sb["wd_slab"][:, sl * 1024:(sl + 1) * 1024],
                            dr["wd_T"][128 * ic:128 * (ic + 1),
                                       1024 * ncg:1024 * (ncg + 1)])
                    for nc2 in range(10):
                        if nc2 < 8:
                            j = G * 8 + nc2
                            if j >= 2:
                                p.qdrain(e, "sync")
                                e.wait_ge(p.sem["ve"], ve0 + j - 1)
                            xix[j] = p.dma(
                                e, "sync",
                                sb["fbuf"][:, (j % 2) * 512:(j % 2) * 512 + 512],
                                dr[xold][1024 * ncg + 128 * nc2:
                                         1024 * ncg + 128 * (nc2 + 1),
                                         tt * 512:(tt + 1) * 512])
                        if nc2 >= 2:
                            js = G * 8 + nc2 - 2
                            p.qdrain(e, "sync")
                            e.wait_ge(p.sem["ve"], ve0 + js + 1)
                            six[js] = p.dma(
                                e, "sync",
                                dr[dst][1024 * ncg + 128 * (nc2 - 2):
                                        1024 * ncg + 128 * (nc2 - 1),
                                        tt * 512:(tt + 1) * 512],
                                sb["fbuf"][:, 1024 + (js % 2) * 512:
                                           1024 + (js % 2) * 512 + 512])
            p.qdrain(e, "sync")

        @block.gpsimd
        def _(e):
            p.qdrain(e, "gp")
            if te0:
                e.wait_ge(p.sem["te"], te0)
            for tt in range(4):
                if tt >= 1:
                    p.qdrain(e, "gp")
                    e.wait_ge(p.sem["te"], te0 + NIC * 4 * tt)
                hix[(tt, 1)] = p.dma(
                    e, "gp",
                    bass.AP(sb["h_tt"], 43 * 512,
                            [[NIC * 512, 128], [512, 43], [1, 512]]),
                    bass.AP(dr["hbuf"], 43 * 128 * T + tt * 512,
                            [[T, 128], [128 * T, 43], [1, 512]]))
                for ncg in range(4):
                    G = tt * 4 + ncg
                    for ic in range(NIC):
                        n = G * NIC + ic
                        if wq[n] != "gp":
                            continue
                        sl = n % 8
                        if n >= 8:
                            p.qdrain(e, "gp")
                            e.wait_ge(p.sem["te"], te0 + n - 7)
                        wix[n] = p.dma(
                            e, "gp",
                            sb["wd_slab"][:, sl * 1024:(sl + 1) * 1024],
                            dr["wd_T"][128 * ic:128 * (ic + 1),
                                       1024 * ncg:1024 * (ncg + 1)])
            p.qdrain(e, "gp")

        @block.tensor
        def _(e):
            for tt in range(4):
                for ncg in range(4):
                    G = tt * 4 + ncg
                    for ic in range(NIC):
                        n = G * NIC + ic
                        sl = n % 8
                        p.wait_dma(e, wq[n], wix[n])
                        if ic == 0:
                            p.wait_dma(e, "sync", hix[(tt, 0)])
                        if ic == 43:
                            p.wait_dma(e, "gp", hix[(tt, 1)])
                        for nc2 in range(8):
                            if ic == 0 and G >= 1:
                                e.wait_ge(p.sem["ve"],
                                          ve0 + 8 * (G - 1) + nc2 + 1)
                            ins = e.matmul(
                                PS[nc2][:, :],
                                sb["wd_slab"][:, sl * 1024 + nc2 * 128:
                                              sl * 1024 + nc2 * 128 + 128],
                                sb["h_tt"][:, ic * 512:(ic + 1) * 512],
                                start=(ic == 0), stop=(ic == NIC - 1))
                        ins.then_inc(p.sem["te"], 1)
                        p.bump("te", 1)

        @block.vector
        def _(e):
            for tt in range(4):
                for ncg in range(4):
                    G = tt * 4 + ncg
                    e.wait_ge(p.sem["te"], te0 + NIC * (G + 1))
                    for nc2 in range(8):
                        j = G * 8 + nc2
                        p.wait_dma(e, "sync", xix[j])
                        if j >= 2:
                            p.wait_dma(e, "sync", six[j - 2])
                        e.tensor_tensor(
                            sb["fbuf"][:, 1024 + (j % 2) * 512:
                                       1024 + (j % 2) * 512 + 512],
                            PS[nc2][:, :],
                            sb["fbuf"][:, (j % 2) * 512:(j % 2) * 512 + 512],
                            OP.add).then_inc(p.sem["ve"], 1)
                        p.bump("ve", 1)


def stage_final(p: P):
    nc = p.nc
    with nc.Block() as block:
        @block.sync
        def _(e):
            qbarrier(p, e, own="sync")
            if p.qn["gp"]:
                e.wait_ge(p.sem["gq"], 16 * p.qn["gp"])


def build_program(debug=(), upto=99):
    p = P(debug=debug)
    declare_io(p)
    alloc_sbuf(p)
    stages = [
        lambda: stage_init(p),
        lambda: stage_dequant(p, "gate_b", "gate_z", "gate_s", "wgS", FI, D, True),
        lambda: stage_dequant(p, "up_b", "up_z", "up_s", "wuS", FI, D, True),
        lambda: stage_dequant(p, "down_b", "down_z", "down_s", "wd_T", D, FI, False),
        lambda: stage_ln(p, "xin", tap="xln_dbg"),
        lambda: stage_proj(p, "wq_a", 32, "store", "qa"),
        lambda: stage_kv_a(p),
        lambda: stage_flagpoll(p, 0, [("sh_ka", "ka_loc"), ("sh_va", "va_loc")]),
        lambda: stage_attn(p, "a"),
        lambda: stage_load_arena(p, "ctxa"),
        lambda: stage_proj(p, "wo_a", 32, "resid", "x1", xold="xin"),
        lambda: stage_ln(p, "x1"),
        lambda: stage_ffn_gu(p),
        lambda: stage_ffn_down(p, "x1", "x2"),
        lambda: stage_ln(p, "x2"),
        lambda: stage_cossin(p),
        lambda: stage_proj(p, "wq_b", 32, "rope", "qb"),
        lambda: stage_proj(p, "wk_b", 8, "rope", "kb_loc"),
        lambda: stage_vb(p),
        lambda: stage_flagpoll(p, 1, [("sh_kb", "kb_loc"), ("sh_vb", "vb_loc")]),
        lambda: stage_attn(p, "b"),
        lambda: stage_load_arena(p, "ctxb"),
        lambda: stage_proj(p, "wo_b", 32, "resid", "x3", xold="x2"),
        lambda: stage_ln(p, "x3"),
        lambda: stage_ffn_gu(p),
        lambda: stage_ffn_down(p, "x3", "xout"),
    ]
    only = globals().get("ONLY_STAGES")
    for i, fn in enumerate(stages):
        if i >= upto:
            break
        if only is not None and i not in only:
            continue
        fn()
    stage_final(p)
    return p


def host_prepare(inputs, nonce):
    import ml_dtypes
    x = np.asarray(inputs["L0_attn_a_hidden_states"])

    def bf(a):
        return np.ascontiguousarray(a).astype(ml_dtypes.bfloat16)

    wq_a = np.ascontiguousarray(np.asarray(inputs["L0_attn_a_Wq"], np.float32))
    wkv_a = np.ascontiguousarray(np.asarray(inputs["L0_attn_a_Wkv"], np.float32))
    wo_a = np.ascontiguousarray(np.asarray(inputs["L0_attn_a_Wo"], np.float32))
    wq_b = np.ascontiguousarray(np.asarray(inputs["L0_attn_b_Wq"], np.float32).T)
    wk_b = np.ascontiguousarray(np.asarray(inputs["L0_attn_b_Wk"], np.float32).T)
    wv_b = np.ascontiguousarray(np.asarray(inputs["L0_attn_b_Wv"], np.float32).T)
    wo_b = np.ascontiguousarray(np.asarray(inputs["L0_attn_b_Wo"], np.float32).T)

    def u8view(a):
        a = np.ascontiguousarray(np.asarray(a, np.int32))
        return np.ascontiguousarray(
            a.view(np.uint8).reshape(a.shape[0], a.shape[1], 4)[:, :, 0])

    gate_b = u8view(inputs["L0_ffn_gate_Wq"])
    up_b = u8view(inputs["L0_ffn_up_Wq"])
    down_b = u8view(inputs["L0_ffn_down_Wq"])

    dk = np.arange(128)[:, None].astype(np.float32)
    dq = np.arange(512)[None, :].astype(np.float32)
    rel = dk - dq
    rel128 = bf(rel - 128.0)
    relmask = bf(np.stack([np.where(rel + 128 * m <= 0, rel + 128 * m, -1e9)
                           for m in range(4)]))
    cmask = bf(np.stack([(rel + 128 * m <= 0).astype(np.float32)
                         for m in range(4)]))
    smat = np.zeros((128, 128), np.float32)
    for m_ in range(64):
        smat[m_ + 64, m_] = -1.0
        smat[m_, m_ + 64] = 1.0
    smat = bf(smat)

    common = dict(
        wq_a=wq_a, wkv_a=wkv_a, wo_a=wo_a, wq_b=wq_b, wk_b=wk_b,
        wv_b=wv_b, wo_b=wo_b, gate_b=gate_b, up_b=up_b, down_b=down_b,
        gate_s=np.asarray(inputs["L0_ffn_gate_scale"], np.float32),
        gate_z=np.asarray(inputs["L0_ffn_gate_zero"], np.float32),
        up_s=np.asarray(inputs["L0_ffn_up_scale"], np.float32),
        up_z=np.asarray(inputs["L0_ffn_up_zero"], np.float32),
        down_s=np.asarray(inputs["L0_ffn_down_scale"], np.float32),
        down_z=np.asarray(inputs["L0_ffn_down_zero"], np.float32),
        rel128=rel128, relmask=relmask, cmask=cmask, smat=smat,
        ident=np.eye(128, dtype=np.float32),
        nonce=nonce,
    )

    in_maps = []
    shard_info = []
    for c in range(8):
        b, par = c // 2, c % 2
        toks = own_tokens(par)
        q0s = own_q0s(par)
        shard_info.append((b, toks))
        xc = np.ascontiguousarray(x[b][toks, :].T.astype(np.float32))
        offv = np.full((4, 32), NEGPAD, np.float32)
        offb = np.full((4, 32), NEG, np.float32)
        for s_ in range(4):
            nst = CS[s_] - 4
            for j in range(32):
                if j < nst and 128 * j < q0s[s_]:
                    offv[s_, j] = 128 * j - q0s[s_] + 128.0
                    offb[s_, j] = 0.0
        m = dict(common)
        m.update(
            xin=xc,
            offv_a=np.broadcast_to(offv.reshape(1, 128), (128, 128)).copy(),
            offv_b=np.broadcast_to(offb.reshape(1, 128), (128, 128)).copy(),
            pos=np.asarray(toks, np.float32).reshape(1, T),
        )
        in_maps.append(m)
    return in_maps, shard_info


_PROG_CACHE = {}


def kernel(**inputs):
    nonce = (np.random.randint(1, 2 ** 30, size=(1, 2)).astype(np.int32)
             + np.asarray([[0, 2 ** 30]], np.int32))
    in_maps, shard_info = host_prepare(inputs, nonce)
    key = (tuple(sorted(DEBUG)), UPTO)
    if key not in _PROG_CACHE:
        _PROG_CACHE[key] = build_program(debug=DEBUG, upto=UPTO)
    p = _PROG_CACHE[key]
    res = run_bass_kernel_spmd(p.nc, in_maps, core_ids=list(range(8)))
    out = np.zeros((4, 4096, 4096), np.float32)
    for c in range(8):
        b, toks = shard_info[c]
        out[b][np.asarray(toks), :] = np.asarray(
            res.results[c]["xout"], np.float32).T
    if DEBUG:
        kernel._debug_results = res.results
    return out


# version 05
# speedup vs baseline: 1.3902x; 1.0237x over previous
import math
from contextlib import ExitStack

import numpy as np

import concourse.bass as bass
import concourse.mybir as mybir
from concourse.bass_utils import run_bass_kernel_spmd

F32 = mybir.dt.float32
BF16 = mybir.dt.bfloat16
I32 = mybir.dt.int32
U8 = mybir.dt.uint8
AF = mybir.ActivationFunctionType
OP = mybir.AluOpType

D = 4096
T = 2048
NKC = 32
NTC = 16
FI = 11008
NIC = 86
GRP = 64
ISCALE = 1.0 / math.sqrt(128.0)
NEG = -1.0e30
NEGPAD = -1.0e35
CS = [12, 16, 28, 32]
_AB = 2.0 ** (-(2.0 ** -(math.log2(32) - 3)))
SLOPES = [_AB ** (i + 1) for i in range(32)]
LN_EPS = 1e-5

DEBUG = ()
UPTO = 99
TRANSPOSE_DOWN = True


def own_tokens(parity):
    if parity == 0:
        return list(range(0, 1024)) + list(range(3072, 4096))
    return list(range(1024, 3072))


def own_q0s(parity):
    return [0, 512, 3072, 3584] if parity == 0 else [1024, 1536, 2048, 2560]


def pos_block_map(g):
    if g in (0, 1):
        return (0, 512 * g)
    if g in (2, 3, 4, 5):
        return (1, 512 * (g - 2))
    return (0, 1024 + 512 * (g - 6))


class P:

    def __init__(self, debug=()):
        self.debug = set(debug)
        self.nc = bass.Bass("TRN2", target_bir_lowering=False,
                            detect_race_conditions=False)
        self.ctx = ExitStack()
        self.sem = {}
        self.cnt = {}
        for name in ["sq", "sqp", "gq", "gqp", "te", "ve", "se", "t1", "t3", "sx", "sm", "pf", "pd"]:
            self.sem[name] = self.ctx.enter_context(self.nc.semaphore(name))
            self.cnt[name] = 0
        self.sb = {}
        self.dram = {}
        self.pspar = 0
        self.par_snap = {}
        self.qn = {"sync": 0, "gp": 0}
        self.qpend = {"sync": [], "gp": []}
        self.qpub = {"sync": 0, "gp": 0}
        self.qdepth = 6
        self.poll_done = 0

    def dma(self, e, which, out, in_, transpose=False):
        raw = self.sem["sq" if which == "sync" else "gq"]
        if transpose:
            e.dma_start_transpose(out, in_).then_inc(raw, 16)
        else:
            e.dma_start(out=out, in_=in_).then_inc(raw, 16)
        idx = self.qn[which]
        self.qn[which] += 1
        self.qpend[which].append(idx)
        if len(self.qpend[which]) > self.qdepth:
            self._flush(e, which)
        return idx

    def _flush(self, e, which):
        if not self.qpend[which]:
            return
        raw = self.sem["sq" if which == "sync" else "gq"]
        pub = self.sem["sqp" if which == "sync" else "gqp"]
        e.wait_ge(raw, 16 * self.qn[which])
        e.sem_inc(pub, len(self.qpend[which]))
        self.qpub[which] = self.qn[which]
        self.qpend[which].clear()

    def qdrain(self, e, which):
        self._flush(e, which)

    def wait_dma(self, e, which, idx):
        e.wait_ge(self.sem["sqp" if which == "sync" else "gqp"], idx + 1)

    def bump(self, name, amt=1):
        self.cnt[name] += amt
        return self.cnt[name]

    def din(self, name, shape, dtype):
        self.dram[name] = self.nc.dram_tensor(name, shape, dtype,
                                              kind="ExternalInput")
        return self.dram[name]

    def dout(self, name, shape, dtype):
        self.dram[name] = self.nc.dram_tensor(name, shape, dtype,
                                              kind="ExternalOutput")
        return self.dram[name]

    def dtmp(self, name, shape, dtype, shared=False):
        kind = "ExternalOutput" if name in self.debug else "Internal"
        self.dram[name] = self.nc.dram_tensor(
            name, shape, dtype, kind=kind,
            addr_space="Shared" if shared else "Local")
        return self.dram[name]

    def sbuf(self, name, shape, dtype):
        self.sb[name] = self.ctx.enter_context(
            self.nc.sbuf_tensor(name, shape, dtype))
        return self.sb[name]

    def alias(self, name, shape, dtype, byte_off):
        base = self.nc.lookup_mloc(self.sb["big"]).addr
        self.sb[name] = self.nc.alloc_sbuf_tensor_at(
            name, shape, dtype, offset=base + byte_off)
        return self.sb[name]


def declare_io(p: P):
    p.din("xin", [D, T], F32)
    p.din("wq_a", [D, D], F32)
    p.din("wkv_a", [D, 256], F32)
    p.din("wo_a", [D, D], F32)
    p.din("wq_b", [D, D], F32)
    p.din("wk_b", [D, 1024], F32)
    p.din("wv_b", [D, 1024], F32)
    p.din("wo_b", [D, D], F32)
    p.din("gate_b", [FI, D // 2], U8)
    p.din("up_b", [FI, D // 2], U8)
    p.din("down_b", [D, FI // 2], U8)
    p.din("gate_s", [FI, D // GRP], F32)
    p.din("gate_z", [FI, D // GRP], F32)
    p.din("up_s", [FI, D // GRP], F32)
    p.din("up_z", [FI, D // GRP], F32)
    p.din("down_s", [D, FI // GRP], F32)
    p.din("down_z", [D, FI // GRP], F32)
    p.din("rel128", [128, 512], BF16)
    p.din("relmask", [4, 128, 512], BF16)
    p.din("cmask", [4, 128, 512], BF16)
    p.din("smat", [128, 128], BF16)
    p.din("ident", [128, 128], F32)
    p.din("offv_a", [128, 128], F32)
    p.din("offv_b", [128, 128], F32)
    p.din("pos", [1, T], F32)
    p.din("nonce", [1, 2], I32)

    p.dout("xout", [D, T], F32)

    p.dtmp("x1", [D, T], F32)
    p.dtmp("x2", [D, T], F32)
    p.dtmp("x3", [D, T], F32)
    p.dtmp("wd_T", [FI, D], BF16)
    p.dtmp("wgS", [NIC, 128, 4096], BF16)
    p.dtmp("wuS", [NIC, 128, 4096], BF16)
    p.dtmp("qa", [D, T], BF16)
    p.dtmp("qb", [D, T], BF16)
    p.dtmp("ctxa", [D, T], BF16)
    p.dtmp("ctxb", [D, T], BF16)
    p.dtmp("hbuf", [FI, T], BF16)
    p.dtmp("ka_loc", [128, T], BF16)
    p.dtmp("va_loc", [T, 128], BF16)
    p.dtmp("kb_loc", [1024, T], BF16)
    p.dtmp("vb_loc", [T, 1024], BF16)
    p.dtmp("sh_ka", [2, 128, T], BF16, shared=True)
    p.dtmp("sh_va", [2, T, 128], BF16, shared=True)
    p.dtmp("sh_kb", [2, 1024, T], BF16, shared=True)
    p.dtmp("sh_vb", [2, T, 1024], BF16, shared=True)
    p.dtmp("sh_flag", [2, 2], I32, shared=True)
    if "xln_dbg" in p.debug:
        p.dtmp("xln_dbg", [D, T], BF16)


def alloc_sbuf(p: P):
    p.sbuf("c_rel128", [128, 512], BF16)
    p.sbuf("c_relmask", [128, 4 * 512], BF16)
    p.sbuf("c_cmask", [128, 4 * 512], BF16)
    p.sbuf("c_smat", [128, 128], BF16)
    p.sbuf("c_ident", [128, 128], F32)
    p.sbuf("c_offa", [128, 128], F32)
    p.sbuf("c_offb", [128, 128], F32)

    p.sbuf("c_nonce", [1, 2], I32)
    p.sbuf("c_ones_bf", [128, 1], BF16)
    p.sbuf("c_ones_f", [1, 128], F32)
    p.sbuf("c_invf", [1, 128], F32)
    p.sbuf("c_bias", [128, 64], F32)
    p.sbuf("c_zs", [128, 384], F32)
    p.sbuf("r_rows", [1, 2048], F32)
    p.sbuf("r_poll", [1, 2], I32)
    p.sbuf("i512", [128, 512], I32)
    p.sbuf("fbuf", [128, 2048], F32)
    p.sbuf("obuf", [128, 4096], BF16)
    p.sbuf("hstage", [128, 1024], BF16)
    p.sbuf("kvstage", [128, 1024], BF16)
    p.sbuf("big", [128, 83968], BF16)
    rem = p.nc.sbuf_bytes_remaining
    assert rem >= 0, f"sbuf over budget by {-rem}"

    p.alias("dq_b32", [128, 2 * 2752], I32, 0)
    p.alias("dq_nib", [128, 2752], I32, 22016)
    p.alias("dq_d1", [128, 5504], F32, 33024)
    p.alias("dq_wf", [128, 2 * 5504], F32, 55040)
    p.alias("tstage", [128, 2 * 5504], BF16, 99072)
    p.alias("xln", [128, NKC * T], BF16, 0)
    p.alias("wslab", [128, 2 * 4096], BF16, 131072)
    p.alias("wkv_sb", [128, 8192], BF16, 131072)
    p.alias("ffnblk", [128, 4 * 4096], BF16, 131072)
    p.alias("cs_fm", [128, 2 * T], BF16, 147456)
    p.alias("c_pos", [1, T], F32, 155648)
    p.alias("k_sb", [128, 4096], BF16, 0)
    p.alias("v_sb", [128, 4096], BF16, 8192)
    p.alias("q_sb", [128, 2 * 2048], BF16, 16384)
    p.alias("e_sb", [128, 4 * 512], BF16, 24576)
    p.alias("pr_sb", [128, 4 * 512], BF16, 28672)
    p.alias("a_sb", [128, 2 * 5 * 512], BF16, 32768)
    p.alias("co_sb", [128, 2 * 512], BF16, 43008)
    p.alias("rb_sb", [128, 2 * 512], F32, 45056)
    p.alias("kdiag", [128, 4 * 512], BF16, 49152)
    p.alias("vdiag", [128, 4 * 512], BF16, 53248)
    p.alias("h_tt", [128, NIC * 512], BF16, 0)
    p.alias("wd_slab", [128, 8 * 1024], BF16, 90112)
    p.PS = [p.ctx.enter_context(p.nc.psum_tensor(f"P{i}", [128, 512], F32))
            for i in range(8)]




def _set_parity(p: P, e, tag):
    r = e.alloc_register(f"par_{tag}")
    ro = e.alloc_register(f"opar_{tag}")
    for arm in e.switch_core_id(128):
        e.reg_mov(r, int(arm) & 1)
    e.reg_mov(ro, 1)
    e.reg_alu(ro, ro, r, OP.subtract)
    p.par_snap[tag] = e.snap(r, min_val=0, max_val=1)
    if not hasattr(p, "opar_snap"):
        p.opar_snap = {}
    p.opar_snap[tag] = e.snap(ro, min_val=0, max_val=1)


def qbarrier(p: P, e, sems=("te", "ve", "se"), own=None):
    if own is not None:
        p.qdrain(e, own)
    if p.qn["sync"]:
        e.wait_ge(p.sem["sqp"], p.qn["sync"])
    if p.qn["gp"]:
        e.wait_ge(p.sem["gqp"], p.qn["gp"])
    for s_ in sems:
        if p.cnt[s_]:
            e.wait_ge(p.sem[s_], p.cnt[s_])


def stage_init(p: P):
    nc, sb, dr = p.nc, p.sb, p.dram
    with nc.Block() as block:
        @block.sync
        def _(e):
            pairs = [
                (sb["c_rel128"][:, :], dr["rel128"][:, :]),
                (bass.AP(sb["c_relmask"], 0, [[2048, 128], [512, 4], [1, 512]]),
                 bass.AP(dr["relmask"], 0, [[512, 128], [128 * 512, 4], [1, 512]])),
                (bass.AP(sb["c_cmask"], 0, [[2048, 128], [512, 4], [1, 512]]),
                 bass.AP(dr["cmask"], 0, [[512, 128], [128 * 512, 4], [1, 512]])),
                (sb["c_smat"][:, :], dr["smat"][:, :]),
                (sb["c_ident"][:, :], dr["ident"][:, :]),
                (sb["c_offa"][:, :], dr["offv_a"][:, :]),
                (sb["c_offb"][:, :], dr["offv_b"][:, :]),
                (sb["c_nonce"][:, :], dr["nonce"][:, :]),
            ]
            for o, i in pairs:
                p.dma(e, "sync", o, i)
            p.qdrain(e, "sync")
            _set_parity(p, e, "sync")

        @block.gpsimd
        def _(e):
            e.memset(sb["c_ones_bf"][:, :], 1.0)
            e.memset(sb["c_ones_f"][:, :], 1.0)
            e.memset(sb["r_poll"][:, :], 0).then_inc(p.sem["sm"], 1)
            p.bump("sm", 1)
            _set_parity(p, e, "gp")
            e.wait_ge(p.sem["sm"], p.cnt["sm"])
            e.dma_start(out=dr["sh_flag"][bass.ds(p.par_snap["gp"], 1), 0:2],
                        in_=sb["r_poll"][0:1, 0:2]).then_inc(p.sem["pf"], 16)


def stage_dequant(p: P, bname, zname, sname, oname, nout, nin, blocks):
    nc, sb, dr = p.nc, p.sb, p.dram
    PS = p.PS
    noc = nout // 128
    passes = [(k * 5504, min(5504, nin - k * 5504))
              for k in range((nin + 5503) // 5504)]
    jobs = [(ic, off, w) for ic in range(noc) for (off, w) in passes]
    ntl = [w // 128 for (_, _, w) in jobs]
    cumt = [0]
    for t_ in ntl:
        cumt.append(cumt[-1] + t_)
    ve0, te0, se0 = p.cnt["ve"], p.cnt["te"], p.cnt["se"]
    ldix = {}
    stix = {}
    with nc.Block() as block:
        @block.gpsimd
        def _(e):
            qbarrier(p, e, own="gp")
            for n, (ic, off, w) in enumerate(jobs):
                if n >= 2:
                    p.qdrain(e, "gp")
                    e.wait_ge(p.sem["ve"], ve0 + (n - 1))
                sl = n % 2
                ng = w // GRP
                p.dma(e, "gp",
                      sb["dq_b32"][:, sl * 2752: sl * 2752 + w // 2],
                      dr[bname][128 * ic:128 * (ic + 1),
                                off // 2: off // 2 + w // 2])
                p.dma(e, "gp",
                      sb["c_zs"][:, sl * 192: sl * 192 + ng],
                      dr[zname][128 * ic:128 * (ic + 1),
                                off // GRP: off // GRP + ng])
                ldix[n] = p.dma(
                    e, "gp",
                    sb["c_zs"][:, sl * 192 + 96: sl * 192 + 96 + ng],
                    dr[sname][128 * ic:128 * (ic + 1),
                              off // GRP: off // GRP + ng])
            p.qdrain(e, "gp")

        @block.sync
        def _(e):
            for n, (ic, off, w) in enumerate(jobs):
                sl = n % 2
                e.wait_ge(p.sem["se"], se0 + cumt[n + 1])
                if blocks:
                    stix[n] = p.dma(
                        e, "sync",
                        dr[oname][ic],
                        sb["tstage"][:, sl * 5504: sl * 5504 + w])
                else:
                    stix[n] = p.dma(
                        e, "sync",
                        bass.AP(dr[oname], off * nout + 128 * ic,
                                [[nout, 128], [128 * nout, w // 128],
                                 [1, 128]]),
                        bass.AP(sb["tstage"], sl * 5504,
                                [[2 * 5504, 128], [128, w // 128], [1, 128]]))
                p.qdrain(e, "sync")
            p.qdrain(e, "sync")

        @block.tensor
        def _(e):
            for n, (ic, off, w) in enumerate(jobs):
                sl = n % 2
                e.wait_ge(p.sem["ve"], ve0 + (n + 1))
                for kt in range(w // 128):
                    g = cumt[n] + kt
                    if g >= 8:
                        e.wait_ge(p.sem["se"], se0 + g - 7)
                    e.transpose(
                        PS[g % 8][:, 0:128],
                        sb["dq_wf"][:, sl * 5504 + kt * 128:
                                    sl * 5504 + kt * 128 + 128],
                        sb["c_ident"][:, :]).then_inc(p.sem["te"], 1)
                    p.bump("te", 1)

        @block.scalar
        def _(e):
            for n, (ic, off, w) in enumerate(jobs):
                sl = n % 2
                if n >= 2:
                    p.wait_dma(e, "sync", stix[n - 2])
                for kt in range(w // 128):
                    g = cumt[n] + kt
                    e.wait_ge(p.sem["te"], te0 + g + 1)
                    e.copy(sb["tstage"][:, sl * 5504 + kt * 128:
                                        sl * 5504 + kt * 128 + 128],
                           PS[g % 8][:, 0:128]).then_inc(p.sem["se"], 1)
                    p.bump("se", 1)

        @block.vector
        def _(e):
            for n, (ic, off, w) in enumerate(jobs):
                sl = n % 2
                ng = w // GRP
                p.wait_dma(e, "gp", ldix[n])
                if n >= 2:
                    e.wait_ge(p.sem["te"], te0 + cumt[n - 1])
                bt = sb["dq_b32"][:, sl * 2752: sl * 2752 + w // 2]
                zb = bass.AP(sb["c_zs"], sl * 192, [[384, 128], [1, ng], [0, 32]])
                sc = bass.AP(sb["c_zs"], sl * 192 + 96,
                             [[384, 128], [1, ng], [0, 32]])
                nib = sb["dq_nib"][:, 0:w // 2]
                for half, op_, ofs in ((0, OP.logical_shift_right, 4),
                                       (1, OP.bitwise_and, 15)):
                    e.tensor_scalar(nib, bt, ofs, None, op_)
                    d1 = bass.AP(sb["dq_d1"], half, [[5504, 128], [GRP, ng], [2, 32]])
                    n3 = bass.AP(sb["dq_nib"], 0, [[2752, 128], [32, ng], [1, 32]])
                    e.tensor_tensor(d1, n3, zb, OP.subtract)
                    wq_ = bass.AP(sb["dq_wf"], sl * 5504 + half,
                                  [[11008, 128], [GRP, ng], [2, 32]])
                    ins = e.tensor_tensor(wq_, d1, sc, OP.mult)
                ins.then_inc(p.sem["ve"], 1)
                p.bump("ve", 1)


def stage_ln(p: P, srcname, tap=None):
    nc, sb, dr = p.nc, p.sb, p.dram
    PS = p.PS

    def xsl(kc, tt):
        return sb["xln"][:, kc * T + tt * 512: kc * T + tt * 512 + 512]

    ldix = {}
    for tt in range(4):
        te0, ve0, se0 = (p.cnt[k] for k in ("te", "ve", "se"))
        with nc.Block() as block:
            @block.gpsimd
            def _(e):
                if tt == 0:
                    qbarrier(p, e, own="gp")
                ldix[tt] = p.dma(
                    e, "gp",
                    bass.AP(sb["xln"], tt * 512,
                            [[NKC * T, 128], [T, NKC], [1, 512]]),
                    bass.AP(dr[srcname], tt * 512,
                            [[T, 128], [128 * T, NKC], [1, 512]]))
                p.qdrain(e, "gp")

            @block.tensor
            def _(e):
                p.wait_dma(e, "gp", ldix[tt])
                for kc in range(NKC):
                    ins = e.matmul(PS[6][0:1, :], sb["c_ones_bf"][:, :],
                                   xsl(kc, tt), start=(kc == 0),
                                   stop=(kc == NKC - 1))
                ins.then_inc(p.sem["te"], 1)
                p.bump("te", 1)
                for kc in range(NKC):
                    e.wait_ge(p.sem["ve"], ve0 + kc + 1)
                    e.matmul(PS[7][0:1, :], sb["c_ones_bf"][:, :],
                             sb["obuf"][:, (kc % 2) * 512:(kc % 2) * 512 + 512],
                             start=(kc == 0), stop=(kc == NKC - 1)
                             ).then_inc(p.sem["te"], 1)
                    p.bump("te", 1)
                e.wait_ge(p.sem["ve"], ve0 + NKC + 2)
                e.matmul(PS[6][:, :], sb["c_ones_f"][:, :],
                         sb["r_rows"][0:1, 512:1024], start=True,
                         stop=True).then_inc(p.sem["te"], 1)
                e.matmul(PS[7][:, :], sb["c_ones_f"][:, :],
                         sb["r_rows"][0:1, 0:512], start=True,
                         stop=True).then_inc(p.sem["te"], 1)
                p.bump("te", 2)

            @block.vector
            def _(e):
                p.wait_dma(e, "gp", ldix[tt])
                for kc in range(NKC):
                    if kc >= 2:
                        e.wait_ge(p.sem["te"], te0 + 1 + (kc - 1))
                    e.tensor_mul(
                        sb["obuf"][:, (kc % 2) * 512:(kc % 2) * 512 + 512],
                        xsl(kc, tt), xsl(kc, tt)).then_inc(p.sem["ve"], 1)
                    p.bump("ve", 1)
                r = sb["r_rows"]
                e.wait_ge(p.sem["te"], te0 + 1 + NKC)
                e.tensor_scalar(r[0:1, 1024:1536], PS[6][0:1, :], 1.0 / D,
                                None, OP.mult)
                e.tensor_scalar(r[0:1, 1536:2048], PS[7][0:1, :], 1.0 / D,
                                None, OP.mult)
                e.tensor_mul(r[0:1, 0:512], r[0:1, 1024:1536],
                             r[0:1, 1024:1536])
                e.tensor_sub(r[0:1, 1536:2048], r[0:1, 1536:2048],
                             r[0:1, 0:512])
                e.tensor_scalar(r[0:1, 1536:2048], r[0:1, 1536:2048],
                                LN_EPS, None, OP.add).then_inc(p.sem["ve"], 1)
                p.bump("ve", 1)
                e.wait_ge(p.sem["se"], se0 + 1)
                e.reciprocal(r[0:1, 512:1024], r[0:1, 0:512])
                e.scalar_tensor_tensor(
                    r[0:1, 0:512], r[0:1, 1024:1536], -1.0,
                    r[0:1, 512:1024], OP.mult, OP.mult
                ).then_inc(p.sem["ve"], 1)
                p.bump("ve", 1)
                e.wait_ge(p.sem["se"], se0 + 3)
                for kc in range(NKC):
                    e.tensor_mul(sb["fbuf"][:, 0:512], xsl(kc, tt),
                                 sb["fbuf"][:, 1024:1536])
                    ins = e.tensor_add(xsl(kc, tt), sb["fbuf"][:, 0:512],
                                       sb["fbuf"][:, 1536:2048])
                ins.then_inc(p.sem["ve"], 1)
                p.bump("ve", 1)

            @block.scalar
            def _(e):
                r = sb["r_rows"]
                e.wait_ge(p.sem["ve"], ve0 + NKC + 1)
                e.activation(r[0:1, 0:512], r[0:1, 1536:2048],
                             AF.Sqrt).then_inc(p.sem["se"], 1)
                p.bump("se", 1)
                e.wait_ge(p.sem["te"], te0 + NKC + 3)
                e.copy(sb["fbuf"][:, 1024:1536], PS[6][:, :]).then_inc(
                    p.sem["se"], 1)
                e.copy(sb["fbuf"][:, 1536:2048], PS[7][:, :]).then_inc(
                    p.sem["se"], 1)
                p.bump("se", 2)

    if tap is not None and tap in p.debug:
        ve_done = p.cnt["ve"]
        with nc.Block() as block:
            @block.sync
            def _(e):
                e.wait_ge(p.sem["ve"], ve_done)
                p.dma(e, "sync",
                      bass.AP(dr[tap], 0, [[T, 128], [128 * T, NKC], [1, T]]),
                      sb["xln"][:, :])
                p.qdrain(e, "sync")


def stage_load_arena(p: P, srcname):
    nc, sb, dr = p.nc, p.sb, p.dram
    qn0s = p.qn["sync"]
    c0 = {k: p.cnt[k] for k in ("te", "ve", "se")}
    with nc.Block() as block:
        @block.sync
        def _(e):
            qbarrier(p, e, own="sync")
            for q in range(2):
                p.dma(e, "sync",
                      bass.AP(sb["xln"], q * 8 * T,
                              [[NKC * T, 128], [T, 8], [1, T]]),
                      bass.AP(dr[srcname], q * 8 * 128 * T,
                              [[T, 128], [128 * T, 8], [1, T]]))
            p.qdrain(e, "sync")

        @block.gpsimd
        def _(e):
            p.qdrain(e, "gp")
            if qn0s:
                e.wait_ge(p.sem["sqp"], qn0s)
            for s_ in ("te", "ve", "se"):
                if c0[s_]:
                    e.wait_ge(p.sem[s_], c0[s_])
            for q in range(2, 4):
                p.dma(e, "gp",
                      bass.AP(sb["xln"], q * 8 * T,
                              [[NKC * T, 128], [T, 8], [1, T]]),
                      bass.AP(dr[srcname], q * 8 * 128 * T,
                              [[T, 128], [128 * T, 8], [1, T]]))
            p.qdrain(e, "gp")
    p.arena_ready = p.qn["sync"]


def stage_proj(p: P, wname, ngroups, mode, dst, xold=None):
    nc, sb, dr = p.nc, p.sb, p.dram
    PS = p.PS
    te0, ve0, se0 = (p.cnt[k] for k in ("te", "ve", "se"))

    def asl(kc, tt):
        return sb["xln"][:, kc * T + tt * 512: kc * T + tt * 512 + 512]

    tepg = 2 if mode == "rope" else 1
    vepg = 8 if mode == "rope" else 4
    wix = {}
    xix = {}
    six = {}
    with nc.Block() as block:
        @block.gpsimd
        def _(e):
            qbarrier(p, e, own="gp")
            for ng in range(ngroups):
                sl = ng % 2
                if ng >= 2:
                    p.qdrain(e, "gp")
                    e.wait_ge(p.sem["te"], te0 + tepg * (ng - 1))
                wix[ng] = p.dma(
                    e, "gp",
                    sb["wslab"][:, sl * 4096:(sl + 1) * 4096],
                    bass.AP(dr[wname], 128 * ng,
                            [[dr[wname].shape[1], 128],
                             [128 * dr[wname].shape[1], NKC], [1, 128]]))
            p.qdrain(e, "gp")

        @block.sync
        def _(e):
            for ng in range(ngroups):
                if mode == "resid":
                    for tt in range(6):
                        if tt < 4:
                            n = ng * 4 + tt
                            if n >= 2:
                                p.qdrain(e, "sync")
                                e.wait_ge(p.sem["ve"], ve0 + 4 * ng + tt - 1)
                            xix[n] = p.dma(
                                e, "sync",
                                sb["fbuf"][:, (n % 2) * 512:(n % 2) * 512 + 512],
                                dr[xold][128 * ng:128 * (ng + 1),
                                         tt * 512:(tt + 1) * 512])
                        if tt >= 2:
                            ns = ng * 4 + tt - 2
                            p.qdrain(e, "sync")
                            e.wait_ge(p.sem["ve"], ve0 + ns + 1 - 4 * ng + 4 * ng)
                            six[ns] = p.dma(
                                e, "sync",
                                dr[dst][128 * ng:128 * (ng + 1),
                                        (tt - 2) * 512:(tt - 1) * 512],
                                sb["fbuf"][:, 1024 + (ns % 2) * 512:
                                           1024 + (ns % 2) * 512 + 512])
                else:
                    p.qdrain(e, "sync")
                    e.wait_ge(p.sem["ve"], ve0 + vepg * ng + vepg)
                    if mode == "store":
                        osl = (ng % 2) * 2048
                        six[ng] = p.dma(
                            e, "sync",
                            dr[dst][128 * ng:128 * (ng + 1), :],
                            sb["obuf"][:, osl:osl + 2048])
                    else:
                        six[ng] = p.dma(
                            e, "sync",
                            dr[dst][128 * ng:128 * (ng + 1), :],
                            sb["obuf"][:, 2048:4096])
            p.qdrain(e, "sync")

        @block.tensor
        def _(e):
            for ng in range(ngroups):
                sl = ng % 2
                q = (ng % 2) * 4 if mode != "rope" else 0
                p.wait_dma(e, "gp", wix[ng])
                if (mode != "rope" and ng >= 2) or (mode == "rope" and ng >= 1):
                    back = 1 if mode == "rope" else 2
                    e.wait_ge(p.sem["ve"], ve0 + vepg * (ng - back) + vepg)
                for kc in range(NKC):
                    for tt in range(4):
                        ins = e.matmul(PS[q + tt][:, :],
                                       sb["wslab"][:, sl * 4096 + kc * 128:
                                                   sl * 4096 + kc * 128 + 128],
                                       asl(kc, tt), start=(kc == 0),
                                       stop=(kc == NKC - 1))
                ins.then_inc(p.sem["te"], 1)
                p.bump("te", 1)
                if mode == "rope":
                    e.wait_ge(p.sem["ve"], ve0 + vepg * ng + 4)
                    for tt in range(4):
                        ins = e.matmul(PS[4 + tt][:, :], sb["c_smat"][:, :],
                                       sb["obuf"][:, tt * 512:(tt + 1) * 512],
                                       start=True, stop=True)
                    ins.then_inc(p.sem["te"], 1)
                    p.bump("te", 1)

        @block.vector
        def _(e):
            for ng in range(ngroups):
                q = (ng % 2) * 4 if mode != "rope" else 0
                if mode == "store":
                    e.wait_ge(p.sem["te"], te0 + tepg * ng + 1)
                    if ng >= 2:
                        p.wait_dma(e, "sync", six[ng - 2])
                    osl = (ng % 2) * 2048
                    for tt in range(4):
                        e.tensor_copy(
                            sb["obuf"][:, osl + tt * 512: osl + tt * 512 + 512],
                            PS[q + tt][:, :]).then_inc(p.sem["ve"], 1)
                        p.bump("ve", 1)
                elif mode == "resid":
                    e.wait_ge(p.sem["te"], te0 + tepg * ng + 1)
                    for tt in range(4):
                        n = ng * 4 + tt
                        p.wait_dma(e, "sync", xix[n])
                        if n >= 2:
                            p.wait_dma(e, "sync", six[n - 2])
                        fs = (n % 2) * 512
                        e.tensor_tensor(
                            sb["fbuf"][:, 1024 + fs:1024 + fs + 512],
                            PS[q + tt][:, :], sb["fbuf"][:, fs:fs + 512],
                            OP.add).then_inc(p.sem["ve"], 1)
                        p.bump("ve", 1)
                else:
                    e.wait_ge(p.sem["te"], te0 + tepg * ng + 1)
                    if ng >= 1:
                        p.wait_dma(e, "sync", six[ng - 1])
                    for tt in range(4):
                        e.tensor_copy(
                            sb["obuf"][:, tt * 512: tt * 512 + 512],
                            PS[tt][:, :]).then_inc(p.sem["ve"], 1)
                        p.bump("ve", 1)
                    e.wait_ge(p.sem["te"], te0 + tepg * ng + 2)
                    for tt in range(4):
                        csl = sb["cs_fm"][:, tt * 512: tt * 512 + 512]
                        ssl = sb["cs_fm"][:, T + tt * 512: T + tt * 512 + 512]
                        e.tensor_mul(sb["fbuf"][:, 0:512],
                                     sb["obuf"][:, tt * 512: tt * 512 + 512],
                                     csl)
                        e.tensor_mul(sb["fbuf"][:, 512:1024],
                                     PS[4 + tt][:, :], ssl)
                        e.tensor_add(
                            sb["obuf"][:, 2048 + tt * 512: 2048 + tt * 512 + 512],
                            sb["fbuf"][:, 0:512],
                            sb["fbuf"][:, 512:1024]).then_inc(p.sem["ve"], 1)
                        p.bump("ve", 1)


def stage_kv_a(p: P):
    nc, sb, dr = p.nc, p.sb, p.dram
    PS = p.PS
    te0, ve0 = p.cnt["te"], p.cnt["ve"]
    wix = {}
    six = {}
    with nc.Block() as block:
        @block.gpsimd
        def _(e):
            qbarrier(p, e, own="gp")
            wix[0] = p.dma(e, "gp", sb["wkv_sb"][:, 0:4096],
                           bass.AP(dr["wkv_a"], 0,
                                   [[256, 128], [128 * 256, NKC], [1, 128]]))
            wix[1] = p.dma(e, "gp", sb["wkv_sb"][:, 4096:8192],
                           bass.AP(dr["wkv_a"], 128,
                                   [[256, 128], [128 * 256, NKC], [1, 128]]))
            p.qdrain(e, "gp")

        @block.sync
        def _(e):
            p.qdrain(e, "sync")
            e.wait_ge(p.sem["ve"], ve0 + 4)
            six[0] = p.dma(e, "sync", dr["ka_loc"][:, :],
                           sb["obuf"][:, 0:2048])
            for tcg in range(2):
                p.qdrain(e, "sync")
                e.wait_ge(p.sem["ve"], ve0 + 4 + 8 * (tcg + 1))
                six[1 + tcg] = p.dma(
                    e, "sync",
                    bass.AP(dr["va_loc"], tcg * 1024 * 128,
                            [[128, 128], [128 * 128, 8], [1, 128]]),
                    bass.AP(sb["kvstage"], 0,
                            [[1024, 128], [128, 8], [1, 128]]))
            p.qdrain(e, "sync")

        @block.tensor
        def _(e):
            p.wait_dma(e, "gp", wix[1])
            for kc in range(NKC):
                for tt in range(4):
                    ins = e.matmul(PS[tt][:, :],
                                   sb["wkv_sb"][:, kc * 128: kc * 128 + 128],
                                   sb["xln"][:, kc * T + tt * 512:
                                             kc * T + tt * 512 + 512],
                                   start=(kc == 0), stop=(kc == NKC - 1))
            ins.then_inc(p.sem["te"], 1)
            p.bump("te", 1)
            for tcg in range(2):
                e.wait_ge(p.sem["ve"], ve0 + 4 + 8 * tcg)
                for kc in range(NKC):
                    for t8 in range(8):
                        tc = tcg * 8 + t8
                        ins = e.matmul(
                            PS[t8][:, 0:128],
                            sb["xln"][:, kc * T + tc * 128: kc * T + tc * 128 + 128],
                            sb["wkv_sb"][:, 4096 + kc * 128: 4096 + kc * 128 + 128],
                            start=(kc == 0), stop=(kc == NKC - 1))
                ins.then_inc(p.sem["te"], 1)
                p.bump("te", 1)

        @block.vector
        def _(e):
            e.wait_ge(p.sem["te"], te0 + 1)
            for tt in range(4):
                e.tensor_copy(sb["obuf"][:, tt * 512:(tt + 1) * 512],
                              PS[tt][:, :]).then_inc(p.sem["ve"], 1)
                p.bump("ve", 1)
            for tcg in range(2):
                e.wait_ge(p.sem["te"], te0 + 2 + tcg)
                if tcg == 1:
                    p.wait_dma(e, "sync", six[1])
                for t8 in range(8):
                    e.tensor_copy(
                        sb["kvstage"][:, t8 * 128: t8 * 128 + 128],
                        PS[t8][:, 0:128]).then_inc(p.sem["ve"], 1)
                    p.bump("ve", 1)


def stage_vb(p: P):
    nc, sb, dr = p.nc, p.sb, p.dram
    PS = p.PS
    te0, ve0 = p.cnt["te"], p.cnt["ve"]
    wix = {}
    six = {}
    with nc.Block() as block:
        @block.gpsimd
        def _(e):
            qbarrier(p, e, own="gp")
            for tcg in range(4):
                for kc in range(NKC):
                    n = tcg * NKC + kc
                    if n >= 2:
                        p.qdrain(e, "gp")
                        e.wait_ge(p.sem["te"], te0 + (n - 1))
                    sl = n % 2
                    wix[n] = p.dma(
                        e, "gp",
                        sb["wslab"][:, sl * 4096: sl * 4096 + 1024],
                        dr["wv_b"][128 * kc:128 * (kc + 1), :])
            p.qdrain(e, "gp")

        @block.sync
        def _(e):
            for tcg in range(4):
                p.qdrain(e, "sync")
                e.wait_ge(p.sem["ve"], ve0 + 8 * (tcg + 1))
                six[tcg] = p.dma(
                    e, "sync",
                    bass.AP(dr["vb_loc"], tcg * 512 * 1024,
                            [[1024, 128], [128 * 1024, 4], [1, 1024]]),
                    bass.AP(sb["obuf"], 0,
                            [[4096, 128], [1024, 4], [1, 1024]]))
            p.qdrain(e, "sync")

        @block.tensor
        def _(e):
            for tcg in range(4):
                if tcg >= 1:
                    e.wait_ge(p.sem["ve"], ve0 + 8 * tcg)
                for kc in range(NKC):
                    n = tcg * NKC + kc
                    sl = n % 2
                    p.wait_dma(e, "gp", wix[n])
                    for t4 in range(4):
                        tc = tcg * 4 + t4
                        for hf in range(2):
                            ins = e.matmul(
                                PS[t4 * 2 + hf][:, :],
                                sb["xln"][:, kc * T + tc * 128:
                                          kc * T + tc * 128 + 128],
                                sb["wslab"][:, sl * 4096 + hf * 512:
                                            sl * 4096 + hf * 512 + 512],
                                start=(kc == 0), stop=(kc == NKC - 1))
                    ins.then_inc(p.sem["te"], 1)
                    p.bump("te", 1)

        @block.vector
        def _(e):
            for tcg in range(4):
                e.wait_ge(p.sem["te"], te0 + NKC * (tcg + 1))
                if tcg >= 1:
                    p.wait_dma(e, "sync", six[tcg - 1])
                for t4 in range(4):
                    for hf in range(2):
                        e.tensor_copy(
                            sb["obuf"][:, t4 * 1024 + hf * 512:
                                       t4 * 1024 + hf * 512 + 512],
                            PS[t4 * 2 + hf][:, :]).then_inc(p.sem["ve"], 1)
                        p.bump("ve", 1)


def stage_flagpoll(p: P, phase, copies):
    nc, sb, dr = p.nc, p.sb, p.dram
    with nc.Block() as block:
        @block.sync
        def _(e):
            qbarrier(p, e, sems=(), own="sync")
            for shname, locname in copies:
                p.dma(e, "sync",
                      dr[shname][bass.ds(p.par_snap["sync"], 1)],
                      dr[locname].ap())
            p.qdrain(e, "sync")

        @block.gpsimd
        def _(e):
            e.wait_ge(p.sem["sqp"], p.qn["sync"])
            e.dma_start(
                out=dr["sh_flag"][bass.ds(p.par_snap["gp"], 1),
                                  phase:phase + 1],
                in_=sb["c_nonce"][0:1, phase:phase + 1],
            ).then_inc(p.sem["pf"], 16)
            with (e.register(f"pr{phase}") as pr, e.register(f"pe{phase}") as pe,
                  e.register(f"nd{phase}") as nd):
                e.reg_load(pe, sb["c_nonce"][0:1, phase:phase + 1])
                e.memset(sb["r_poll"][0:1, 0:1], 0)
                e.reg_mov(nd, 1)
                with e.While(nd):
                    e.dma_start(
                        out=sb["r_poll"][0:1, 0:1],
                        in_=dr["sh_flag"][bass.ds(p.opar_snap["gp"], 1),
                                          phase:phase + 1]
                    ).then_inc(p.sem["pf"], 16)
                    e.nop(cycle_cnt=30000)
                    e.reg_load(pr, sb["r_poll"][0:1, 0:1])
                    e.reg_alu(nd, pr, pe, OP.not_equal)
            e.sem_inc(p.sem["pd"], 1)
            p.poll_done = getattr(p, "poll_done", 0) + 1


def stage_cossin(p: P):
    nc, sb, dr = p.nc, p.sb, p.dram
    PS = p.PS
    TWO_PI = 2.0 * math.pi
    ve0, se0 = p.cnt["ve"], p.cnt["se"]
    ldp = {}
    with nc.Block() as block:
        @block.sync
        def _(e):
            qbarrier(p, e, own="sync")
            ldp[0] = p.dma(e, "sync", sb["c_pos"][:, :], dr["pos"][:, :])
            p.qdrain(e, "sync")

        @block.gpsimd
        def _(e):
            e.iota(sb["r_rows"][0:1, 0:64], [[1, 64]], base=0,
                   channel_multiplier=0,
                   allow_small_or_imprecise_dtypes=True).then_inc(
                       p.sem["sm"], 1)
            p.bump("sm", 1)

        @block.scalar
        def _(e):
            e.wait_ge(p.sem["sm"], p.cnt["sm"])
            e.activation(sb["r_rows"][0:1, 64:128], sb["r_rows"][0:1, 0:64],
                         AF.Exp, scale=-math.log(500000.0) / 64.0
                         ).then_inc(p.sem["se"], 1)
            p.bump("se", 1)

        @block.vector
        def _(e):
            e.wait_ge(p.sem["se"], se0 + 1)
            r = sb["r_rows"]
            f0 = r[0:1, 64:128]
            e.tensor_scalar(r[0:1, 128:192], f0, TWO_PI / 8192.0, None,
                            OP.is_lt)
            e.tensor_scalar(r[0:1, 192:256], f0, TWO_PI / 2048.0, None,
                            OP.is_gt)
            e.tensor_scalar(r[0:1, 256:320], f0, 8192.0 / (TWO_PI * 3.0),
                            -1.0 / 3.0, OP.mult, OP.add)
            e.tensor_scalar(r[0:1, 256:320], r[0:1, 256:320], 31.0 / 32.0,
                            1.0 / 32.0, OP.mult, OP.add)
            e.tensor_mul(r[0:1, 256:320], r[0:1, 256:320], f0)
            e.tensor_scalar(r[0:1, 320:384], r[0:1, 128:192], -1.0, 1.0,
                            OP.mult, OP.add)
            e.tensor_scalar(r[0:1, 384:448], r[0:1, 192:256], -1.0, 1.0,
                            OP.mult, OP.add)
            e.tensor_mul(r[0:1, 320:384], r[0:1, 320:384], r[0:1, 384:448])
            e.tensor_scalar(r[0:1, 384:448], f0, 1.0 / 32.0, None, OP.mult)
            e.tensor_mul(r[0:1, 384:448], r[0:1, 384:448], r[0:1, 128:192])
            e.tensor_mul(r[0:1, 448:512], f0, r[0:1, 192:256])
            e.tensor_add(r[0:1, 384:448], r[0:1, 384:448], r[0:1, 448:512])
            e.tensor_mul(r[0:1, 256:320], r[0:1, 256:320], r[0:1, 320:384])
            e.tensor_add(r[0:1, 384:448], r[0:1, 384:448], r[0:1, 256:320])
            e.tensor_copy(sb["c_invf"][0:1, 0:64], r[0:1, 384:448])
            e.tensor_copy(sb["c_invf"][0:1, 64:128], r[0:1, 384:448]
                          ).then_inc(p.sem["ve"], 1)
            p.bump("ve", 1)

    for tt in range(4):
        for fn in range(2):
            te0b, ve0b, se0b = p.cnt["te"], p.cnt["ve"], p.cnt["se"]
            with nc.Block() as block:
                @block.tensor
                def _(e):
                    e.wait_ge(p.sem["ve"], ve0 + 1)
                    p.wait_dma(e, "sync", ldp[0])
                    if fn == 1 or tt > 0:
                        e.wait_ge(p.sem["se"], se0b)
                    e.matmul(PS[6][:, :], sb["c_invf"][0:1, :],
                             sb["c_pos"][0:1, tt * 512:(tt + 1) * 512],
                             start=True, stop=True).then_inc(p.sem["te"], 1)
                    p.bump("te", 1)

                @block.vector
                def _(e):
                    e.wait_ge(p.sem["te"], te0b + 1)
                    f = sb["fbuf"]
                    e.tensor_scalar(f[:, 0:512], PS[6][:, :], 1.0 / TWO_PI,
                                    0.25 if fn == 0 else 0.0, OP.mult, OP.add)
                    e.tensor_copy(sb["i512"][:, :], f[:, 0:512])
                    e.tensor_copy(f[:, 512:1024], sb["i512"][:, :])
                    e.tensor_sub(f[:, 0:512], f[:, 0:512], f[:, 512:1024])
                    e.tensor_scalar(f[:, 512:1024], f[:, 0:512], 0.5, None,
                                    OP.is_gt)
                    e.tensor_sub(f[:, 0:512], f[:, 0:512], f[:, 512:1024])
                    e.tensor_scalar(f[:, 512:1024], f[:, 0:512], -0.5, None,
                                    OP.is_lt)
                    e.tensor_add(f[:, 0:512], f[:, 0:512], f[:, 512:1024]
                                 ).then_inc(p.sem["ve"], 1)
                    p.bump("ve", 1)

                @block.scalar
                def _(e):
                    e.wait_ge(p.sem["ve"], ve0b + 1)
                    dst = sb["cs_fm"][:, fn * T + tt * 512:
                                      fn * T + tt * 512 + 512]
                    e.activation(dst, sb["fbuf"][:, 0:512], AF.Sin,
                                 scale=TWO_PI).then_inc(p.sem["se"], 1)
                    p.bump("se", 1)


CSC = [0, 12, 28, 56, 88]
TPH = 88


def stage_attn(p: P, which):
    nc, sb, dr = p.nc, p.sb, p.dram
    PS = p.PS
    a = which == "a"
    KVG, HPG = (1, 32) if a else (8, 4)
    qsrc = "qa" if a else "qb"
    cdst = "ctxa" if a else "ctxb"
    V = 3 if a else 2
    SEH = 5 if a else 4
    t10, t30, sx0, sm0, te0, ve0, se0 = (
        p.cnt[k] for k in ("t1", "t3", "sx", "sm", "te", "ve", "se"))

    def kof(h, s, idx):
        return TPH * h + CSC[s] + idx

    def kend(q):
        return TPH * (q // 4) + CSC[(q % 4) + 1]

    def rbev(q):
        return se0 + SEH * (q // 4) + (q % 4) + (2 if a else 1)

    def kvsrc_k(kvh, g):
        slot, loc = pos_block_map(g)
        if a:
            return dr["sh_ka"][slot][:, loc:loc + 512]
        return dr["sh_kb"][slot][kvh * 128:(kvh + 1) * 128, loc:loc + 512]

    def kvsrc_v(kvh, g):
        slot, loc = pos_block_map(g)
        if a:
            return bass.AP(dr["sh_va"], T * 128 * slot + loc * 128,
                           [[128, 128], [128 * 128, 4], [1, 128]])
        return bass.AP(dr["sh_vb"], T * 1024 * slot + loc * 1024 + kvh * 128,
                       [[1024, 128], [128 * 1024, 4], [1, 128]])

    kvix = {}
    qix = {}
    six = {}
    with nc.Block() as block:
        @block.sync
        def _(e):
            e.wait_ge(p.sem["pd"], p.poll_done)

            def store_q(h_, s_):
                q = h_ * 4 + s_
                p.qdrain(e, "sync")
                e.wait_ge(p.sem["ve"], ve0 + V * q + V)
                six[q] = p.dma(
                    e, "sync",
                    dr[cdst][128 * h_:128 * (h_ + 1),
                             512 * s_:512 * (s_ + 1)],
                    sb["co_sb"][:, (q % 2) * 512:(q % 2) * 512 + 512])

            for kvg in range(KVG):
                if kvg >= 1:
                    for s_ in range(4):
                        store_q(kvg * HPG - 1, s_)
                    p.qdrain(e, "sync")
                    e.wait_ge(p.sem["t3"], t30 + TPH * HPG * kvg)
                last = None
                for g in range(8):
                    p.dma(e, "sync", sb["k_sb"][:, 512 * g:512 * (g + 1)],
                          kvsrc_k(kvg, g))
                    last = p.dma(
                        e, "sync",
                        bass.AP(sb["v_sb"], 512 * g,
                                [[4096, 128], [128, 4], [1, 128]]),
                        kvsrc_v(kvg, g))
                for s_ in range(4):
                    if a:
                        ksrc = dr["ka_loc"][:, 512 * s_:512 * (s_ + 1)]
                    else:
                        ksrc = dr["kb_loc"][kvg * 128:(kvg + 1) * 128,
                                            512 * s_:512 * (s_ + 1)]
                    p.dma(e, "sync", sb["kdiag"][:, s_ * 512:(s_ + 1) * 512],
                          ksrc)
                    for kt in range(4):
                        row0 = 512 * s_ + 128 * kt
                        if a:
                            vsrc = dr["va_loc"][row0:row0 + 128, :]
                        else:
                            vsrc = dr["vb_loc"][row0:row0 + 128,
                                                kvg * 128:(kvg + 1) * 128]
                        last = p.dma(
                            e, "sync",
                            sb["vdiag"][:, s_ * 512 + kt * 128:
                                        s_ * 512 + kt * 128 + 128],
                            vsrc)
                kvix[kvg] = last
                for hh in range(HPG):
                    h = kvg * HPG + hh
                    if h >= 2:
                        p.qdrain(e, "sync")
                        e.wait_ge(p.sem["t1"], t10 + TPH * (h - 1))
                    qix[h] = p.dma(
                        e, "sync",
                        sb["q_sb"][:, (h % 2) * 2048:(h % 2) * 2048 + 2048],
                        dr[qsrc][128 * h:128 * (h + 1), :])
                    if hh >= 1:
                        for s_ in range(4):
                            store_q(h - 1, s_)
            for s_ in range(4):
                store_q(KVG * HPG - 1, s_)
            p.qdrain(e, "sync")

        @block.tensor
        def _(e):
            LA = 4
            for kvg in range(KVG):
                for hh in range(HPG):
                    h = kvg * HPG + hh
                    p.wait_dma(e, "sync", qix[h])
                    p.wait_dma(e, "sync", kvix[kvg])
                    tiles = [(s_, i_) for s_ in range(4)
                             for i_ in range(CS[s_])]
                    for j in range(len(tiles) + LA):
                        if j < len(tiles):
                            s, idx = tiles[j]
                            k = kof(h, s, idx)
                            nst = CS[s] - 4
                            diag = idx >= nst
                            m = idx - nst
                            if k >= 4:
                                e.wait_ge(p.sem["sx"], sx0 + k - 3)
                            if diag:
                                lw = sb["kdiag"][:, s * 512 + m * 128:
                                                 s * 512 + m * 128 + 128]
                            else:
                                lw = sb["k_sb"][:, 128 * idx:128 * idx + 128]
                            e.matmul(PS[2 + (k % 4)][:, :], lw,
                                     sb["q_sb"][:, (h % 2) * 2048 + 512 * s:
                                                (h % 2) * 2048 + 512 * s + 512],
                                     start=True, stop=True
                                     ).then_inc(p.sem["t1"], 1)
                            p.bump("t1", 1)
                        if j >= LA:
                            s, idx = tiles[j - LA]
                            q = h * 4 + s
                            k = kof(h, s, idx)
                            nst = CS[s] - 4
                            diag = idx >= nst
                            m = idx - nst
                            if idx == 0:
                                e.wait_ge(p.sem["ve"], ve0 + V * q)
                            if a or diag:
                                e.wait_ge(p.sem["sm"],
                                          sm0 + (k + 1 if a else q * 4 + m + 1))
                            else:
                                e.wait_ge(p.sem["sx"], sx0 + k + 1)
                            if diag:
                                lv = sb["vdiag"][:, s * 512 + m * 128:
                                                 s * 512 + m * 128 + 128]
                            else:
                                lv = sb["v_sb"][:, 128 * idx:128 * idx + 128]
                            prs = sb["pr_sb"][:, (k % 4) * 512:(k % 4) * 512 + 512]
                            e.matmul(PS[0][:, :], lv, prs,
                                     start=(idx == 0), stop=(idx == CS[s] - 1))
                            e.matmul(PS[1][0:1, :], sb["c_ones_bf"][:, :], prs,
                                     start=(idx == 0), stop=(idx == CS[s] - 1)
                                     ).then_inc(p.sem["t3"], 1)
                            p.bump("t3", 1)
                            if idx == CS[s] - 1:
                                e.wait_ge(p.sem["ve"],
                                          ve0 + V * q + (3 if a else 2) - 1)
                                if q >= 1:
                                    e.wait_ge(p.sem["se"], rbev(q - 1))
                                e.matmul(PS[6][:, :], sb["c_ones_f"][:, :],
                                         sb["r_rows"][0:1, (q % 2) * 512:
                                                      (q % 2) * 512 + 512],
                                         start=True, stop=True
                                         ).then_inc(p.sem["te"], 1)
                                p.bump("te", 1)

        @block.scalar
        def _(e):
            for kvg in range(KVG):
                for hh in range(HPG):
                    h = kvg * HPG + hh
                    asl = (h % 2) * 2560
                    if a:
                        if h >= 2:
                            e.wait_ge(p.sem["sm"], sm0 + TPH * (h - 1))
                        sl_ = SLOPES[h]
                        e.activation(sb["a_sb"][:, asl:asl + 512],
                                     sb["c_rel128"][:, :], AF.Exp, scale=sl_)
                        for m in range(4):
                            ins = e.activation(
                                sb["a_sb"][:, asl + (1 + m) * 512:
                                           asl + (2 + m) * 512],
                                sb["c_relmask"][:, m * 512:(m + 1) * 512],
                                AF.Exp, scale=sl_)
                        ins.then_inc(p.sem["se"], 1)
                        p.bump("se", 1)
                    for s in range(4):
                        q = h * 4 + s
                        nst = CS[s] - 4
                        for idx in range(CS[s]):
                            k = kof(h, s, idx)
                            diag = idx >= nst
                            e.wait_ge(p.sem["t1"], t10 + k + 1)
                            if a:
                                if k >= 4:
                                    e.wait_ge(p.sem["sm"], sm0 + k - 3)
                                if idx == 0:
                                    e.wait_ge(p.sem["ve"], ve0 + V * q + 1)
                                dst = sb["e_sb"][:, (k % 4) * 512:
                                                 (k % 4) * 512 + 512]
                                bias = (None if diag else
                                        sb["c_bias"][:, (q % 2) * 32 + idx:
                                                     (q % 2) * 32 + idx + 1])
                            else:
                                if k >= 4:
                                    e.wait_ge(p.sem["t3"], t30 + k - 3)
                                dst = sb["pr_sb"][:, (k % 4) * 512:
                                                  (k % 4) * 512 + 512]
                                bias = (None if diag else
                                        sb["c_offb"][:, s * 32 + idx:
                                                     s * 32 + idx + 1])
                            if bias is None:
                                ins = e.activation(dst, PS[2 + (k % 4)][:, :],
                                                   AF.Exp, scale=ISCALE)
                            else:
                                ins = e.activation(dst, PS[2 + (k % 4)][:, :],
                                                   AF.Exp, bias=bias,
                                                   scale=ISCALE)
                            ins.then_inc(p.sem["sx"], 1)
                            p.bump("sx", 1)
                        e.wait_ge(p.sem["te"], te0 + q + 1)
                        if q >= 2:
                            e.wait_ge(p.sem["ve"], ve0 + V * (q - 2) + V)
                        e.copy(sb["rb_sb"][:, (q % 2) * 512:(q % 2) * 512 + 512],
                               PS[6][:, :]).then_inc(p.sem["se"], 1)
                        p.bump("se", 1)

        @block.vector
        def _(e):
            for kvg in range(KVG):
                for hh in range(HPG):
                    h = kvg * HPG + hh
                    asl = (h % 2) * 2560
                    for s in range(4):
                        q = h * 4 + s
                        nst = CS[s] - 4
                        if a:
                            if q >= 2:
                                e.wait_ge(p.sem["sx"], sx0 + kend(q - 2))
                            e.tensor_scalar(
                                sb["c_bias"][:, (q % 2) * 32:(q % 2) * 32 + 32],
                                sb["c_offa"][:, s * 32:s * 32 + 32],
                                SLOPES[h], None, OP.mult
                            ).then_inc(p.sem["ve"], 1)
                            p.bump("ve", 1)
                        for idx in range(CS[s]):
                            k = kof(h, s, idx)
                            diag = idx >= nst
                            m = idx - nst
                            if a:
                                e.wait_ge(p.sem["sx"], sx0 + k + 1)
                                if k >= 4:
                                    e.wait_ge(p.sem["t3"], t30 + k - 3)
                                if idx == 0 and s == 0:
                                    e.wait_ge(p.sem["se"], se0 + SEH * h + 1)
                                aof = asl + ((1 + m) * 512 if diag else 0)
                                e.tensor_mul(
                                    sb["pr_sb"][:, (k % 4) * 512:
                                                (k % 4) * 512 + 512],
                                    sb["e_sb"][:, (k % 4) * 512:
                                               (k % 4) * 512 + 512],
                                    sb["a_sb"][:, aof:aof + 512]
                                ).then_inc(p.sem["sm"], 1)
                                p.bump("sm", 1)
                            elif diag:
                                e.wait_ge(p.sem["sx"], sx0 + k + 1)
                                prs = sb["pr_sb"][:, (k % 4) * 512:
                                                  (k % 4) * 512 + 512]
                                e.tensor_mul(prs, prs,
                                             sb["c_cmask"][:, m * 512:
                                                           (m + 1) * 512]
                                             ).then_inc(p.sem["sm"], 1)
                                p.bump("sm", 1)
                        e.wait_ge(p.sem["t3"], t30 + kend(q))
                        e.reciprocal(sb["r_rows"][0:1, (q % 2) * 512:
                                                  (q % 2) * 512 + 512],
                                     PS[1][0:1, :]).then_inc(p.sem["ve"], 1)
                        p.bump("ve", 1)
                        e.wait_ge(p.sem["se"], rbev(q))
                        if q >= 2 and (q - 2) in six:
                            p.wait_dma(e, "sync", six[q - 2])
                        e.tensor_tensor(
                            sb["co_sb"][:, (q % 2) * 512:(q % 2) * 512 + 512],
                            PS[0][:, :],
                            sb["rb_sb"][:, (q % 2) * 512:(q % 2) * 512 + 512],
                            OP.mult).then_inc(p.sem["ve"], 1)
                        p.bump("ve", 1)


def stage_ffn_gu(p: P):
    nc, sb, dr = p.nc, p.sb, p.dram
    PS = p.PS
    te0, ve0, se0 = (p.cnt[k] for k in ("te", "ve", "se"))

    def asl(kc, tt):
        return sb["xln"][:, kc * T + tt * 512: kc * T + tt * 512 + 512]

    wixg = {}
    wixu = {}
    six = {}
    with nc.Block() as block:
        @block.sync
        def _(e):
            qbarrier(p, e, own="sync")
            for ic in range(NIC):
                sl = ic % 2
                if ic >= 2:
                    p.qdrain(e, "sync")
                    e.wait_ge(p.sem["te"], te0 + 2 * (ic - 1))
                wixg[ic] = p.dma(e, "sync",
                                 sb["ffnblk"][:, sl * 4096:(sl + 1) * 4096],
                                 dr["wgS"][ic])
                wixu[ic] = p.dma(e, "sync",
                                 sb["ffnblk"][:, 8192 + sl * 4096:
                                              8192 + (sl + 1) * 4096],
                                 dr["wuS"][ic])
                if ic >= 1:
                    for tt in range(4):
                        n = (ic - 1) * 4 + tt
                        p.qdrain(e, "sync")
                        e.wait_ge(p.sem["ve"], ve0 + n + 1)
                        six[n] = p.dma(
                            e, "sync",
                            dr["hbuf"][128 * (ic - 1):128 * ic,
                                       tt * 512:(tt + 1) * 512],
                            sb["hstage"][:, (n % 2) * 512:(n % 2) * 512 + 512])
            for tt in range(4):
                n = (NIC - 1) * 4 + tt
                p.qdrain(e, "sync")
                e.wait_ge(p.sem["ve"], ve0 + n + 1)
                six[n] = p.dma(
                    e, "sync",
                    dr["hbuf"][128 * (NIC - 1):128 * NIC,
                               tt * 512:(tt + 1) * 512],
                    sb["hstage"][:, (n % 2) * 512:(n % 2) * 512 + 512])
            p.qdrain(e, "sync")

        @block.tensor
        def _(e):
            for ic in range(NIC):
                sl = ic % 2
                p.wait_dma(e, "sync", wixu[ic])
                if ic >= 1:
                    e.wait_ge(p.sem["se"], se0 + 4 * ic)
                for kc in range(NKC):
                    for tt in range(4):
                        ins = e.matmul(PS[tt][:, :],
                                       sb["ffnblk"][:, sl * 4096 + kc * 128:
                                                    sl * 4096 + kc * 128 + 128],
                                       asl(kc, tt), start=(kc == 0),
                                       stop=(kc == NKC - 1))
                ins.then_inc(p.sem["te"], 1)
                p.bump("te", 1)
                if ic >= 1:
                    e.wait_ge(p.sem["ve"], ve0 + 4 * ic)
                for kc in range(NKC):
                    for tt in range(4):
                        ins = e.matmul(PS[4 + tt][:, :],
                                       sb["ffnblk"][:, 8192 + sl * 4096 + kc * 128:
                                                    8192 + sl * 4096 + kc * 128 + 128],
                                       asl(kc, tt), start=(kc == 0),
                                       stop=(kc == NKC - 1))
                ins.then_inc(p.sem["te"], 1)
                p.bump("te", 1)

        @block.scalar
        def _(e):
            for ic in range(NIC):
                e.wait_ge(p.sem["te"], te0 + 2 * ic + 1)
                for tt in range(4):
                    if ic >= 1:
                        e.wait_ge(p.sem["ve"], ve0 + 4 * (ic - 1) + tt + 1)
                    e.activation(sb["obuf"][:, tt * 512:(tt + 1) * 512],
                                 PS[tt][:, :], AF.Silu).then_inc(p.sem["se"], 1)
                    p.bump("se", 1)

        @block.vector
        def _(e):
            for ic in range(NIC):
                e.wait_ge(p.sem["te"], te0 + 2 * ic + 2)
                for tt in range(4):
                    n = ic * 4 + tt
                    e.wait_ge(p.sem["se"], se0 + 4 * ic + tt + 1)
                    if n >= 2:
                        p.wait_dma(e, "sync", six[n - 2])
                    e.tensor_tensor(
                        sb["hstage"][:, (n % 2) * 512:(n % 2) * 512 + 512],
                        sb["obuf"][:, tt * 512:(tt + 1) * 512],
                        PS[4 + tt][:, :], OP.mult).then_inc(p.sem["ve"], 1)
                    p.bump("ve", 1)


def stage_ffn_down(p: P, xold, dst):
    nc, sb, dr = p.nc, p.sb, p.dram
    PS = p.PS
    te0, ve0 = p.cnt["te"], p.cnt["ve"]
    hix = {}
    wix = {}
    wq = {}
    xix = {}
    six = {}
    for G in range(16):
        for ic in range(NIC):
            n = G * NIC + ic
            wq[n] = "sync" if ic % 2 == 0 else "gp"
    with nc.Block() as block:
        @block.sync
        def _(e):
            qbarrier(p, e, own="sync")
            for tt in range(4):
                if tt >= 1:
                    p.qdrain(e, "sync")
                    e.wait_ge(p.sem["te"], te0 + NIC * 4 * (tt - 1) + 3 * NIC + 43)
                hix[(tt, 0)] = p.dma(
                    e, "sync",
                    bass.AP(sb["h_tt"], 0,
                            [[NIC * 512, 128], [512, 43], [1, 512]]),
                    bass.AP(dr["hbuf"], tt * 512,
                            [[T, 128], [128 * T, 43], [1, 512]]))
                for ncg in range(4):
                    G = tt * 4 + ncg
                    for ic in range(NIC):
                        n = G * NIC + ic
                        if wq[n] != "sync":
                            continue
                        sl = n % 8
                        if n >= 8:
                            p.qdrain(e, "sync")
                            e.wait_ge(p.sem["te"], te0 + n - 7)
                        wix[n] = p.dma(
                            e, "sync",
                            sb["wd_slab"][:, sl * 1024:(sl + 1) * 1024],
                            dr["wd_T"][128 * ic:128 * (ic + 1),
                                       1024 * ncg:1024 * (ncg + 1)])
                    for nc2 in range(10):
                        if nc2 < 8:
                            j = G * 8 + nc2
                            if j >= 2:
                                p.qdrain(e, "sync")
                                e.wait_ge(p.sem["ve"], ve0 + j - 1)
                            xix[j] = p.dma(
                                e, "sync",
                                sb["fbuf"][:, (j % 2) * 512:(j % 2) * 512 + 512],
                                dr[xold][1024 * ncg + 128 * nc2:
                                         1024 * ncg + 128 * (nc2 + 1),
                                         tt * 512:(tt + 1) * 512])
                        if nc2 >= 2:
                            js = G * 8 + nc2 - 2
                            p.qdrain(e, "sync")
                            e.wait_ge(p.sem["ve"], ve0 + js + 1)
                            six[js] = p.dma(
                                e, "sync",
                                dr[dst][1024 * ncg + 128 * (nc2 - 2):
                                        1024 * ncg + 128 * (nc2 - 1),
                                        tt * 512:(tt + 1) * 512],
                                sb["fbuf"][:, 1024 + (js % 2) * 512:
                                           1024 + (js % 2) * 512 + 512])
            p.qdrain(e, "sync")

        @block.gpsimd
        def _(e):
            p.qdrain(e, "gp")
            if te0:
                e.wait_ge(p.sem["te"], te0)
            for tt in range(4):
                if tt >= 1:
                    p.qdrain(e, "gp")
                    e.wait_ge(p.sem["te"], te0 + NIC * 4 * tt)
                hix[(tt, 1)] = p.dma(
                    e, "gp",
                    bass.AP(sb["h_tt"], 43 * 512,
                            [[NIC * 512, 128], [512, 43], [1, 512]]),
                    bass.AP(dr["hbuf"], 43 * 128 * T + tt * 512,
                            [[T, 128], [128 * T, 43], [1, 512]]))
                for ncg in range(4):
                    G = tt * 4 + ncg
                    for ic in range(NIC):
                        n = G * NIC + ic
                        if wq[n] != "gp":
                            continue
                        sl = n % 8
                        if n >= 8:
                            p.qdrain(e, "gp")
                            e.wait_ge(p.sem["te"], te0 + n - 7)
                        wix[n] = p.dma(
                            e, "gp",
                            sb["wd_slab"][:, sl * 1024:(sl + 1) * 1024],
                            dr["wd_T"][128 * ic:128 * (ic + 1),
                                       1024 * ncg:1024 * (ncg + 1)])
            p.qdrain(e, "gp")

        @block.tensor
        def _(e):
            for tt in range(4):
                for ncg in range(4):
                    G = tt * 4 + ncg
                    for ic in range(NIC):
                        n = G * NIC + ic
                        sl = n % 8
                        p.wait_dma(e, wq[n], wix[n])
                        if ic == 0:
                            p.wait_dma(e, "sync", hix[(tt, 0)])
                        if ic == 43:
                            p.wait_dma(e, "gp", hix[(tt, 1)])
                        for nc2 in range(8):
                            if ic == 0 and G >= 1:
                                e.wait_ge(p.sem["ve"],
                                          ve0 + 8 * (G - 1) + nc2 + 1)
                            ins = e.matmul(
                                PS[nc2][:, :],
                                sb["wd_slab"][:, sl * 1024 + nc2 * 128:
                                              sl * 1024 + nc2 * 128 + 128],
                                sb["h_tt"][:, ic * 512:(ic + 1) * 512],
                                start=(ic == 0), stop=(ic == NIC - 1))
                        ins.then_inc(p.sem["te"], 1)
                        p.bump("te", 1)

        @block.vector
        def _(e):
            for tt in range(4):
                for ncg in range(4):
                    G = tt * 4 + ncg
                    e.wait_ge(p.sem["te"], te0 + NIC * (G + 1))
                    for nc2 in range(8):
                        j = G * 8 + nc2
                        p.wait_dma(e, "sync", xix[j])
                        if j >= 2:
                            p.wait_dma(e, "sync", six[j - 2])
                        e.tensor_tensor(
                            sb["fbuf"][:, 1024 + (j % 2) * 512:
                                       1024 + (j % 2) * 512 + 512],
                            PS[nc2][:, :],
                            sb["fbuf"][:, (j % 2) * 512:(j % 2) * 512 + 512],
                            OP.add).then_inc(p.sem["ve"], 1)
                        p.bump("ve", 1)


def stage_final(p: P):
    nc = p.nc
    with nc.Block() as block:
        @block.sync
        def _(e):
            qbarrier(p, e, own="sync")
            if p.qn["gp"]:
                e.wait_ge(p.sem["gq"], 16 * p.qn["gp"])


def build_program(debug=(), upto=99):
    p = P(debug=debug)
    declare_io(p)
    alloc_sbuf(p)
    stages = [
        lambda: stage_init(p),
        lambda: stage_dequant(p, "gate_b", "gate_z", "gate_s", "wgS", FI, D, True),
        lambda: stage_dequant(p, "up_b", "up_z", "up_s", "wuS", FI, D, True),
        lambda: stage_dequant(p, "down_b", "down_z", "down_s", "wd_T", D, FI, False),
        lambda: stage_ln(p, "xin", tap="xln_dbg"),
        lambda: stage_proj(p, "wq_a", 32, "store", "qa"),
        lambda: stage_kv_a(p),
        lambda: stage_flagpoll(p, 0, [("sh_ka", "ka_loc"), ("sh_va", "va_loc")]),
        lambda: stage_attn(p, "a"),
        lambda: stage_load_arena(p, "ctxa"),
        lambda: stage_proj(p, "wo_a", 32, "resid", "x1", xold="xin"),
        lambda: stage_ln(p, "x1"),
        lambda: stage_ffn_gu(p),
        lambda: stage_ffn_down(p, "x1", "x2"),
        lambda: stage_ln(p, "x2"),
        lambda: stage_cossin(p),
        lambda: stage_proj(p, "wq_b", 32, "rope", "qb"),
        lambda: stage_proj(p, "wk_b", 8, "rope", "kb_loc"),
        lambda: stage_vb(p),
        lambda: stage_flagpoll(p, 1, [("sh_kb", "kb_loc"), ("sh_vb", "vb_loc")]),
        lambda: stage_attn(p, "b"),
        lambda: stage_load_arena(p, "ctxb"),
        lambda: stage_proj(p, "wo_b", 32, "resid", "x3", xold="x2"),
        lambda: stage_ln(p, "x3"),
        lambda: stage_ffn_gu(p),
        lambda: stage_ffn_down(p, "x3", "xout"),
    ]
    only = globals().get("ONLY_STAGES")
    for i, fn in enumerate(stages):
        if i >= upto:
            break
        if only is not None and i not in only:
            continue
        fn()
    stage_final(p)
    return p


def host_prepare(inputs, nonce):
    import ml_dtypes
    x = np.asarray(inputs["L0_attn_a_hidden_states"])

    def bf(a):
        return np.ascontiguousarray(a).astype(ml_dtypes.bfloat16)

    wq_a = np.ascontiguousarray(np.asarray(inputs["L0_attn_a_Wq"], np.float32))
    wkv_a = np.ascontiguousarray(np.asarray(inputs["L0_attn_a_Wkv"], np.float32))
    wo_a = np.ascontiguousarray(np.asarray(inputs["L0_attn_a_Wo"], np.float32))
    wq_b = np.ascontiguousarray(np.asarray(inputs["L0_attn_b_Wq"], np.float32).T)
    wk_b = np.ascontiguousarray(np.asarray(inputs["L0_attn_b_Wk"], np.float32).T)
    wv_b = np.ascontiguousarray(np.asarray(inputs["L0_attn_b_Wv"], np.float32).T)
    wo_b = np.ascontiguousarray(np.asarray(inputs["L0_attn_b_Wo"], np.float32).T)

    def u8view(a):
        a = np.ascontiguousarray(np.asarray(a, np.int32))
        return np.ascontiguousarray(
            a.view(np.uint8).reshape(a.shape[0], a.shape[1], 4)[:, :, 0])

    gate_b = u8view(inputs["L0_ffn_gate_Wq"])
    up_b = u8view(inputs["L0_ffn_up_Wq"])
    down_b = u8view(inputs["L0_ffn_down_Wq"])

    dk = np.arange(128)[:, None].astype(np.float32)
    dq = np.arange(512)[None, :].astype(np.float32)
    rel = dk - dq
    rel128 = bf(rel - 128.0)
    relmask = bf(np.stack([np.where(rel + 128 * m <= 0, rel + 128 * m, -1e9)
                           for m in range(4)]))
    cmask = bf(np.stack([(rel + 128 * m <= 0).astype(np.float32)
                         for m in range(4)]))
    smat = np.zeros((128, 128), np.float32)
    for m_ in range(64):
        smat[m_ + 64, m_] = -1.0
        smat[m_, m_ + 64] = 1.0
    smat = bf(smat)

    common = dict(
        wq_a=wq_a, wkv_a=wkv_a, wo_a=wo_a, wq_b=wq_b, wk_b=wk_b,
        wv_b=wv_b, wo_b=wo_b, gate_b=gate_b, up_b=up_b, down_b=down_b,
        gate_s=np.asarray(inputs["L0_ffn_gate_scale"], np.float32),
        gate_z=np.asarray(inputs["L0_ffn_gate_zero"], np.float32),
        up_s=np.asarray(inputs["L0_ffn_up_scale"], np.float32),
        up_z=np.asarray(inputs["L0_ffn_up_zero"], np.float32),
        down_s=np.asarray(inputs["L0_ffn_down_scale"], np.float32),
        down_z=np.asarray(inputs["L0_ffn_down_zero"], np.float32),
        rel128=rel128, relmask=relmask, cmask=cmask, smat=smat,
        ident=np.eye(128, dtype=np.float32),
        nonce=nonce,
    )

    in_maps = []
    shard_info = []
    for c in range(8):
        b, par = c // 2, c % 2
        toks = own_tokens(par)
        q0s = own_q0s(par)
        shard_info.append((b, toks))
        xc = np.ascontiguousarray(x[b][toks, :].T.astype(np.float32))
        offv = np.full((4, 32), NEGPAD, np.float32)
        offb = np.full((4, 32), NEG, np.float32)
        for s_ in range(4):
            nst = CS[s_] - 4
            for j in range(32):
                if j < nst and 128 * j < q0s[s_]:
                    offv[s_, j] = 128 * j - q0s[s_] + 128.0
                    offb[s_, j] = 0.0
        m = dict(common)
        m.update(
            xin=xc,
            offv_a=np.broadcast_to(offv.reshape(1, 128), (128, 128)).copy(),
            offv_b=np.broadcast_to(offb.reshape(1, 128), (128, 128)).copy(),
            pos=np.asarray(toks, np.float32).reshape(1, T),
        )
        in_maps.append(m)
    return in_maps, shard_info


_PROG_CACHE = {}


def kernel(**inputs):
    nonce = (np.random.randint(1, 2 ** 30, size=(1, 2)).astype(np.int32)
             + np.asarray([[0, 2 ** 30]], np.int32))
    in_maps, shard_info = host_prepare(inputs, nonce)
    key = (tuple(sorted(DEBUG)), UPTO)
    if key not in _PROG_CACHE:
        _PROG_CACHE[key] = build_program(debug=DEBUG, upto=UPTO)
    p = _PROG_CACHE[key]
    res = run_bass_kernel_spmd(p.nc, in_maps, core_ids=list(range(8)))
    out = np.zeros((4, 4096, 4096), np.float32)
    for c in range(8):
        b, toks = shard_info[c]
        out[b][np.asarray(toks), :] = np.asarray(
            res.results[c]["xout"], np.float32).T
    if DEBUG:
        kernel._debug_results = res.results
    return out
